# version 01
# baseline (speedup 1.0000x reference)
import contextlib
import numpy as np
import concourse.bass as bass
import concourse.mybir as mybir
from concourse.bass_utils import run_bass_kernel_spmd

F32 = mybir.dt.float32
BF16 = mybir.dt.bfloat16
AF = mybir.ActivationFunctionType
ALU = mybir.AluOpType
AX = mybir.AxisListType

P = 128
NSEQ = 4
T = 2048
D = 1024
NCH = 8
HS = 2048
NHS = 16
NH = 16
DH = 64
HM = 4096
NHM = 32
NE = 8
NTT = 4
TT = 512
KB = 1024

STAGE_FULL = 7
ATT_PART = 4
TRCP_ON = True
QT_N = 4
HP_N = 8
OVERSYNC = False
MOE_PART = 9


class Ctr:
    __slots__ = ("sem", "n")

    def __init__(self, sem):
        self.sem = sem
        self.n = 0


class Bld:
    def __init__(self, nc):
        self.nc = nc
        self.q = {"sy": [], "gp": [], "te": [], "ve": [], "se": []}
        self.stack = contextlib.ExitStack()

    def on(self, eng, fn):
        self.q[eng].append(fn)

    def ctr(self, name):
        return Ctr(self.stack.enter_context(self.nc.semaphore(name)))

    def wait(self, eng, c, v=None):
        v = c.n if v is None else v
        if v > 0:
            self.on(eng, lambda e, s=c.sem, vv=v: e.wait_ge(s, vv))

    def inc(self, eng, c, k, fn):
        c.n += k
        self.on(eng, lambda e, f=fn, s=c.sem, kk=k: f(e).then_inc(s, kk))

    def make_arena(self, total_bytes):
        self.arena = self.stack.enter_context(
            self.nc.sbuf_tensor("arena", [P, total_bytes], mybir.dt.uint8))
        self.arena_base = self.nc.lookup_mloc(self.arena).addr

    def at(self, name, off_bytes, shape, dtype):
        nc = self.nc
        h = nc.virtual_sbuf_tensor(name, list(shape), dtype)
        nc._tensor(h.name, list(shape), dtype, type="SB")
        mloc = nc.lookup_mloc(h)
        mloc.allocated = not nc.target_bir_lowering
        mloc.addr = self.arena_base + off_bytes
        return h

    def flush(self, blk):
        for name, dec in (("sy", blk.sync), ("gp", blk.gpsimd),
                          ("te", blk.tensor), ("ve", blk.vector),
                          ("se", blk.scalar)):
            fns = self.q[name]
            self.q[name] = []
            if fns:
                def runner(engine, fns=fns):
                    for f in fns:
                        f(engine)
                dec(runner)


class Ring:

    def __init__(self, b, nslots, load_ctr, cons_ctr):
        self.b = b
        self.nslots = nslots
        self.load_ctr = load_ctr
        self.cons_ctr = cons_ctr
        self.load_vals = []
        self.cons_vals = {}
        self.floor = cons_ctr.n

    def load(self, eng, fn):
        i = len(self.load_vals)
        if i >= self.nslots:
            j = i - self.nslots
            self.b.on(eng, lambda e, r=self, j=j: e.wait_ge(
                r.cons_ctr.sem, r.cons_vals[j]))
        elif self.floor > 0:
            self.b.wait(eng, self.cons_ctr, self.floor)
        self.b.inc(eng, self.load_ctr, 16, fn)
        self.load_vals.append(self.load_ctr.n)
        return i

    def use(self, i, eng="te"):
        self.b.wait(eng, self.load_ctr, self.load_vals[i])

    def consumed(self, i):
        self.cons_vals[i] = self.cons_ctr.n


def build_program(stage=STAGE_FULL):
    nc = bass.Bass("TRN2", target_bir_lowering=False)
    b = Bld(nc)

    dt = {}
    dt["xs"] = nc.dram_tensor("xs", [NSEQ, D, T], F32, kind="ExternalInput").ap()
    for n_, sh in [("wfc", [D, HS]), ("wpr", [HS, D]), ("wqkv", [D, 3 * D]),
                   ("wo", [D, D]), ("wg", [D, NE]), ("w1", [NE, D, HM]),
                   ("w2", [NE, HM, D]), ("bqkv", [P, 24]),
                   ("b1s", [P, NE * NHM]), ("b2l", [NE, D])]:
        dt[n_] = nc.dram_tensor(n_, sh, F32, kind="ExternalInput").ap()
    for v in ["rms1w", "alog", "cout", "dskip", "rms2w",
              "ln1w", "ln1b", "ln2w", "ln2b", "bo"]:
        dt[v] = nc.dram_tensor(v, [P, NCH], F32, kind="ExternalInput").ap()
    dt["ys"] = nc.dram_tensor("ys", [NSEQ, D, T], F32, kind="ExternalOutput").ap()
    dt["xnew_s"] = nc.dram_tensor("xnew_s", [NSEQ, D, T], F32, kind="Internal").ap()
    dt["w1b"] = nc.dram_tensor("w1b", [NE, D, HM], BF16, kind="Internal").ap()
    dt["w2b"] = nc.dram_tensor("w2b", [NE, HM, D], BF16, kind="Internal").ap()

    xs_c = dt["xs"].rearrange("s (c p) t -> s c p t", p=P)
    ys_c = dt["ys"].rearrange("s (c p) t -> s c p t", p=P)
    xnew_c = dt["xnew_s"].rearrange("s (c p) t -> s c p t", p=P)
    wfc_c = dt["wfc"].rearrange("(c p) h -> p c h", p=P)
    wpr_c = dt["wpr"].rearrange("(c p) o -> p c o", p=P)
    wqkv_c = dt["wqkv"].rearrange("(c p) o -> p c o", p=P)
    wo_c = dt["wo"].rearrange("(c p) o -> p c o", p=P)
    wg_c = dt["wg"].rearrange("(c p) e -> p c e", p=P)
    w1_c = dt["w1"].rearrange("e (c p) h -> e p c h", p=P)
    w2_c = dt["w2"].rearrange("e (c p) o -> e p c o", p=P)
    w1b_c = dt["w1b"].rearrange("e (c p) h -> e p c h", p=P)
    w2b_c = dt["w2b"].rearrange("e (c p) o -> e p c o", p=P)

    b.make_arena(207 * KB + 768)
    A_ = b.at
    RX = A_("RX", 0, [P, NCH * T], F32)
    QB = A_("QB", 0, [P, NCH, T], BF16)
    KBU = A_("KBU", 32 * KB, [P, NCH, T], BF16)
    GBUF = A_("GBUF", 0, [P, NHM, TT], BF16)
    MCAST = A_("MCAST", 32 * KB, [P, NCH * TT], BF16)
    MSQ = A_("MSQ", 40 * KB, [P, NCH * TT], BF16)
    XNT8 = A_("XNT8", 32 * KB, [P, NCH, TT], F32)
    B2F = A_("B2F", 60 * KB, [NE, D], F32)
    WEFM = A_("WEFM", 48 * KB, [NE, T], F32)
    ZB = A_("ZB", 56 * KB, [P, P], F32)
    GE = A_("GE", 56 * KB + 512, [P, P], F32)
    GP_ = A_("GP", 57 * KB, [P, P], F32)
    GMK = A_("GMK", 57 * KB + 512, [P, P], F32)
    GIND = A_("GIND", 58 * KB, [P, P], F32)
    GM1 = A_("GM1", 58 * KB + 512, [P, 16], F32)
    GM2 = A_("GM2", 58 * KB + 576, [P, 16], F32)
    GS = A_("GS", 58 * KB + 640, [P, 16], F32)
    WETM = A_("WETM", 59 * KB, [P, P], F32)
    BF = A_("BF", 64 * KB, [P, NCH * T], F32)
    CB1 = A_("CB1", 128 * KB, [P, NCH * T], BF16)
    PT0 = A_("PT0", 128 * KB, [P, 4 * TT], BF16)
    PT1 = A_("PT1", 132 * KB, [P, 4 * TT], BF16)
    CTXB = A_("CTXB", 136 * KB, [P, NCH * TT], BF16)
    SR = A_("SR", 144 * KB, [P, TT], F32)
    CSQ = A_("CSQ", 146 * KB, [P, T], BF16)
    CQM = A_("CQM", 150 * KB, [P, NCH], F32)
    CKM = A_("CKM", 150 * KB + 64, [P, NCH], F32)
    CBD = A_("CBD", 150 * KB + 128, [P, NCH], F32)
    VFM = A_("VFM", 198 * KB, [P, T], BF16)
    CASTA = A_("CASTA", 160 * KB, [P, NCH * TT], BF16)
    SQA = A_("SQA", 168 * KB, [P, NCH * TT], BF16)
    HID = A_("HID", 160 * KB, [P, NHS, TT], BF16)
    WFC = A_("WFC", 176 * KB, [P, 4, NCH, P], BF16)
    WPR = A_("WPR", 184 * KB, [P, 2, NHS, P], BF16)
    VT = A_("VT", 160 * KB, [P, 16, NH * DH], BF16)
    WQO = A_("WQO", 192 * KB + 2 * KB, [P, 2, NCH, P], BF16)
    W1E = A_("W1E", 160 * KB, [P, 2, NCH, TT], BF16)
    W2E = A_("W2E", 176 * KB, [P, 2, 4, D], BF16)
    WEB = A_("WEB", 192 * KB, [P, TT], F32)
    MTMP = A_("MTMP", 194 * KB, [P, TT], F32)
    MEANT = A_("MEANT", 198 * KB, [P, TT], F32)
    RSTDT = A_("RSTDT", 200 * KB, [P, TT], F32)
    FTMP = A_("FTMP", 202 * KB, [P, TT], F32)
    DTMP = A_("DTMP", 192 * KB, [P, T], F32)
    ONESB = A_("ONESB", 204 * KB, [P, P], BF16)
    IDFB = A_("IDFB", 204 * KB + 256, [P, P], BF16)
    IDF = A_("IDF", 204 * KB + 512, [P, P], F32)
    ONESR = A_("ONESR", 205 * KB, [1, P], F32)
    VEC = {}
    off = 205 * KB + 512
    for v in ["rms1w", "alog", "cout", "dskip", "rms2w",
              "ln1w", "ln1b", "ln2w", "ln2b", "bo"]:
        VEC[v] = A_("sb_" + v, off, [P, NCH], F32)
        off += 32
    BQKV = A_("BQKV", off, [P, 24], F32); off += 96
    AV = A_("AV", off, [P, NCH], F32); off += 32
    AVT = A_("AVT", off, [P, NCH], F32); off += 32
    B1S = A_("B1S", off, [P, NE * NHM], F32); off += 1024
    WGB = A_("WGB", off, [P, NCH, NE], BF16); off += 128
    WGF = A_("WGF", off, [P, NCH, NE], F32); off += 256
    VNOP = A_("VNOP", off, [1, 8], F32); off += 32
    SNOP = A_("SNOP", off, [1, 8], F32); off += 32
    IOF = A_("IOF", 0, [P, P], F32)
    IOP = A_("IOP", 512, [P, 1], F32)
    assert off <= 207 * KB + 768, off

    psA = b.stack.enter_context(nc.psum_tensor("psA", [P, 4 * TT], F32))
    psB = b.stack.enter_context(nc.psum_tensor("psB", [P, 2 * TT], F32))
    pb6 = b.stack.enter_context(nc.psum_tensor("pb6", [P, TT], F32))
    pb7 = b.stack.enter_context(nc.psum_tensor("pb7", [P, TT], F32))
    psTv = pb6[:].bitcast(BF16)

    def psA_s(i):
        return psA[:, (i % 4) * TT:((i % 4) + 1) * TT]

    def psB_s(i):
        return psB[:, (i % 2) * TT:((i % 2) + 1) * TT]

    def moe_ps(oc):
        if oc < 4:
            return psA_s(oc)
        if oc < 6:
            return psB_s(oc - 4)
        return pb6[:] if oc == 6 else pb7[:]

    g_ld = b.ctr("g_ld")
    g_cmp = b.ctr("g_cmp")
    g_wb = b.ctr("g_wb")
    C = []
    for s in range(NSEQ):
        C.append({k: b.ctr(f"{k}{s}") for k in
                  ["ld", "wg", "sq", "pe", "ve", "se", "vch", "sch", "out"]})

    def vedge(c):
        ch = c["vch"]
        ch.n += 1
        b.on("ve", lambda e, s=ch.sem: e.tensor_copy(
            out=VNOP[:, 0:1], in_=VNOP[:, 1:2]).then_inc(s, 1))
        b.on("ve", lambda e, s=ch.sem, v=ch.n: e.wait_ge(s, v))

    b.on("gp", lambda g: g.iota(IOF[:], pattern=[[1, P]], channel_multiplier=0,
                                allow_small_or_imprecise_dtypes=True))
    b.inc("gp", g_cmp, 1, lambda g: g.iota(
        IOP[:], pattern=[[0, 1]], channel_multiplier=1,
        allow_small_or_imprecise_dtypes=True))
    b.wait("ve", g_cmp, 1)
    b.on("ve", lambda v: v.tensor_scalar(out=IDF[:], in0=IOF[:],
                                         scalar1=IOP[:, 0:1], scalar2=None,
                                         op0=ALU.is_equal))
    b.on("ve", lambda v: v.tensor_copy(out=IDFB[:], in_=IDF[:]))
    b.on("ve", lambda v: v.memset(ONESB[:], 1.0))
    b.inc("ve", g_cmp, 1, lambda v: v.memset(ONESR[:], 1.0))
    for v in ["rms1w", "alog", "cout", "dskip", "rms2w",
              "ln1w", "ln1b", "ln2w", "ln2b", "bo"]:
        b.inc("sy", g_ld, 16, lambda s, d=VEC[v], src=dt[v]:
              s.dma_start(out=d[:], in_=src))
    b.inc("sy", g_ld, 16, lambda s: s.dma_start(out=BQKV[:], in_=dt["bqkv"]))
    b.inc("sy", g_ld, 16, lambda s: s.dma_start(out=B1S[:], in_=dt["b1s"]))
    b.inc("sy", g_ld, 16, lambda s: s.dma_start(out=WGF[:], in_=wg_c))
    b.wait("se", g_ld, 16 * 2)
    b.inc("se", g_cmp, 1, lambda s: s.activation(out=AVT[:], in_=VEC["alog"][:],
                                                 func=AF.Exp))
    b.on("se", lambda s, v=g_cmp.n: s.wait_ge(g_cmp.sem, v))
    b.inc("se", g_cmp, 1, lambda s: s.activation(out=AV[:], in_=AVT[:],
                                                 func=AF.Exp, scale=-1.0))


    def stats(c, src, tt, mean, eps, cast, sq, tmp, src_stride=T):
        t0 = tt * TT
        b.wait("se", c["ve"])
        for ch in range(NCH):
            so = ch * src_stride + t0
            b.on("se", lambda s, so=so, i=ch: s.activation(
                out=sq[:, i * TT:(i + 1) * TT], in_=src[:, so:so + TT],
                func=AF.Square))
        b.inc("se", c["se"], 1, lambda s: s.copy(out=SNOP[:, 0:1], in_=SNOP[:, 1:2]))
        if mean:
            for ch in range(NCH):
                so = ch * src_stride + t0
                b.on("ve", lambda v, so=so, i=ch: v.tensor_copy(
                    out=cast[:, i * TT:(i + 1) * TT], in_=src[:, so:so + TT]))
            b.inc("ve", c["ve"], 1, lambda v: v.tensor_copy(out=VNOP[:, 0:1], in_=VNOP[:, 1:2]))
        b.wait("te", c["se"])
        b.wait("te", c["ve"])
        for ch in range(NCH):
            last = ch == NCH - 1
            if last and not mean:
                b.inc("te", c["pe"], 1, lambda t, i=ch: t.matmul(
                    psB[:, TT:2 * TT], ONESB[:], sq[:, i * TT:(i + 1) * TT],
                    start=(i == 0), stop=True))
            else:
                b.on("te", lambda t, i=ch: t.matmul(
                    psB[:, TT:2 * TT], ONESB[:], sq[:, i * TT:(i + 1) * TT],
                    start=(i == 0), stop=(i == NCH - 1)))
        if mean:
            for ch in range(NCH):
                if ch == NCH - 1:
                    b.inc("te", c["pe"], 1, lambda t, i=ch: t.matmul(
                        psB[:, 0:TT], ONESB[:], cast[:, i * TT:(i + 1) * TT],
                        start=(i == 0), stop=True))
                else:
                    b.on("te", lambda t, i=ch: t.matmul(
                        psB[:, 0:TT], ONESB[:], cast[:, i * TT:(i + 1) * TT],
                        start=(i == 0), stop=False))
        b.wait("ve", c["pe"])
        inv = 1.0 / D
        if mean:
            b.on("ve", lambda v: v.tensor_scalar(
                out=MEANT[:], in0=psB[:, 0:TT], scalar1=inv, scalar2=None,
                op0=ALU.mult))
            b.on("ve", lambda v: v.tensor_scalar(
                out=RSTDT[:], in0=psB[:, TT:2 * TT], scalar1=inv, scalar2=None,
                op0=ALU.mult))
            b.on("ve", lambda v: v.tensor_mul(out=tmp[:, 0:TT], in0=MEANT[:],
                                              in1=MEANT[:]))
            b.on("ve", lambda v: v.scalar_tensor_tensor(
                out=RSTDT[:], in0=RSTDT[:], scalar=eps, in1=tmp[:, 0:TT],
                op0=ALU.add, op1=ALU.subtract))
        else:
            b.on("ve", lambda v: v.tensor_scalar(
                out=RSTDT[:], in0=psB[:, TT:2 * TT], scalar1=inv, scalar2=eps,
                op0=ALU.mult, op1=ALU.add))
        b.inc("ve", c["ve"], 1, lambda v: v.tensor_copy(out=VNOP[:, 0:1], in_=VNOP[:, 1:2]))
        b.wait("se", c["ve"])
        b.inc("se", c["se"], 1, lambda s: s.activation(out=RSTDT[:],
                                                       in_=RSTDT[:],
                                                       func=AF.Sqrt))
        b.wait("ve", c["se"])
        b.inc("ve", c["ve"], 1, lambda v: v.reciprocal(out=RSTDT[:],
                                                       in_=RSTDT[:]))

    def dump_and_finish(s, c, src):
        b.wait("sy", c["ve"])
        b.wait("sy", c["se"])
        for ch in range(NCH):
            b.inc("sy", c["out"], 16, lambda y, i=ch, d=src: y.dma_start(
                out=ys_c[s, i], in_=d[:, i * T:(i + 1) * T]))

    if stage == 45:
        stage5_qt1 = True
    def walk_seq(s, c):
        if s == 0:
            b.wait("sy", g_cmp, 2)
        else:
            for eng in ("sy", "gp", "te", "ve", "se"):
                for k in ("pe", "ve", "se", "out", "ld", "wg", "sq"):
                    b.wait(eng, C[s - 1][k])
        for ch in range(NCH):
            b.inc("sy", c["ld"], 16, lambda y, i=ch: y.dma_start(
                out=RX[:, i * T:(i + 1) * T], in_=xs_c[s, i]))
        x_loaded = c["ld"].n

        for eng in ("ve", "se", "te"):
            b.wait(eng, c["ld"], x_loaded)
            b.wait(eng, g_ld)
            b.wait(eng, g_cmp)
        for tt in range(NTT):
            stats(c, RX, tt, True, 1e-5, CASTA, SQA, DTMP)
            for ch in range(NCH):
                o = ch * T + tt * TT
                b.on("ve", lambda v, o=o: v.tensor_sub(
                    out=BF[:, o:o + TT], in0=RX[:, o:o + TT], in1=MEANT[:]))
                b.on("ve", lambda v, o=o: v.tensor_mul(
                    out=BF[:, o:o + TT], in0=BF[:, o:o + TT], in1=RSTDT[:]))
        for tt in range(NTT):
            stats(c, BF, tt, False, 1e-6, CASTA, SQA, DTMP)
            for ch in range(NCH):
                o = ch * T + tt * TT
                b.on("ve", lambda v, o=o, i=ch: v.scalar_tensor_tensor(
                    out=RX[:, o:o + TT], in0=BF[:, o:o + TT],
                    scalar=VEC["rms1w"][:, i:i + 1], in1=RSTDT[:],
                    op0=ALU.mult, op1=ALU.mult))
        for ch in range(NCH):
            o = ch * T
            b.on("ve", lambda v, o=o, i=ch: v.tensor_tensor_scan(
                out=DTMP[:, 0:T], data0=AV[:, i:i + 1].broadcast_to([P, T]),
                data1=RX[:, o:o + T], initial=0.0, op0=ALU.mult, op1=ALU.add))
            b.on("ve", lambda v, o=o, i=ch: v.scalar_tensor_tensor(
                out=BF[:, o:o + T], in0=DTMP[:, 0:T],
                scalar=VEC["cout"][:, i:i + 1], in1=BF[:, o:o + T],
                op0=ALU.mult, op1=ALU.add))
            b.on("ve", lambda v, o=o, i=ch: v.scalar_tensor_tensor(
                out=BF[:, o:o + T], in0=RX[:, o:o + T],
                scalar=VEC["dskip"][:, i:i + 1], in1=BF[:, o:o + T],
                op0=ALU.mult, op1=ALU.add))
        b.inc("ve", c["ve"], 1, lambda v: v.tensor_copy(out=VNOP[:, 0:1], in_=VNOP[:, 1:2]))
        scan_done = c["ve"].n

        if stage == 1:
            dump_and_finish(s, c, BF)
            return

        b.wait("sy", c["ve"], scan_done)
        for ch in range(NCH):
            b.inc("sy", c["ld"], 16, lambda y, i=ch: y.dma_start(
                out=RX[:, i * T:(i + 1) * T], in_=xs_c[s, i]))
        x_reloaded = c["ld"].n

        for tt in range(NTT):
            stats(c, BF, tt, False, 1e-6, CASTA, SQA, DTMP)
            for ch in range(NCH):
                o = ch * T + tt * TT
                b.on("ve", lambda v, o=o, i=ch: v.scalar_tensor_tensor(
                    out=CB1[:, o:o + TT], in0=BF[:, o:o + TT],
                    scalar=VEC["rms2w"][:, i:i + 1], in1=RSTDT[:],
                    op0=ALU.mult, op1=ALU.mult))
        b.inc("ve", c["ve"], 1, lambda v: v.tensor_copy(out=VNOP[:, 0:1], in_=VNOP[:, 1:2]))
        n2_done = c["ve"].n
        b.wait("te", c["ve"], n2_done)

        rwfc = Ring(b, 4, c["wg"], c["pe"])
        rwpr = Ring(b, 2, c["wg"], c["pe"])
        sched = []
        for tt in range(NTT):
            sched += [("fc", tt, hc) for hc in range(NHS)]
            sched += [("pr", tt, oc) for oc in range(NCH)]
        wfc_idx = {}
        wpr_idx = {}

        def emit_load(item):
            kind, tt, i = item
            if kind == "fc":
                li = rwfc.load("gp", lambda g, i=i, sl=len(rwfc.load_vals) % 4:
                               g.dma_start(out=WFC[:, sl],
                                           in_=wfc_c[:, :, i * P:(i + 1) * P]))
                wfc_idx[(tt, i)] = li
            else:
                li = rwpr.load("gp", lambda g, i=i, sl=len(rwpr.load_vals) % 2:
                               g.dma_start(out=WPR[:, sl],
                                           in_=wpr_c[:, :, i * P:(i + 1) * P]))
                wpr_idx[(tt, i)] = li
        LA = 2
        for k in range(LA):
            emit_load(sched[k])
        load_pos = LA
        ps_evac = {}
        psb_evac = {}
        for tt in range(NTT):
            for hc in range(NHS):
                li = wfc_idx[(tt, hc)]
                rwfc.use(li)
                slot = li % 4
                if (hc % 4) in ps_evac:
                    b.wait("te", c["se"], ps_evac[hc % 4])
                for ch in range(NCH):
                    if ch == NCH - 1:
                        b.inc("te", c["pe"], 1, lambda t, sl=slot, i=ch,
                              tt=tt, hc=hc: t.matmul(
                                  psA_s(hc), WFC[:, sl, i, :],
                                  CB1[:, i * T + tt * TT:i * T + tt * TT + TT],
                                  start=False, stop=True))
                    else:
                        b.on("te", lambda t, sl=slot, i=ch, tt=tt, hc=hc:
                             t.matmul(
                                 psA_s(hc), WFC[:, sl, i, :],
                                 CB1[:, i * T + tt * TT:i * T + tt * TT + TT],
                                 start=(i == 0), stop=False))
                rwfc.consumed(li)
                if load_pos < len(sched):
                    emit_load(sched[load_pos])
                    load_pos += 1
                b.wait("se", c["pe"])
                b.inc("se", c["se"], 1, lambda s_, hc=hc: s_.activation(
                    out=HID[:, hc, :], in_=psA_s(hc), func=AF.Silu))
                ps_evac[hc % 4] = c["se"].n
            b.wait("te", c["se"])
            for oc in range(NCH):
                li = wpr_idx[(tt, oc)]
                rwpr.use(li)
                slot = li % 2
                if (oc % 2) in psb_evac:
                    b.wait("te", c["ve"], psb_evac[oc % 2])
                for hc in range(NHS):
                    if hc == NHS - 1:
                        b.inc("te", c["pe"], 1, lambda t, sl=slot, h=hc, oc=oc:
                              t.matmul(psB_s(oc), WPR[:, sl, h, :],
                                       HID[:, h, :], start=False, stop=True))
                    else:
                        b.on("te", lambda t, sl=slot, h=hc, oc=oc: t.matmul(
                            psB_s(oc), WPR[:, sl, h, :], HID[:, h, :],
                            start=(h == 0), stop=False))
                rwpr.consumed(li)
                if load_pos < len(sched):
                    emit_load(sched[load_pos])
                    load_pos += 1
                b.wait("ve", c["pe"])
                b.inc("ve", c["ve"], 1, lambda v, oc=oc, tt=tt: v.tensor_add(
                    out=BF[:, oc * T + tt * TT:oc * T + tt * TT + TT],
                    in0=psB_s(oc),
                    in1=BF[:, oc * T + tt * TT:oc * T + tt * TT + TT]))
                psb_evac[oc % 2] = c["ve"].n
        if stage == 2:
            dump_and_finish(s, c, BF)
            return

        b.wait("ve", c["ld"], x_reloaded)
        for tt in range(NTT):
            stats(c, BF, tt, True, 1e-5, CASTA, SQA, DTMP)
            for ch in range(NCH):
                o = ch * T + tt * TT
                b.on("ve", lambda v, o=o: v.tensor_sub(
                    out=DTMP[:, 0:TT], in0=BF[:, o:o + TT], in1=MEANT[:]))
                b.on("ve", lambda v, o=o: v.tensor_mul(
                    out=DTMP[:, 0:TT], in0=DTMP[:, 0:TT], in1=RSTDT[:]))
                b.on("ve", lambda v, o=o: v.tensor_add(
                    out=RX[:, o:o + TT], in0=RX[:, o:o + TT],
                    in1=DTMP[:, 0:TT]))
        b.inc("ve", c["ve"], 1, lambda v: v.tensor_copy(out=VNOP[:, 0:1], in_=VNOP[:, 1:2]))
        xnew_done = c["ve"].n
        b.wait("sy", c["ve"], xnew_done)
        for ch in range(NCH):
            b.inc("sy", c["out"], 16, lambda y, i=ch: y.dma_start(
                out=xnew_c[s, i], in_=RX[:, i * T:(i + 1) * T]))
        if stage == 3:
            dump_and_finish(s, c, RX)
            return

        for tt in range(NTT):
            stats(c, RX, tt, True, 1e-5, CASTA, SQA, DTMP)
            for ch in range(NCH):
                o = ch * T + tt * TT
                b.on("ve", lambda v, o=o: v.tensor_sub(
                    out=BF[:, o:o + TT], in0=RX[:, o:o + TT], in1=MEANT[:]))
                b.on("ve", lambda v, o=o: v.tensor_mul(
                    out=BF[:, o:o + TT], in0=BF[:, o:o + TT], in1=RSTDT[:]))
        for tt in range(NTT):
            stats(c, BF, tt, True, 1e-5, CASTA, SQA, DTMP)
            for ch in range(NCH):
                o = ch * T + tt * TT
                b.on("ve", lambda v, o=o: v.tensor_sub(
                    out=DTMP[:, 0:TT], in0=BF[:, o:o + TT], in1=MEANT[:]))
                b.on("ve", lambda v, o=o: v.tensor_mul(
                    out=DTMP[:, 0:TT], in0=DTMP[:, 0:TT], in1=RSTDT[:]))
                b.on("ve", lambda v, o=o, i=ch: v.tensor_scalar(
                    out=CB1[:, o:o + TT], in0=DTMP[:, 0:TT],
                    scalar1=VEC["ln1w"][:, i:i + 1],
                    scalar2=VEC["ln1b"][:, i:i + 1],
                    op0=ALU.mult, op1=ALU.add))
        b.inc("ve", c["ve"], 1, lambda v: v.tensor_copy(out=VNOP[:, 0:1], in_=VNOP[:, 1:2]))
        ln1_done = c["ve"].n
        if stage == 4:
            dump_and_finish(s, c, BF)
            return

        b.wait("te", c["ve"], ln1_done)
        rqo = Ring(b, 2, c["wg"], c["pe"])
        qkv_idx = [rqo.load("gp", lambda g, i=ot, sl=ot % 2: g.dma_start(
            out=WQO[:, sl], in_=wqkv_c[:, :, i * P:(i + 1) * P]))
            for ot in range(24)]
        ps_free = {}
        last_evac_ve = 0
        last_evac_se = 0
        vtr = 0
        for ot in range(24):
            li = qkv_idx[ot]
            rqo.use(li)
            slot = li % 2
            b.wait("te", c["ve"], last_evac_ve)
            if OVERSYNC:
                b.wait("te", c["ve"])
                b.wait("te", c["se"])
            for ch in range(NCH):
                for tt in range(NTT):
                    lastmm = (ch == NCH - 1 and tt == NTT - 1)
                    if lastmm:
                        b.inc("te", c["pe"], 1, lambda t, sl=slot, i=ch, tt=tt:
                              t.matmul(psA_s(tt), WQO[:, sl, i, :],
                                       CB1[:, i * T + tt * TT:
                                           i * T + tt * TT + TT],
                                       start=False, stop=True))
                    else:
                        b.on("te", lambda t, sl=slot, i=ch, tt=tt: t.matmul(
                            psA_s(tt), WQO[:, sl, i, :],
                            CB1[:, i * T + tt * TT:i * T + tt * TT + TT],
                            start=(i == 0), stop=(i == NCH - 1)))
            rqo.consumed(li)
            if stage == 44 and ot == 23:
                b.wait("gp", c["pe"])
                b.inc("gp", c["out"], 16, lambda g, sl=slot: g.dma_start(
                    out=ys_c[s, 3][:, 0:1024],
                    in_=WQO[:, sl].rearrange("p a b -> p (a b)")))
            b.wait("ve", c["pe"])
            bias = BQKV[:, ot:ot + 1]
            if ot < 8:
                dst = QB[:, ot, :]
            elif ot < 16:
                dst = KBU[:, ot - 8, :]
            else:
                dst = VFM[:]
            b.inc("ve", c["ve"], 1, lambda v, d=dst, bi=bias: v.tensor_scalar(
                out=d, in0=psA[:], scalar1=bi, scalar2=None, op0=ALU.add))
            last_evac_ve = c["ve"].n
            if ot >= 16 and ATT_PART >= 1:
                vc = ot - 16
                b.wait("te", c["ve"], last_evac_ve)
                tr_marks = {}
                for k in range(16):
                    sl = k % 8
                    if k >= 1:
                        b.wait("te", c["ve"], tr_marks[k - 1])
                    b.inc("te", c["pe"], 1, lambda t, k=k, sl=sl: t.transpose(
                        psTv[:, sl * P:(sl + 1) * P], VFM[:, k * P:(k + 1) * P],
                        IDFB[:]))
                    b.wait("ve", c["pe"])
                    if TRCP_ON:
                        b.inc("ve", c["ve"], 1, lambda v, k=k, sl=sl, vc=vc:
                              v.tensor_copy(
                                  out=VT[:, k, 2 * vc * DH:(2 * vc + 2) * DH],
                                  in_=psTv[:, sl * P:(sl + 1) * P]))
                    else:
                        b.inc("ve", c["ve"], 1, lambda v, sl=sl:
                              v.tensor_copy(
                                  out=VFM[:, 0:P],
                                  in_=psTv[:, sl * P:(sl + 1) * P]))
                    tr_marks[k] = c["ve"].n
                last_evac_ve = c["ve"].n

        if stage == 5 and ATT_PART <= 1:
            dump_and_finish(s, c, BF)
            return
        for which, buf, dstm in ((0, QB, CQM), (1, KBU, CKM)):
            for hp in range(NCH):
                b.wait("se", c["ve"])
                b.inc("se", c["se"], 1, lambda s_, bu=buf, hp=hp: s_.activation(
                    out=CSQ[:], in_=bu[:, hp, :], func=AF.Square))
                b.wait("te", c["se"])
                b.wait("te", c["ve"])
                for q4 in range(4):
                    b.inc("te", c["pe"], 1, lambda t, q4=q4: t.matmul(
                        psA_s(q4), ONESB[:], CSQ[:, q4 * TT:(q4 + 1) * TT],
                        start=True, stop=True))
                b.wait("ve", c["pe"])
                b.inc("ve", c["ve"], 1, lambda v, d=dstm, hp=hp: v.reduce_max(
                    out=d[:, hp:hp + 1], in_=psA[:], axis=AX.X))
        vedge(c)
        b.wait("se", c["ve"])
        b.inc("se", c["se"], 1, lambda s_: s_.activation(out=CQM[:], in_=CQM[:],
                                                         func=AF.Sqrt))
        b.on("se", lambda s_, v=c["se"].n: s_.wait_ge(c["se"].sem, v))
        b.inc("se", c["se"], 1, lambda s_: s_.activation(out=CKM[:], in_=CKM[:],
                                                         func=AF.Sqrt))
        b.wait("ve", c["se"])
        b.inc("ve", c["ve"], 1, lambda v: v.tensor_mul(out=CBD[:], in0=CQM[:],
                                                       in1=CKM[:]))
        vedge(c)
        b.inc("ve", c["ve"], 1, lambda v: v.tensor_scalar(
            out=CBD[:], in0=CBD[:], scalar1=-0.125, scalar2=None, op0=ALU.mult))
        vedge(c)
        cbd_done = c["ve"].n

        if stage == 5 and ATT_PART == 2:
            dump_and_finish(s, c, BF)
            return
        b.wait("te", c["ve"], cbd_done)
        b.wait("se", c["ve"], cbd_done)
        rwo = Ring(b, 2, c["wg"], c["pe"])
        n_qt = 1 if (stage in (44, 45) or ATT_PART < 4) else QT_N
        n_hp = HP_N if ATT_PART >= 4 else 1
        wo_idx = {}
        for qt in range(n_qt):
            for oc in range(NCH):
                wo_idx[(qt, oc)] = rwo.load(
                    "gp", lambda g, i=oc, sl=(qt * NCH + oc) % 2: g.dma_start(
                        out=WQO[:, sl], in_=wo_c[:, :, i * P:(i + 1) * P]))
        last_exp = 0
        last_ctx_evac = 0
        for qt in range(n_qt):
            for hp in range(n_hp):
                b.wait("te", c["ve"], last_ctx_evac)
                for ktp in range(8):
                    b.wait("te", c["se"], last_exp)
                    for j in range(2):
                        kt = 2 * ktp + j
                        for half in range(2):
                            lastmm = (j == 1 and half == 1)
                            def smm(t, j=j, half=half, kt=kt, hp=hp, qt=qt):
                                return t.matmul(
                                    psA_s(2 * j + half),
                                    KBU[half * DH:(half + 1) * DH, hp,
                                        kt * P:(kt + 1) * P],
                                    QB[half * DH:(half + 1) * DH, hp,
                                       qt * TT:(qt + 1) * TT],
                                    start=True, stop=True,
                                    tile_position=(half * DH, 0))
                            if lastmm:
                                b.inc("te", c["pe"], 1, smm)
                            else:
                                b.on("te", smm)
                    b.wait("se", c["pe"])
                    pt = (PT0, PT1)[ktp % 2]
                    b.inc("se", c["se"], 1, lambda s_, pt=pt, hp=hp:
                          s_.activation(out=pt[:], in_=psA[:], func=AF.Exp,
                                        scale=0.125,
                                        bias=CBD[:, hp:hp + 1]))
                    last_exp = c["se"].n
                    b.wait("te", c["se"], last_exp)
                    first = ktp == 0
                    lastk = ktp == 7
                    for j in range(2):
                        kt = 2 * ktp + j
                        st = first and j == 0
                        sp = lastk and j == 1
                        b.on("te", lambda t, pt=pt, j=j, kt=kt, hp=hp, st=st,
                             sp=sp: t.matmul(
                                 pb6[0:DH, :],
                                 VT[:, kt, (2 * hp) * DH:(2 * hp + 1) * DH],
                                 pt[:, (2 * j) * TT:(2 * j + 1) * TT],
                                 start=st, stop=sp))
                        b.on("te", lambda t, pt=pt, j=j, kt=kt, hp=hp, st=st,
                             sp=sp: t.matmul(
                                 pb7[DH:P, :],
                                 VT[:, kt, (2 * hp + 1) * DH:(2 * hp + 2) * DH],
                                 pt[:, (2 * j + 1) * TT:(2 * j + 2) * TT],
                                 start=st, stop=sp, tile_position=(0, DH)))
                        b.on("te", lambda t, pt=pt, j=j, st=st, sp=sp:
                             t.matmul(
                                 psB[0:DH, 0:TT], ONESB[:, 0:DH],
                                 pt[:, (2 * j) * TT:(2 * j + 1) * TT],
                                 start=st, stop=sp))
                        def smm2(t, pt=pt, j=j, st=st, sp=sp):
                            return t.matmul(
                                psB[DH:P, TT:2 * TT], ONESB[:, 0:DH],
                                pt[:, (2 * j + 1) * TT:(2 * j + 2) * TT],
                                start=st, stop=sp, tile_position=(0, DH))
                        if sp:
                            b.inc("te", c["pe"], 1, smm2)
                        else:
                            b.on("te", smm2)
                b.wait("ve", c["pe"])
                if stage == 44 and qt == 0 and hp == 1:
                    b.on("ve", lambda v: v.tensor_copy(out=MEANT[:],
                                                       in_=psB[:, 0:TT]))
                    b.inc("ve", c["ve"], 1, lambda v: v.tensor_copy(
                        out=RSTDT[:], in_=psB[:, TT:2 * TT]))
                b.on("ve", lambda v: v.reciprocal(out=SR[0:DH, :],
                                                  in_=psB[0:DH, 0:TT]))
                b.on("ve", lambda v: v.reciprocal(out=SR[DH:P, :],
                                                  in_=psB[DH:P, TT:2 * TT]))
                b.on("ve", lambda v, hp=hp: v.tensor_mul(
                    out=CTXB[0:DH, hp * TT:(hp + 1) * TT], in0=pb6[0:DH, :],
                    in1=SR[0:DH, :]))
                b.inc("ve", c["ve"], 1, lambda v, hp=hp: v.tensor_mul(
                    out=CTXB[DH:P, hp * TT:(hp + 1) * TT], in0=pb7[DH:P, :],
                    in1=SR[DH:P, :]))
                last_ctx_evac = c["ve"].n
            b.wait("te", c["ve"], last_ctx_evac)
            opj_evac = {}
            for oc in range(NCH):
                li = wo_idx[(qt, oc)]
                rwo.use(li)
                slot = li % 2
                if (oc % 4) in opj_evac:
                    b.wait("te", c["ve"], opj_evac[oc % 4])
                for hp in range(NCH):
                    lastmm = hp == NCH - 1
                    def omm(t, sl=slot, hp=hp, oc=oc):
                        return t.matmul(psA_s(oc), WQO[:, sl, hp, :],
                                        CTXB[:, hp * TT:(hp + 1) * TT],
                                        start=(hp == 0), stop=(hp == NCH - 1))
                    if lastmm:
                        b.inc("te", c["pe"], 1, omm)
                    else:
                        b.on("te", omm)
                rwo.consumed(li)
                b.wait("ve", c["pe"])
                b.inc("ve", c["ve"], 1, lambda v, oc=oc, qt=qt: (
                    v.scalar_tensor_tensor(
                        out=BF[:, oc * T + qt * TT:oc * T + qt * TT + TT],
                        in0=psA_s(oc), scalar=VEC["bo"][:, oc:oc + 1],
                        in1=BF[:, oc * T + qt * TT:oc * T + qt * TT + TT],
                        op0=ALU.add, op1=ALU.add)))
                opj_evac[oc % 4] = c["ve"].n
            last_ctx_evac = c["ve"].n
            if stage == 44 and qt == 0:
                b.wait("gp", c["ve"])
                b.inc("gp", c["out"], 16, lambda g: g.dma_start(
                    out=ys_c[s, 0][:, 0:1024], in_=VT[:, 0, :]))
                b.inc("gp", c["out"], 16, lambda g: g.dma_start(
                    out=ys_c[s, 0][:, 1024:2048], in_=VT[:, 5, :]))

                b.inc("gp", c["out"], 16, lambda g: g.dma_start(
                    out=ys_c[s, 1][:, 0:TT], in_=MEANT[:]))
                b.inc("gp", c["out"], 16, lambda g: g.dma_start(
                    out=ys_c[s, 1][:, TT:2 * TT], in_=RSTDT[:]))
                b.inc("gp", c["out"], 16, lambda g: g.dma_start(
                    out=ys_c[s, 2][:, 0:NCH], in_=CBD[:]))
                return
            if stage == 45 and qt == 0:
                b.wait("gp", c["ve"])
                b.inc("gp", c["out"], 16, lambda g: g.dma_start(
                    out=ys_c[s, 0], in_=CTXB[:, 0:T]))
                b.inc("gp", c["out"], 16, lambda g: g.dma_start(
                    out=ys_c[s, 1], in_=CTXB[:, T:2 * T]))
                b.inc("gp", c["out"], 16, lambda g: g.dma_start(
                    out=ys_c[s, 2][:, 0:NCH], in_=CBD[:]))
                b.inc("gp", c["out"], 16, lambda g: g.dma_start(
                    out=ys_c[s, 2][:, 8:16], in_=CQM[:]))
                b.inc("gp", c["out"], 16, lambda g: g.dma_start(
                    out=ys_c[s, 2][:, 16:24], in_=CKM[:]))
                b.inc("gp", c["out"], 16, lambda g: g.dma_start(
                    out=ys_c[s, 2][:, 512:1024], in_=SR[:]))
                return
        if stage == 5:
            dump_and_finish(s, c, BF)
            return

        for tt in range(NTT):
            stats(c, BF, tt, True, 1e-5, CASTA, SQA, DTMP)
            for ch in range(NCH):
                o = ch * T + tt * TT
                b.on("ve", lambda v, o=o: v.tensor_sub(
                    out=DTMP[:, 0:TT], in0=BF[:, o:o + TT], in1=MEANT[:]))
                b.on("ve", lambda v, o=o: v.tensor_mul(
                    out=DTMP[:, 0:TT], in0=DTMP[:, 0:TT], in1=RSTDT[:]))
                b.on("ve", lambda v, o=o, i=ch: v.tensor_scalar(
                    out=CB1[:, o:o + TT], in0=DTMP[:, 0:TT],
                    scalar1=VEC["ln2w"][:, i:i + 1],
                    scalar2=VEC["ln2b"][:, i:i + 1],
                    op0=ALU.mult, op1=ALU.add))
                b.inc("ve", c["ve"], 1, lambda v, i=ch: v.tensor_scalar(
                    out=DTMP[:, TT:2 * TT], in0=DTMP[:, 0:TT],
                    scalar1=VEC["ln2w"][:, i:i + 1],
                    scalar2=VEC["ln2b"][:, i:i + 1],
                    op0=ALU.mult, op1=ALU.add))
                b.wait("te", c["ve"])
                for p4 in range(4):
                    lastmm = ch == NCH - 1 and p4 == 3
                    def gmm(t, ch=ch, tt=tt, p4=p4):
                        sl = psA[:, p4 * TT:p4 * TT + NE]
                        return t.matmul(sl,
                                        DTMP[:, TT + p4 * P:TT + (p4 + 1) * P],
                                        WGF[:, ch, :],
                                        start=(ch == 0), stop=(ch == NCH - 1))
                    if lastmm:
                        b.inc("te", c["pe"], 1, gmm)
                    else:
                        b.on("te", gmm)
            b.wait("ve", c["pe"])
            for p4 in range(4):
                pt = tt * 4 + p4
                last = p4 == 3
                def zev(v, pt=pt, p4=p4):
                    return v.tensor_copy(
                        out=ZB[:, pt * NE:(pt + 1) * NE],
                        in_=psA[:, p4 * TT:p4 * TT + NE])
                if last:
                    b.inc("ve", c["ve"], 1, zev)
                else:
                    b.on("ve", zev)
        b.inc("ve", c["ve"], 1, lambda v: v.tensor_copy(out=VNOP[:, 0:1], in_=VNOP[:, 1:2]))
        ln2_done = c["ve"].n
        b.wait("te", c["ve"], ln2_done)
        zb3 = ZB[:].rearrange("p (a b) -> p a b", b=NE)
        ge3 = GE[:].rearrange("p (a b) -> p a b", b=NE)
        gp3 = GP_[:].rearrange("p (a b) -> p a b", b=NE)
        gmk3 = GMK[:].rearrange("p (a b) -> p a b", b=NE)
        gind3 = GIND[:].rearrange("p (a b) -> p a b", b=NE)
        wet3 = WETM[:].rearrange("p (a b) -> p a b", b=NE)
        b.on("ve", lambda v: v.reduce_max(out=GM1[:, :, None], in_=zb3,
                                          axis=AX.X, negate=True))
        vedge(c)
        b.on("ve", lambda v: v.tensor_add(
            out=ge3, in0=zb3, in1=GM1[:, :, None].broadcast_to([P, 16, NE])))
        b.inc("ve", c["ve"], 1, lambda v: v.tensor_copy(out=VNOP[:, 0:1], in_=VNOP[:, 1:2]))
        b.wait("se", c["ve"])
        b.inc("se", c["se"], 1, lambda s_: s_.activation(out=GE[:], in_=GE[:],
                                                         func=AF.Exp))
        b.wait("ve", c["se"])
        b.on("ve", lambda v: v.reduce_sum(out=GS[:, :, None], in_=ge3,
                                          axis=AX.X))
        vedge(c)
        b.on("ve", lambda v: v.reciprocal(out=GS[:], in_=GS[:]))
        vedge(c)
        b.on("ve", lambda v: v.tensor_mul(
            out=gp3, in0=ge3, in1=GS[:, :, None].broadcast_to([P, 16, NE])))
        b.on("ve", lambda v: v.reduce_max(out=GM1[:, :, None], in_=gp3,
                                          axis=AX.X))
        vedge(c)
        b.on("ve", lambda v: v.tensor_tensor(
            out=gmk3, in0=gp3, in1=GM1[:, :, None].broadcast_to([P, 16, NE]),
            op=ALU.is_equal))
        b.on("ve", lambda v: v.scalar_tensor_tensor(
            out=GMK[:], in0=GMK[:], scalar=-1e30, in1=GP_[:], op0=ALU.mult,
            op1=ALU.add))
        b.on("ve", lambda v: v.reduce_max(out=GM2[:, :, None], in_=gmk3,
                                          axis=AX.X))
        vedge(c)
        b.on("ve", lambda v: v.tensor_tensor(
            out=gind3, in0=gp3, in1=GM2[:, :, None].broadcast_to([P, 16, NE]),
            op=ALU.is_ge))
        b.on("ve", lambda v: v.tensor_add(out=GM1[:], in0=GM1[:], in1=GM2[:]))
        vedge(c)
        b.on("ve", lambda v: v.reciprocal(out=GM1[:], in_=GM1[:]))
        vedge(c)
        b.on("ve", lambda v: v.tensor_mul(out=gp3, in0=gp3, in1=gind3))
        b.inc("ve", c["ve"], 1, lambda v: v.tensor_mul(
            out=wet3, in0=gp3, in1=GM1[:, :, None].broadcast_to([P, 16, NE])))
        we_done = c["ve"].n
        b.wait("te", c["ve"], we_done)
        tr_evac = {}
        for pt in range(16):
            if pt - 1 in tr_evac:
                b.wait("te", c["ve"], tr_evac[pt - 1])
            b.inc("te", c["pe"], 1, lambda t, pt=pt: t.transpose(
                pb7[0:NE, (pt % 4) * P:(pt % 4 + 1) * P],
                WETM[:, pt * NE:(pt + 1) * NE], IDF[:]))
            b.wait("ve", c["pe"])
            b.inc("ve", c["ve"], 1, lambda v, pt=pt: v.tensor_copy(
                out=WEFM[:, pt * P:(pt + 1) * P],
                in_=pb7[0:NE, (pt % 4) * P:(pt % 4 + 1) * P]))
            tr_evac[pt] = c["ve"].n
        if stage >= 6 and MOE_PART == 1:
            dump_and_finish(s, c, BF)
            return

        if stage >= 6 and MOE_PART == 0:
            dump_and_finish(s, c, BF)
            return
        b.wait("sy", c["ve"])
        b.inc("sy", c["ld"], 16, lambda y: y.dma_start(out=B2F[:],
                                                       in_=dt["b2l"]))
        b2f_ld = c["ld"].n
        b.wait("te", c["ld"], b2f_ld)
        first_core_pass = (s == 0)
        for ttg in range(NTT):
            t0 = ttg * TT
            b.wait("te", c["ve"])
            acc_evac = {}
            for oc in range(NCH):
                if (oc % 4) in acc_evac:
                    b.wait("te", c["ve"], acc_evac[oc % 4])
                b.inc("te", c["pe"], 1, lambda t, oc=oc, t0=t0: t.matmul(
                    psA_s(oc), B2F[:, oc * P:(oc + 1) * P],
                    WEFM[:, t0:t0 + TT], start=True, stop=True))
                b.wait("ve", c["pe"])
                b.inc("ve", c["ve"], 1, lambda v, oc=oc, t0=t0: v.tensor_add(
                    out=BF[:, oc * T + t0:oc * T + t0 + TT], in0=psA_s(oc),
                    in1=BF[:, oc * T + t0:oc * T + t0 + TT]))
                acc_evac[oc % 4] = c["ve"].n
            if stage >= 6 and MOE_PART == 2 and ttg == 1:
                break
            first_pass = first_core_pass and ttg == 0

            def mk_loader(ring, ldq, wb_vals, dram_first, dram_scr, scr_back):
                def do_load(e, he):
                    li = len(ring.load_vals)
                    if first_pass:
                        if li >= ring.nslots:
                            b.on(ldq, lambda g, j=li - ring.nslots:
                                 g.wait_ge(g_wb.sem, wb_vals[j]))
                        ring.load(ldq, lambda g, e=e, he=he, sl=li % 2:
                                  g.dma_start(out=dram_first[0][:, sl],
                                              in_=dram_first[1](e, he)))
                        b.wait("sy", c["wg"], ring.load_vals[li])
                        b.inc("sy", g_wb, 16, lambda y, e=e, he=he, sl=li % 2:
                              y.dma_start(out=scr_back(e, he),
                                          in_=dram_first[0][:, sl]))
                        wb_vals[li] = g_wb.n
                    else:
                        ring.load(ldq, lambda s_, e=e, he=he, sl=li % 2:
                                  s_.dma_start(out=dram_first[0][:, sl],
                                               in_=dram_scr(e, he)))
                    return li
                return do_load

            ldq = "gp" if first_pass else "se"
            ldc = c["wg"] if first_pass else c["sq"]
            if not first_pass and s == 0:
                b.wait("se", g_wb)
            r1 = Ring(b, 2, ldc, c["pe"])
            r2 = Ring(b, 2, ldc, c["pe"])
            wb1, wb2 = {}, {}
            load_w1 = mk_loader(
                r1, ldq, wb1,
                (W1E, lambda e, he: w1_c[e][:, :, he * TT:(he + 1) * TT]),
                lambda e, he: w1b_c[e][:, :, he * TT:(he + 1) * TT],
                lambda e, he: w1b_c[e][:, :, he * TT:(he + 1) * TT])
            load_w2 = mk_loader(
                r2, ldq, wb2,
                (W2E, lambda e, he: w2_c[e][:, he * 4:(he + 1) * 4, :]),
                lambda e, he: w2b_c[e][:, he * 4:(he + 1) * 4, :],
                lambda e, he: w2b_c[e][:, he * 4:(he + 1) * 4, :])

            for e in range(NE if MOE_PART >= 9 else (MOE_PART - 2 if MOE_PART >= 3 else 0)):
                b.wait("gp", c["pe"])
                b.inc("gp", c["wg"], 16, lambda g, e=e, t0=t0: g.dma_start(
                    out=FTMP[0:1, 0:TT], in_=WEFM[e:e + 1, t0:t0 + TT]))
                werow_ld = c["wg"].n
                l1 = [load_w1(e, 0), load_w1(e, 1)]
                b.wait("te", c["ve"])
                gelu_mark = {}
                for he in range(NE):
                    li = l1[he]
                    r1.use(li)
                    slot = li % 2
                    for hh in range(4):
                        hcg = he * 4 + hh
                        if (hcg % 4) in gelu_mark:
                            b.wait("te", c["se"], gelu_mark[hcg % 4])
                        for ch in range(NCH):
                            lastmm = ch == NCH - 1
                            def amm(t, sl=slot, ch=ch, hh=hh, hcg=hcg, t0=t0):
                                return t.matmul(
                                    psA_s(hcg),
                                    W1E[:, sl, ch, hh * P:(hh + 1) * P],
                                    CB1[:, ch * T + t0:ch * T + t0 + TT],
                                    start=(ch == 0), stop=(ch == NCH - 1))
                            if lastmm:
                                b.inc("te", c["pe"], 1, amm)
                            else:
                                b.on("te", amm)
                        b.wait("se", c["pe"])
                        b.inc("se", c["se"], 1, lambda s_, hcg=hcg, e=e:
                              s_.activation(
                                  out=GBUF[:, hcg, :], in_=psA_s(hcg),
                                  func=AF.Gelu_apprx_tanh,
                                  bias=B1S[:, e * NHM + hcg:e * NHM + hcg + 1]))
                        gelu_mark[hcg % 4] = c["se"].n
                    r1.consumed(li)
                    if he + 2 < NE:
                        l1.append(load_w1(e, he + 2))
                b.wait("te", c["wg"], werow_ld)
                b.wait("te", c["ve"])
                b.inc("te", c["pe"], 1, lambda t: t.matmul(
                    pb6[:], ONESR[:], FTMP[0:1, 0:TT], start=True, stop=True))
                b.wait("ve", c["pe"])
                b.inc("ve", c["ve"], 1, lambda v: v.tensor_copy(out=WEB[:],
                                                                in_=pb6[:]))
                web_done = c["ve"].n
                b.wait("te", c["se"])
                b.wait("te", c["ve"])
                l2 = [load_w2(e, 0), load_w2(e, 1)]
                for he in range(NE):
                    li = l2[he]
                    r2.use(li)
                    slot = li % 2
                    for oc in range(NCH):
                        for hh in range(4):
                            hcg = he * 4 + hh
                            lastmm = (oc == NCH - 1 and hh == 3)
                            def bmm(t, sl=slot, oc=oc, hh=hh, hcg=hcg, he=he):
                                return t.matmul(
                                    moe_ps(oc),
                                    W2E[:, sl, hh, oc * P:(oc + 1) * P],
                                    GBUF[:, hcg, :],
                                    start=(he == 0 and hh == 0),
                                    stop=(he == NE - 1 and hh == 3))
                            if lastmm:
                                b.inc("te", c["pe"], 1, bmm)
                            else:
                                b.on("te", bmm)
                    r2.consumed(li)
                    if he + 2 < NE:
                        l2.append(load_w2(e, he + 2))
                b.wait("ve", c["pe"])
                b.wait("ve", c["ve"], web_done)
                for oc in range(NCH):
                    b.on("ve", lambda v, oc=oc: v.tensor_mul(
                        out=MTMP[:], in0=moe_ps(oc), in1=WEB[:]))
                    b.on("ve", lambda v, oc=oc, t0=t0: v.tensor_add(
                        out=BF[:, oc * T + t0:oc * T + t0 + TT], in0=MTMP[:],
                        in1=BF[:, oc * T + t0:oc * T + t0 + TT]))
                b.inc("ve", c["ve"], 1, lambda v: v.tensor_copy(out=VNOP[:, 0:1], in_=VNOP[:, 1:2]))
            if stage == 6 or MOE_PART < 9:
                continue
            stats(c, BF, ttg, True, 1e-5, MCAST, MSQ, FTMP)
            b.wait("sy", c["pe"])
            xnt_ld = []
            for ch in range(NCH):
                b.inc("sy", c["ld"], 16, lambda y, i=ch, t0=t0: y.dma_start(
                    out=XNT8[:, i, :], in_=xnew_c[s, i][:, t0:t0 + TT]))
                xnt_ld.append(c["ld"].n)
            for ch in range(NCH):
                o = ch * T + t0
                b.on("ve", lambda v, o=o: v.tensor_sub(
                    out=FTMP[:], in0=BF[:, o:o + TT], in1=MEANT[:]))
                b.on("ve", lambda v, o=o: v.tensor_mul(
                    out=FTMP[:], in0=FTMP[:], in1=RSTDT[:]))
                b.wait("ve", c["ld"], xnt_ld[ch])
                b.inc("ve", c["ve"], 1, lambda v, o=o, i=ch: v.tensor_add(
                    out=BF[:, o:o + TT], in0=XNT8[:, i, :], in1=FTMP[:]))
            b.wait("sy", c["ve"])
            for ch in range(NCH):
                b.inc("sy", c["out"], 16, lambda y, i=ch, t0=t0: y.dma_start(
                    out=ys_c[s, i][:, t0:t0 + TT],
                    in_=BF[:, i * T + t0:i * T + t0 + TT]))
        if stage == 6 or (stage >= 6 and MOE_PART < 9):
            dump_and_finish(s, c, BF)
            return

    for s in range(NSEQ):
        walk_seq(s, C[s])

    for s in range(NSEQ):
        b.wait("sy", C[s]["out"])

    blk = b.stack.enter_context(nc.Block())
    b.flush(blk)
    b.stack.close()
    return nc


IDFB = None


def _prep_inputs(inputs):
    x = inputs["L0_attn_x"]
    v128 = lambda v: np.ascontiguousarray(
        np.asarray(v, dtype=np.float32).reshape(NCH, P).T)
    common = {
        "wfc": np.ascontiguousarray(np.asarray(inputs["L0_attn_ff_fc_w"]).T),
        "wpr": np.ascontiguousarray(np.asarray(inputs["L0_attn_ff_proj_w"]).T),
        "wqkv": np.ascontiguousarray(inputs["L0_ffn_w_qkv"]),
        "wo": np.ascontiguousarray(inputs["L0_ffn_w_o"]),
        "wg": np.ascontiguousarray(inputs["L0_ffn_w_gate"]),
        "w1": np.ascontiguousarray(inputs["L0_ffn_w1"]),
        "w2": np.ascontiguousarray(inputs["L0_ffn_w2"]),
        "rms1w": v128(inputs["L0_attn_input_norm_w"]),
        "alog": v128(inputs["L0_attn_a_log"]),
        "cout": v128(inputs["L0_attn_c_out"]),
        "dskip": v128(inputs["L0_attn_d_skip"]),
        "rms2w": v128(inputs["L0_attn_ff_norm_w"]),
        "ln1w": v128(inputs["L0_ffn_ln1_w"]),
        "ln1b": v128(inputs["L0_ffn_ln1_b"]),
        "ln2w": v128(inputs["L0_ffn_ln2_w"]),
        "ln2b": v128(inputs["L0_ffn_ln2_b"]),
        "bo": v128(inputs["L0_ffn_b_o"]),
        "bqkv": np.ascontiguousarray(
            np.asarray(inputs["L0_ffn_b_qkv"], dtype=np.float32)
            .reshape(24, P).T),
        "b1s": np.ascontiguousarray(
            np.asarray(inputs["L0_ffn_b1"], dtype=np.float32)
            .reshape(NE, NHM, P).transpose(2, 0, 1).reshape(P, NE * NHM)),
        "b2l": np.ascontiguousarray(inputs["L0_ffn_b2"]),
    }
    common = {k: np.asarray(v, dtype=np.float32) for k, v in common.items()}
    in_maps = []
    for core in range(8):
        xs = np.ascontiguousarray(
            np.asarray(x[4 * core:4 * core + 4]).transpose(0, 2, 1),
            dtype=np.float32)
        m = {"xs": xs}
        m.update(common)
        in_maps.append(m)
    return in_maps


def kernel(**inputs):
    in_maps = _prep_inputs(inputs)
    nc = build_program(STAGE_FULL)
    res = run_bass_kernel_spmd(nc, in_maps, core_ids=list(range(8)))
    out = np.empty((32, T, D), dtype=np.float32)
    for core in range(8):
        ys = res.results[core]["ys"]
        for i in range(NSEQ):
            out[4 * core + i] = ys[i].T
    return out


# version 02
# speedup vs baseline: 1.0732x; 1.0732x over previous
import contextlib
import numpy as np
import concourse.bass as bass
import concourse.mybir as mybir
from concourse.bass_utils import run_bass_kernel_spmd

F32 = mybir.dt.float32
BF16 = mybir.dt.bfloat16
AF = mybir.ActivationFunctionType
ALU = mybir.AluOpType
AX = mybir.AxisListType

P = 128
NSEQ = 4
T = 2048
D = 1024
NCH = 8
HS = 2048
NHS = 16
NH = 16
DH = 64
HM = 4096
NHM = 32
NE = 8
NTT = 4
TT = 512
KB = 1024

STAGE_FULL = 7
ATT_PART = 4
TRCP_ON = True
QT_N = 4
HP_N = 8
OVERSYNC = False
MOE_PART = 9


class Ctr:
    __slots__ = ("sem", "n")

    def __init__(self, sem):
        self.sem = sem
        self.n = 0


class Bld:
    def __init__(self, nc):
        self.nc = nc
        self.q = {"sy": [], "gp": [], "te": [], "ve": [], "se": []}
        self.stack = contextlib.ExitStack()

    def on(self, eng, fn):
        self.q[eng].append(fn)

    def ctr(self, name):
        return Ctr(self.stack.enter_context(self.nc.semaphore(name)))

    def wait(self, eng, c, v=None):
        v = c.n if v is None else v
        if v > 0:
            self.on(eng, lambda e, s=c.sem, vv=v: e.wait_ge(s, vv))

    def inc(self, eng, c, k, fn):
        c.n += k
        self.on(eng, lambda e, f=fn, s=c.sem, kk=k: f(e).then_inc(s, kk))

    def make_arena(self, total_bytes):
        self.arena = self.stack.enter_context(
            self.nc.sbuf_tensor("arena", [P, total_bytes], mybir.dt.uint8))
        self.arena_base = self.nc.lookup_mloc(self.arena).addr

    def at(self, name, off_bytes, shape, dtype):
        nc = self.nc
        h = nc.virtual_sbuf_tensor(name, list(shape), dtype)
        nc._tensor(h.name, list(shape), dtype, type="SB")
        mloc = nc.lookup_mloc(h)
        mloc.allocated = not nc.target_bir_lowering
        mloc.addr = self.arena_base + off_bytes
        return h

    def flush(self, blk):
        for name, dec in (("sy", blk.sync), ("gp", blk.gpsimd),
                          ("te", blk.tensor), ("ve", blk.vector),
                          ("se", blk.scalar)):
            fns = self.q[name]
            self.q[name] = []
            if fns:
                def runner(engine, fns=fns):
                    for f in fns:
                        f(engine)
                dec(runner)


class Ring:

    def __init__(self, b, nslots, load_ctr, cons_ctr):
        self.b = b
        self.nslots = nslots
        self.load_ctr = load_ctr
        self.cons_ctr = cons_ctr
        self.load_vals = []
        self.cons_vals = {}
        self.floor = cons_ctr.n

    def load(self, eng, fn):
        i = len(self.load_vals)
        if i >= self.nslots:
            j = i - self.nslots
            self.b.on(eng, lambda e, r=self, j=j: e.wait_ge(
                r.cons_ctr.sem, r.cons_vals[j]))
        elif self.floor > 0:
            self.b.wait(eng, self.cons_ctr, self.floor)
        self.b.inc(eng, self.load_ctr, 16, fn)
        self.load_vals.append(self.load_ctr.n)
        return i

    def use(self, i, eng="te"):
        self.b.wait(eng, self.load_ctr, self.load_vals[i])

    def consumed(self, i):
        self.cons_vals[i] = self.cons_ctr.n


def build_program(stage=STAGE_FULL):
    nc = bass.Bass("TRN2", target_bir_lowering=False)
    b = Bld(nc)

    dt = {}
    dt["xs"] = nc.dram_tensor("xs", [NSEQ, D, T], F32, kind="ExternalInput").ap()
    for n_, sh in [("wfc", [D, HS]), ("wpr", [HS, D]), ("wqkv", [D, 3 * D]),
                   ("wo", [D, D]), ("wg", [D, NE]), ("w1", [NE, D, HM]),
                   ("w2", [NE, HM, D]), ("bqkv", [P, 24]),
                   ("b1s", [P, NE * NHM]), ("b2l", [NE, D])]:
        dt[n_] = nc.dram_tensor(n_, sh, F32, kind="ExternalInput").ap()
    for v in ["rms1w", "alog", "cout", "dskip", "rms2w",
              "ln1w", "ln1b", "ln2w", "ln2b", "bo"]:
        dt[v] = nc.dram_tensor(v, [P, NCH], F32, kind="ExternalInput").ap()
    dt["ys"] = nc.dram_tensor("ys", [NSEQ, D, T], F32, kind="ExternalOutput").ap()
    dt["xnew_s"] = nc.dram_tensor("xnew_s", [NSEQ, D, T], F32, kind="Internal").ap()
    dt["w1b"] = nc.dram_tensor("w1b", [NE, D, HM], BF16, kind="Internal").ap()
    dt["w2b"] = nc.dram_tensor("w2b", [NE, HM, D], BF16, kind="Internal").ap()

    xs_c = dt["xs"].rearrange("s (c p) t -> s c p t", p=P)
    ys_c = dt["ys"].rearrange("s (c p) t -> s c p t", p=P)
    xnew_c = dt["xnew_s"].rearrange("s (c p) t -> s c p t", p=P)
    wfc_c = dt["wfc"].rearrange("(c p) h -> p c h", p=P)
    wpr_c = dt["wpr"].rearrange("(c p) o -> p c o", p=P)
    wqkv_c = dt["wqkv"].rearrange("(c p) o -> p c o", p=P)
    wo_c = dt["wo"].rearrange("(c p) o -> p c o", p=P)
    wg_c = dt["wg"].rearrange("(c p) e -> p c e", p=P)
    w1_c = dt["w1"].rearrange("e (c p) h -> e p c h", p=P)
    w2_c = dt["w2"].rearrange("e (c p) o -> e p c o", p=P)
    w1b_c = dt["w1b"].rearrange("e (c p) h -> e p c h", p=P)
    w2b_c = dt["w2b"].rearrange("e (c p) o -> e p c o", p=P)

    b.make_arena(207 * KB + 768)
    A_ = b.at
    RX = A_("RX", 0, [P, NCH * T], F32)
    QB = A_("QB", 0, [P, NCH, T], BF16)
    KBU = A_("KBU", 32 * KB, [P, NCH, T], BF16)
    GBUF = A_("GBUF", 0, [P, NHM, TT], BF16)
    MCAST = A_("MCAST", 32 * KB, [P, NCH * TT], BF16)
    MSQ = A_("MSQ", 40 * KB, [P, NCH * TT], BF16)
    XNT8 = A_("XNT8", 32 * KB, [P, NCH, TT], F32)
    B2F = A_("B2F", 60 * KB, [NE, D], F32)
    WEFM = A_("WEFM", 48 * KB, [NE, T], F32)
    ZB = A_("ZB", 56 * KB, [P, P], F32)
    GE = A_("GE", 56 * KB + 512, [P, P], F32)
    GP_ = A_("GP", 57 * KB, [P, P], F32)
    GMK = A_("GMK", 57 * KB + 512, [P, P], F32)
    GIND = A_("GIND", 58 * KB, [P, P], F32)
    GM1 = A_("GM1", 58 * KB + 512, [P, 16], F32)
    GM2 = A_("GM2", 58 * KB + 576, [P, 16], F32)
    GS = A_("GS", 58 * KB + 640, [P, 16], F32)
    WETM = A_("WETM", 59 * KB, [P, P], F32)
    BF = A_("BF", 64 * KB, [P, NCH * T], F32)
    CB1 = A_("CB1", 128 * KB, [P, NCH * T], BF16)
    PT0 = A_("PT0", 128 * KB, [P, 4 * TT], BF16)
    PT1 = A_("PT1", 132 * KB, [P, 4 * TT], BF16)
    CTXB = A_("CTXB", 136 * KB, [P, NCH * TT], BF16)
    SR = A_("SR", 144 * KB, [P, TT], F32)
    CSQ = A_("CSQ", 146 * KB, [P, T], BF16)
    CQM = A_("CQM", 150 * KB, [P, NCH], F32)
    CKM = A_("CKM", 150 * KB + 64, [P, NCH], F32)
    CBD = A_("CBD", 150 * KB + 128, [P, NCH], F32)
    VFM = A_("VFM", 198 * KB, [P, T], BF16)
    CASTA = A_("CASTA", 160 * KB, [P, NCH * TT], BF16)
    SQA = A_("SQA", 168 * KB, [P, NCH * TT], BF16)
    HID = A_("HID", 160 * KB, [P, NHS, TT], BF16)
    WFC = A_("WFC", 176 * KB, [P, 4, NCH, P], BF16)
    WPR = A_("WPR", 184 * KB, [P, 2, NHS, P], BF16)
    VT = A_("VT", 160 * KB, [P, 16, NH * DH], BF16)
    WQO = A_("WQO", 192 * KB + 2 * KB, [P, 2, NCH, P], BF16)
    W1E = A_("W1E", 160 * KB, [P, 2, NCH, TT], BF16)
    W2E = A_("W2E", 176 * KB, [P, 2, 4, D], BF16)
    WEB = A_("WEB", 192 * KB, [P, TT], F32)
    MTMP = A_("MTMP", 194 * KB, [P, TT], F32)
    MEANT = A_("MEANT", 198 * KB, [P, TT], F32)
    RSTDT = A_("RSTDT", 200 * KB, [P, TT], F32)
    FTMP = A_("FTMP", 202 * KB, [P, TT], F32)
    DTMP = A_("DTMP", 192 * KB, [P, T], F32)
    ONESB = A_("ONESB", 204 * KB, [P, P], BF16)
    IDFB = A_("IDFB", 204 * KB + 256, [P, P], BF16)
    IDF = A_("IDF", 204 * KB + 512, [P, P], F32)
    ONESR = A_("ONESR", 205 * KB, [1, P], F32)
    VEC = {}
    off = 205 * KB + 512
    for v in ["rms1w", "alog", "cout", "dskip", "rms2w",
              "ln1w", "ln1b", "ln2w", "ln2b", "bo"]:
        VEC[v] = A_("sb_" + v, off, [P, NCH], F32)
        off += 32
    BQKV = A_("BQKV", off, [P, 24], F32); off += 96
    AV = A_("AV", off, [P, NCH], F32); off += 32
    AVT = A_("AVT", off, [P, NCH], F32); off += 32
    B1S = A_("B1S", off, [P, NE * NHM], F32); off += 1024
    WGB = A_("WGB", off, [P, NCH, NE], BF16); off += 128
    WGF = A_("WGF", off, [P, NCH, NE], F32); off += 256
    VNOP = A_("VNOP", off, [1, 8], F32); off += 32
    SNOP = A_("SNOP", off, [1, 8], F32); off += 32
    IOF = A_("IOF", 0, [P, P], F32)
    IOP = A_("IOP", 512, [P, 1], F32)
    assert off <= 207 * KB + 768, off

    psA = b.stack.enter_context(nc.psum_tensor("psA", [P, 4 * TT], F32))
    psB = b.stack.enter_context(nc.psum_tensor("psB", [P, 2 * TT], F32))
    pb6 = b.stack.enter_context(nc.psum_tensor("pb6", [P, TT], F32))
    pb7 = b.stack.enter_context(nc.psum_tensor("pb7", [P, TT], F32))
    psTv = pb6[:].bitcast(BF16)

    def psA_s(i):
        return psA[:, (i % 4) * TT:((i % 4) + 1) * TT]

    def psB_s(i):
        return psB[:, (i % 2) * TT:((i % 2) + 1) * TT]

    def moe_ps(oc):
        if oc < 4:
            return psA_s(oc)
        if oc < 6:
            return psB_s(oc - 4)
        return pb6[:] if oc == 6 else pb7[:]

    g_ld = b.ctr("g_ld")
    g_cmp = b.ctr("g_cmp")
    g_wb = b.ctr("g_wb")
    C = []
    for s in range(NSEQ):
        C.append({k: b.ctr(f"{k}{s}") for k in
                  ["ld", "wg", "sq", "pe", "ve", "se", "vch", "sch", "out"]})

    def vedge(c):
        ch = c["vch"]
        ch.n += 1
        b.on("ve", lambda e, s=ch.sem: e.tensor_copy(
            out=VNOP[:, 0:1], in_=VNOP[:, 1:2]).then_inc(s, 1))
        b.on("ve", lambda e, s=ch.sem, v=ch.n: e.wait_ge(s, v))

    b.on("gp", lambda g: g.iota(IOF[:], pattern=[[1, P]], channel_multiplier=0,
                                allow_small_or_imprecise_dtypes=True))
    b.inc("gp", g_cmp, 1, lambda g: g.iota(
        IOP[:], pattern=[[0, 1]], channel_multiplier=1,
        allow_small_or_imprecise_dtypes=True))
    b.wait("ve", g_cmp, 1)
    b.on("ve", lambda v: v.tensor_scalar(out=IDF[:], in0=IOF[:],
                                         scalar1=IOP[:, 0:1], scalar2=None,
                                         op0=ALU.is_equal))
    b.on("ve", lambda v: v.tensor_copy(out=IDFB[:], in_=IDF[:]))
    b.on("ve", lambda v: v.memset(ONESB[:], 1.0))
    b.inc("ve", g_cmp, 1, lambda v: v.memset(ONESR[:], 1.0))
    for v in ["rms1w", "alog", "cout", "dskip", "rms2w",
              "ln1w", "ln1b", "ln2w", "ln2b", "bo"]:
        b.inc("sy", g_ld, 16, lambda s, d=VEC[v], src=dt[v]:
              s.dma_start(out=d[:], in_=src))
    b.inc("sy", g_ld, 16, lambda s: s.dma_start(out=BQKV[:], in_=dt["bqkv"]))
    b.inc("sy", g_ld, 16, lambda s: s.dma_start(out=B1S[:], in_=dt["b1s"]))
    b.inc("sy", g_ld, 16, lambda s: s.dma_start(out=WGF[:], in_=wg_c))
    b.wait("se", g_ld, 16 * 2)
    b.inc("se", g_cmp, 1, lambda s: s.activation(out=AVT[:], in_=VEC["alog"][:],
                                                 func=AF.Exp))
    b.on("se", lambda s, v=g_cmp.n: s.wait_ge(g_cmp.sem, v))
    b.inc("se", g_cmp, 1, lambda s: s.activation(out=AV[:], in_=AVT[:],
                                                 func=AF.Exp, scale=-1.0))


    def stats(c, src, tt, mean, eps, cast, sq, tmp, src_stride=T):
        t0 = tt * TT
        b.wait("se", c["ve"])
        for ch in range(NCH):
            so = ch * src_stride + t0
            b.on("se", lambda s, so=so, i=ch: s.activation(
                out=sq[:, i * TT:(i + 1) * TT], in_=src[:, so:so + TT],
                func=AF.Square))
        b.inc("se", c["se"], 1, lambda s: s.copy(out=SNOP[:, 0:1], in_=SNOP[:, 1:2]))
        if mean:
            for ch in range(NCH):
                so = ch * src_stride + t0
                b.on("ve", lambda v, so=so, i=ch: v.tensor_copy(
                    out=cast[:, i * TT:(i + 1) * TT], in_=src[:, so:so + TT]))
            b.inc("ve", c["ve"], 1, lambda v: v.tensor_copy(out=VNOP[:, 0:1], in_=VNOP[:, 1:2]))
        b.wait("te", c["se"])
        b.wait("te", c["ve"])
        for ch in range(NCH):
            last = ch == NCH - 1
            if last and not mean:
                b.inc("te", c["pe"], 1, lambda t, i=ch: t.matmul(
                    psB[:, TT:2 * TT], ONESB[:], sq[:, i * TT:(i + 1) * TT],
                    start=(i == 0), stop=True))
            else:
                b.on("te", lambda t, i=ch: t.matmul(
                    psB[:, TT:2 * TT], ONESB[:], sq[:, i * TT:(i + 1) * TT],
                    start=(i == 0), stop=(i == NCH - 1)))
        if mean:
            for ch in range(NCH):
                if ch == NCH - 1:
                    b.inc("te", c["pe"], 1, lambda t, i=ch: t.matmul(
                        psB[:, 0:TT], ONESB[:], cast[:, i * TT:(i + 1) * TT],
                        start=(i == 0), stop=True))
                else:
                    b.on("te", lambda t, i=ch: t.matmul(
                        psB[:, 0:TT], ONESB[:], cast[:, i * TT:(i + 1) * TT],
                        start=(i == 0), stop=False))
        b.wait("ve", c["pe"])
        inv = 1.0 / D
        if mean:
            b.on("ve", lambda v: v.tensor_scalar(
                out=MEANT[:], in0=psB[:, 0:TT], scalar1=inv, scalar2=None,
                op0=ALU.mult))
            b.on("ve", lambda v: v.tensor_scalar(
                out=RSTDT[:], in0=psB[:, TT:2 * TT], scalar1=inv, scalar2=None,
                op0=ALU.mult))
            b.on("ve", lambda v: v.tensor_mul(out=tmp[:, 0:TT], in0=MEANT[:],
                                              in1=MEANT[:]))
            b.on("ve", lambda v: v.scalar_tensor_tensor(
                out=RSTDT[:], in0=RSTDT[:], scalar=eps, in1=tmp[:, 0:TT],
                op0=ALU.add, op1=ALU.subtract))
        else:
            b.on("ve", lambda v: v.tensor_scalar(
                out=RSTDT[:], in0=psB[:, TT:2 * TT], scalar1=inv, scalar2=eps,
                op0=ALU.mult, op1=ALU.add))
        b.inc("ve", c["ve"], 1, lambda v: v.tensor_copy(out=VNOP[:, 0:1], in_=VNOP[:, 1:2]))
        b.wait("se", c["ve"])
        b.inc("se", c["se"], 1, lambda s: s.activation(out=RSTDT[:],
                                                       in_=RSTDT[:],
                                                       func=AF.Sqrt))
        b.wait("ve", c["se"])
        b.inc("ve", c["ve"], 1, lambda v: v.reciprocal(out=RSTDT[:],
                                                       in_=RSTDT[:]))

    def dump_and_finish(s, c, src):
        b.wait("sy", c["ve"])
        b.wait("sy", c["se"])
        for ch in range(NCH):
            b.inc("sy", c["out"], 16, lambda y, i=ch, d=src: y.dma_start(
                out=ys_c[s, i], in_=d[:, i * T:(i + 1) * T]))

    if stage == 45:
        stage5_qt1 = True
    def walk_seq(s, c):
        if s == 0:
            b.wait("sy", g_cmp, 2)
        else:
            for eng in ("sy", "gp", "te", "ve", "se"):
                for k in ("pe", "ve", "se", "out", "ld", "wg", "sq"):
                    b.wait(eng, C[s - 1][k])
        for ch in range(NCH):
            b.inc("sy", c["ld"], 16, lambda y, i=ch: y.dma_start(
                out=RX[:, i * T:(i + 1) * T], in_=xs_c[s, i]))
        x_loaded = c["ld"].n

        for eng in ("ve", "se", "te"):
            b.wait(eng, c["ld"], x_loaded)
            b.wait(eng, g_ld)
            b.wait(eng, g_cmp)
        for tt in range(NTT):
            stats(c, RX, tt, True, 1e-5, CASTA, SQA, DTMP)
            for ch in range(NCH):
                o = ch * T + tt * TT
                b.on("ve", lambda v, o=o: v.tensor_sub(
                    out=BF[:, o:o + TT], in0=RX[:, o:o + TT], in1=MEANT[:]))
                b.on("ve", lambda v, o=o: v.tensor_mul(
                    out=BF[:, o:o + TT], in0=BF[:, o:o + TT], in1=RSTDT[:]))
        for tt in range(NTT):
            stats(c, BF, tt, False, 1e-6, CASTA, SQA, DTMP)
            for ch in range(NCH):
                o = ch * T + tt * TT
                b.on("ve", lambda v, o=o, i=ch: v.scalar_tensor_tensor(
                    out=RX[:, o:o + TT], in0=BF[:, o:o + TT],
                    scalar=VEC["rms1w"][:, i:i + 1], in1=RSTDT[:],
                    op0=ALU.mult, op1=ALU.mult))
        for ch in range(NCH):
            o = ch * T
            b.on("ve", lambda v, o=o, i=ch: v.tensor_tensor_scan(
                out=DTMP[:, 0:T], data0=AV[:, i:i + 1].broadcast_to([P, T]),
                data1=RX[:, o:o + T], initial=0.0, op0=ALU.mult, op1=ALU.add))
            b.on("ve", lambda v, o=o, i=ch: v.scalar_tensor_tensor(
                out=BF[:, o:o + T], in0=DTMP[:, 0:T],
                scalar=VEC["cout"][:, i:i + 1], in1=BF[:, o:o + T],
                op0=ALU.mult, op1=ALU.add))
            b.on("ve", lambda v, o=o, i=ch: v.scalar_tensor_tensor(
                out=BF[:, o:o + T], in0=RX[:, o:o + T],
                scalar=VEC["dskip"][:, i:i + 1], in1=BF[:, o:o + T],
                op0=ALU.mult, op1=ALU.add))
        b.inc("ve", c["ve"], 1, lambda v: v.tensor_copy(out=VNOP[:, 0:1], in_=VNOP[:, 1:2]))
        scan_done = c["ve"].n

        if stage == 1:
            dump_and_finish(s, c, BF)
            return

        b.wait("sy", c["ve"], scan_done)
        for ch in range(NCH):
            b.inc("sy", c["ld"], 16, lambda y, i=ch: y.dma_start(
                out=RX[:, i * T:(i + 1) * T], in_=xs_c[s, i]))
        x_reloaded = c["ld"].n

        for tt in range(NTT):
            stats(c, BF, tt, False, 1e-6, CASTA, SQA, DTMP)
            for ch in range(NCH):
                o = ch * T + tt * TT
                b.on("ve", lambda v, o=o, i=ch: v.scalar_tensor_tensor(
                    out=CB1[:, o:o + TT], in0=BF[:, o:o + TT],
                    scalar=VEC["rms2w"][:, i:i + 1], in1=RSTDT[:],
                    op0=ALU.mult, op1=ALU.mult))
        b.inc("ve", c["ve"], 1, lambda v: v.tensor_copy(out=VNOP[:, 0:1], in_=VNOP[:, 1:2]))
        n2_done = c["ve"].n
        b.wait("te", c["ve"], n2_done)

        rwfc = Ring(b, 4, c["wg"], c["pe"])
        rwpr = Ring(b, 2, c["wg"], c["pe"])
        sched = []
        for tt in range(NTT):
            sched += [("fc", tt, hc) for hc in range(NHS)]
            sched += [("pr", tt, oc) for oc in range(NCH)]
        wfc_idx = {}
        wpr_idx = {}

        def emit_load(item):
            kind, tt, i = item
            if kind == "fc":
                li = rwfc.load("gp", lambda g, i=i, sl=len(rwfc.load_vals) % 4:
                               g.dma_start(out=WFC[:, sl],
                                           in_=wfc_c[:, :, i * P:(i + 1) * P]))
                wfc_idx[(tt, i)] = li
            else:
                li = rwpr.load("gp", lambda g, i=i, sl=len(rwpr.load_vals) % 2:
                               g.dma_start(out=WPR[:, sl],
                                           in_=wpr_c[:, :, i * P:(i + 1) * P]))
                wpr_idx[(tt, i)] = li
        LA = 2
        for k in range(LA):
            emit_load(sched[k])
        load_pos = LA
        ps_evac = {}
        psb_evac = {}
        for tt in range(NTT):
            for hc in range(NHS):
                li = wfc_idx[(tt, hc)]
                rwfc.use(li)
                slot = li % 4
                if (hc % 4) in ps_evac:
                    b.wait("te", c["se"], ps_evac[hc % 4])
                for ch in range(NCH):
                    if ch == NCH - 1:
                        b.inc("te", c["pe"], 1, lambda t, sl=slot, i=ch,
                              tt=tt, hc=hc: t.matmul(
                                  psA_s(hc), WFC[:, sl, i, :],
                                  CB1[:, i * T + tt * TT:i * T + tt * TT + TT],
                                  start=False, stop=True))
                    else:
                        b.on("te", lambda t, sl=slot, i=ch, tt=tt, hc=hc:
                             t.matmul(
                                 psA_s(hc), WFC[:, sl, i, :],
                                 CB1[:, i * T + tt * TT:i * T + tt * TT + TT],
                                 start=(i == 0), stop=False))
                rwfc.consumed(li)
                if load_pos < len(sched):
                    emit_load(sched[load_pos])
                    load_pos += 1
                b.wait("se", c["pe"])
                b.inc("se", c["se"], 1, lambda s_, hc=hc: s_.activation(
                    out=HID[:, hc, :], in_=psA_s(hc), func=AF.Silu))
                ps_evac[hc % 4] = c["se"].n
            b.wait("te", c["se"])
            for oc in range(NCH):
                li = wpr_idx[(tt, oc)]
                rwpr.use(li)
                slot = li % 2
                if (oc % 2) in psb_evac:
                    b.wait("te", c["ve"], psb_evac[oc % 2])
                for hc in range(NHS):
                    if hc == NHS - 1:
                        b.inc("te", c["pe"], 1, lambda t, sl=slot, h=hc, oc=oc:
                              t.matmul(psB_s(oc), WPR[:, sl, h, :],
                                       HID[:, h, :], start=False, stop=True))
                    else:
                        b.on("te", lambda t, sl=slot, h=hc, oc=oc: t.matmul(
                            psB_s(oc), WPR[:, sl, h, :], HID[:, h, :],
                            start=(h == 0), stop=False))
                rwpr.consumed(li)
                if load_pos < len(sched):
                    emit_load(sched[load_pos])
                    load_pos += 1
                b.wait("ve", c["pe"])
                b.inc("ve", c["ve"], 1, lambda v, oc=oc, tt=tt: v.tensor_add(
                    out=BF[:, oc * T + tt * TT:oc * T + tt * TT + TT],
                    in0=psB_s(oc),
                    in1=BF[:, oc * T + tt * TT:oc * T + tt * TT + TT]))
                psb_evac[oc % 2] = c["ve"].n
        if stage == 2:
            dump_and_finish(s, c, BF)
            return

        b.wait("ve", c["ld"], x_reloaded)
        for tt in range(NTT):
            stats(c, BF, tt, True, 1e-5, CASTA, SQA, DTMP)
            for ch in range(NCH):
                o = ch * T + tt * TT
                b.on("ve", lambda v, o=o: v.tensor_sub(
                    out=DTMP[:, 0:TT], in0=BF[:, o:o + TT], in1=MEANT[:]))
                b.on("ve", lambda v, o=o: v.tensor_mul(
                    out=DTMP[:, 0:TT], in0=DTMP[:, 0:TT], in1=RSTDT[:]))
                b.on("ve", lambda v, o=o: v.tensor_add(
                    out=RX[:, o:o + TT], in0=RX[:, o:o + TT],
                    in1=DTMP[:, 0:TT]))
        b.inc("ve", c["ve"], 1, lambda v: v.tensor_copy(out=VNOP[:, 0:1], in_=VNOP[:, 1:2]))
        xnew_done = c["ve"].n
        b.wait("sy", c["ve"], xnew_done)
        for ch in range(NCH):
            b.inc("sy", c["out"], 16, lambda y, i=ch: y.dma_start(
                out=xnew_c[s, i], in_=RX[:, i * T:(i + 1) * T]))
        if stage == 3:
            dump_and_finish(s, c, RX)
            return

        for tt in range(NTT):
            stats(c, RX, tt, True, 1e-5, CASTA, SQA, DTMP)
            for ch in range(NCH):
                o = ch * T + tt * TT
                b.on("ve", lambda v, o=o: v.tensor_sub(
                    out=BF[:, o:o + TT], in0=RX[:, o:o + TT], in1=MEANT[:]))
                b.on("ve", lambda v, o=o: v.tensor_mul(
                    out=BF[:, o:o + TT], in0=BF[:, o:o + TT], in1=RSTDT[:]))
        for tt in range(NTT):
            stats(c, BF, tt, True, 1e-5, CASTA, SQA, DTMP)
            for ch in range(NCH):
                o = ch * T + tt * TT
                b.on("ve", lambda v, o=o: v.tensor_sub(
                    out=DTMP[:, 0:TT], in0=BF[:, o:o + TT], in1=MEANT[:]))
                b.on("ve", lambda v, o=o: v.tensor_mul(
                    out=DTMP[:, 0:TT], in0=DTMP[:, 0:TT], in1=RSTDT[:]))
                b.on("ve", lambda v, o=o, i=ch: v.tensor_scalar(
                    out=CB1[:, o:o + TT], in0=DTMP[:, 0:TT],
                    scalar1=VEC["ln1w"][:, i:i + 1],
                    scalar2=VEC["ln1b"][:, i:i + 1],
                    op0=ALU.mult, op1=ALU.add))
        b.inc("ve", c["ve"], 1, lambda v: v.tensor_copy(out=VNOP[:, 0:1], in_=VNOP[:, 1:2]))
        ln1_done = c["ve"].n
        if stage == 4:
            dump_and_finish(s, c, BF)
            return

        b.wait("te", c["ve"], ln1_done)
        rqo = Ring(b, 2, c["wg"], c["pe"])
        qkv_idx = [rqo.load("gp", lambda g, i=ot, sl=ot % 2: g.dma_start(
            out=WQO[:, sl], in_=wqkv_c[:, :, i * P:(i + 1) * P]))
            for ot in range(24)]
        ps_free = {}
        last_evac_ve = 0
        last_evac_se = 0
        vtr = 0
        for ot in range(24):
            li = qkv_idx[ot]
            rqo.use(li)
            slot = li % 2
            b.wait("te", c["ve"], last_evac_ve)
            if OVERSYNC:
                b.wait("te", c["ve"])
                b.wait("te", c["se"])
            for ch in range(NCH):
                for tt in range(NTT):
                    lastmm = (ch == NCH - 1 and tt == NTT - 1)
                    if lastmm:
                        b.inc("te", c["pe"], 1, lambda t, sl=slot, i=ch, tt=tt:
                              t.matmul(psA_s(tt), WQO[:, sl, i, :],
                                       CB1[:, i * T + tt * TT:
                                           i * T + tt * TT + TT],
                                       start=False, stop=True))
                    else:
                        b.on("te", lambda t, sl=slot, i=ch, tt=tt: t.matmul(
                            psA_s(tt), WQO[:, sl, i, :],
                            CB1[:, i * T + tt * TT:i * T + tt * TT + TT],
                            start=(i == 0), stop=(i == NCH - 1)))
            rqo.consumed(li)
            if stage == 44 and ot == 23:
                b.wait("gp", c["pe"])
                b.inc("gp", c["out"], 16, lambda g, sl=slot: g.dma_start(
                    out=ys_c[s, 3][:, 0:1024],
                    in_=WQO[:, sl].rearrange("p a b -> p (a b)")))
            b.wait("ve", c["pe"])
            bias = BQKV[:, ot:ot + 1]
            if ot < 8:
                dst = QB[:, ot, :]
            elif ot < 16:
                dst = KBU[:, ot - 8, :]
            else:
                dst = VFM[:]
            b.inc("ve", c["ve"], 1, lambda v, d=dst, bi=bias: v.tensor_scalar(
                out=d, in0=psA[:], scalar1=bi, scalar2=None, op0=ALU.add))
            last_evac_ve = c["ve"].n
            if ot >= 16 and ATT_PART >= 1:
                vc = ot - 16
                b.wait("te", c["ve"], last_evac_ve)
                tr_marks = {}
                for k in range(16):
                    sl = k % 8
                    if k >= 1:
                        b.wait("te", c["ve"], tr_marks[k - 1])
                    b.inc("te", c["pe"], 1, lambda t, k=k, sl=sl: t.transpose(
                        psTv[:, sl * P:(sl + 1) * P], VFM[:, k * P:(k + 1) * P],
                        IDFB[:]))
                    b.wait("ve", c["pe"])
                    if TRCP_ON:
                        b.inc("ve", c["ve"], 1, lambda v, k=k, sl=sl, vc=vc:
                              v.tensor_copy(
                                  out=VT[:, k, 2 * vc * DH:(2 * vc + 2) * DH],
                                  in_=psTv[:, sl * P:(sl + 1) * P]))
                    else:
                        b.inc("ve", c["ve"], 1, lambda v, sl=sl:
                              v.tensor_copy(
                                  out=VFM[:, 0:P],
                                  in_=psTv[:, sl * P:(sl + 1) * P]))
                    tr_marks[k] = c["ve"].n
                last_evac_ve = c["ve"].n

        if stage == 5 and ATT_PART <= 1:
            dump_and_finish(s, c, BF)
            return
        for which, buf, dstm in ((0, QB, CQM), (1, KBU, CKM)):
            for hp in range(NCH):
                b.wait("se", c["ve"])
                b.inc("se", c["se"], 1, lambda s_, bu=buf, hp=hp: s_.activation(
                    out=CSQ[:], in_=bu[:, hp, :], func=AF.Square))
                b.wait("te", c["se"])
                b.wait("te", c["ve"])
                for q4 in range(4):
                    b.inc("te", c["pe"], 1, lambda t, q4=q4: t.matmul(
                        psA_s(q4), ONESB[:], CSQ[:, q4 * TT:(q4 + 1) * TT],
                        start=True, stop=True))
                b.wait("ve", c["pe"])
                b.inc("ve", c["ve"], 1, lambda v, d=dstm, hp=hp: v.reduce_max(
                    out=d[:, hp:hp + 1], in_=psA[:], axis=AX.X))
        vedge(c)
        b.wait("se", c["ve"])
        b.inc("se", c["se"], 1, lambda s_: s_.activation(out=CQM[:], in_=CQM[:],
                                                         func=AF.Sqrt))
        b.on("se", lambda s_, v=c["se"].n: s_.wait_ge(c["se"].sem, v))
        b.inc("se", c["se"], 1, lambda s_: s_.activation(out=CKM[:], in_=CKM[:],
                                                         func=AF.Sqrt))
        b.wait("ve", c["se"])
        b.inc("ve", c["ve"], 1, lambda v: v.tensor_mul(out=CBD[:], in0=CQM[:],
                                                       in1=CKM[:]))
        vedge(c)
        b.inc("ve", c["ve"], 1, lambda v: v.tensor_scalar(
            out=CBD[:], in0=CBD[:], scalar1=-0.125, scalar2=None, op0=ALU.mult))
        vedge(c)
        cbd_done = c["ve"].n

        if stage == 5 and ATT_PART == 2:
            dump_and_finish(s, c, BF)
            return
        b.wait("te", c["ve"], cbd_done)
        b.wait("se", c["ve"], cbd_done)
        rwo = Ring(b, 2, c["wg"], c["pe"])
        n_qt = 1 if (stage in (44, 45) or ATT_PART < 4) else QT_N
        n_hp = HP_N if ATT_PART >= 4 else 1
        wo_idx = {}
        for qt in range(n_qt):
            for oc in range(NCH):
                wo_idx[(qt, oc)] = rwo.load(
                    "gp", lambda g, i=oc, sl=(qt * NCH + oc) % 2: g.dma_start(
                        out=WQO[:, sl], in_=wo_c[:, :, i * P:(i + 1) * P]))
        last_ctx_evac = 0
        exp_marks = {}
        ktg = 0
        for qt in range(n_qt):
            for hp in range(n_hp):
                b.wait("te", c["ve"], last_ctx_evac)

                def emit_ctx(kt, pt, hp=hp):
                    st = kt == 0
                    sp = kt == 15
                    b.on("te", lambda t, pt=pt, kt=kt, hp=hp, st=st, sp=sp:
                         t.matmul(
                             pb6[0:DH, :],
                             VT[:, kt, (2 * hp) * DH:(2 * hp + 1) * DH],
                             pt[:, 0:TT], start=st, stop=sp))
                    b.on("te", lambda t, pt=pt, kt=kt, hp=hp, st=st, sp=sp:
                         t.matmul(
                             pb7[DH:P, :],
                             VT[:, kt, (2 * hp + 1) * DH:(2 * hp + 2) * DH],
                             pt[:, TT:2 * TT], start=st, stop=sp,
                             tile_position=(0, DH)))
                    b.on("te", lambda t, pt=pt, st=st, sp=sp: t.matmul(
                        psB[0:DH, 0:TT], ONESB[:, 0:DH], pt[:, 0:TT],
                        start=st, stop=sp))
                    def smm2(t, pt=pt, st=st, sp=sp):
                        return t.matmul(
                            psB[DH:P, TT:2 * TT], ONESB[:, 0:DH],
                            pt[:, TT:2 * TT], start=st, stop=sp,
                            tile_position=(0, DH))
                    if sp:
                        b.inc("te", c["pe"], 1, smm2)
                    else:
                        b.on("te", smm2)

                prev = None
                for kt in range(16):
                    hq = ktg % 2
                    if ktg - 2 in exp_marks:
                        b.wait("te", c["se"], exp_marks[ktg - 2])
                    for half in range(2):
                        def smm(t, hq=hq, half=half, kt=kt, hp=hp, qt=qt):
                            return t.matmul(
                                psA_s(2 * hq + half),
                                KBU[half * DH:(half + 1) * DH, hp,
                                    kt * P:(kt + 1) * P],
                                QB[half * DH:(half + 1) * DH, hp,
                                   qt * TT:(qt + 1) * TT],
                                start=True, stop=True,
                                tile_position=(half * DH, 0))
                        if half == 1:
                            b.inc("te", c["pe"], 1, smm)
                        else:
                            b.on("te", smm)
                    b.wait("se", c["pe"])
                    pt = (PT0, PT1)[ktg % 2]
                    b.inc("se", c["se"], 1, lambda s_, pt=pt, hp=hp, hq=hq:
                          s_.activation(out=pt[:, 0:2 * TT],
                                        in_=psA[:, hq * 2 * TT:
                                                (hq + 1) * 2 * TT],
                                        func=AF.Exp, scale=0.125,
                                        bias=CBD[:, hp:hp + 1]))
                    exp_marks[ktg] = c["se"].n
                    if prev is not None:
                        pk, ppt, pm = prev
                        b.wait("te", c["se"], pm)
                        emit_ctx(pk, ppt)
                    prev = (kt, pt, exp_marks[ktg])
                    ktg += 1
                pk, ppt, pm = prev
                b.wait("te", c["se"], pm)
                emit_ctx(pk, ppt)
                b.wait("ve", c["pe"])
                if stage == 44 and qt == 0 and hp == 1:
                    b.on("ve", lambda v: v.tensor_copy(out=MEANT[:],
                                                       in_=psB[:, 0:TT]))
                    b.inc("ve", c["ve"], 1, lambda v: v.tensor_copy(
                        out=RSTDT[:], in_=psB[:, TT:2 * TT]))
                b.on("ve", lambda v: v.reciprocal(out=SR[0:DH, :],
                                                  in_=psB[0:DH, 0:TT]))
                b.on("ve", lambda v: v.reciprocal(out=SR[DH:P, :],
                                                  in_=psB[DH:P, TT:2 * TT]))
                b.on("ve", lambda v, hp=hp: v.tensor_mul(
                    out=CTXB[0:DH, hp * TT:(hp + 1) * TT], in0=pb6[0:DH, :],
                    in1=SR[0:DH, :]))
                b.inc("ve", c["ve"], 1, lambda v, hp=hp: v.tensor_mul(
                    out=CTXB[DH:P, hp * TT:(hp + 1) * TT], in0=pb7[DH:P, :],
                    in1=SR[DH:P, :]))
                last_ctx_evac = c["ve"].n
            b.wait("te", c["ve"], last_ctx_evac)
            opj_evac = {}
            for oc in range(NCH):
                li = wo_idx[(qt, oc)]
                rwo.use(li)
                slot = li % 2
                if (oc % 4) in opj_evac:
                    b.wait("te", c["ve"], opj_evac[oc % 4])
                for hp in range(NCH):
                    lastmm = hp == NCH - 1
                    def omm(t, sl=slot, hp=hp, oc=oc):
                        return t.matmul(psA_s(oc), WQO[:, sl, hp, :],
                                        CTXB[:, hp * TT:(hp + 1) * TT],
                                        start=(hp == 0), stop=(hp == NCH - 1))
                    if lastmm:
                        b.inc("te", c["pe"], 1, omm)
                    else:
                        b.on("te", omm)
                rwo.consumed(li)
                b.wait("ve", c["pe"])
                b.inc("ve", c["ve"], 1, lambda v, oc=oc, qt=qt: (
                    v.scalar_tensor_tensor(
                        out=BF[:, oc * T + qt * TT:oc * T + qt * TT + TT],
                        in0=psA_s(oc), scalar=VEC["bo"][:, oc:oc + 1],
                        in1=BF[:, oc * T + qt * TT:oc * T + qt * TT + TT],
                        op0=ALU.add, op1=ALU.add)))
                opj_evac[oc % 4] = c["ve"].n
            last_ctx_evac = c["ve"].n
            if stage == 44 and qt == 0:
                b.wait("gp", c["ve"])
                b.inc("gp", c["out"], 16, lambda g: g.dma_start(
                    out=ys_c[s, 0][:, 0:1024], in_=VT[:, 0, :]))
                b.inc("gp", c["out"], 16, lambda g: g.dma_start(
                    out=ys_c[s, 0][:, 1024:2048], in_=VT[:, 5, :]))

                b.inc("gp", c["out"], 16, lambda g: g.dma_start(
                    out=ys_c[s, 1][:, 0:TT], in_=MEANT[:]))
                b.inc("gp", c["out"], 16, lambda g: g.dma_start(
                    out=ys_c[s, 1][:, TT:2 * TT], in_=RSTDT[:]))
                b.inc("gp", c["out"], 16, lambda g: g.dma_start(
                    out=ys_c[s, 2][:, 0:NCH], in_=CBD[:]))
                return
            if stage == 45 and qt == 0:
                b.wait("gp", c["ve"])
                b.inc("gp", c["out"], 16, lambda g: g.dma_start(
                    out=ys_c[s, 0], in_=CTXB[:, 0:T]))
                b.inc("gp", c["out"], 16, lambda g: g.dma_start(
                    out=ys_c[s, 1], in_=CTXB[:, T:2 * T]))
                b.inc("gp", c["out"], 16, lambda g: g.dma_start(
                    out=ys_c[s, 2][:, 0:NCH], in_=CBD[:]))
                b.inc("gp", c["out"], 16, lambda g: g.dma_start(
                    out=ys_c[s, 2][:, 8:16], in_=CQM[:]))
                b.inc("gp", c["out"], 16, lambda g: g.dma_start(
                    out=ys_c[s, 2][:, 16:24], in_=CKM[:]))
                b.inc("gp", c["out"], 16, lambda g: g.dma_start(
                    out=ys_c[s, 2][:, 512:1024], in_=SR[:]))
                return
        if stage == 5:
            dump_and_finish(s, c, BF)
            return

        for tt in range(NTT):
            stats(c, BF, tt, True, 1e-5, CASTA, SQA, DTMP)
            for ch in range(NCH):
                o = ch * T + tt * TT
                b.on("ve", lambda v, o=o: v.tensor_sub(
                    out=DTMP[:, 0:TT], in0=BF[:, o:o + TT], in1=MEANT[:]))
                b.on("ve", lambda v, o=o: v.tensor_mul(
                    out=DTMP[:, 0:TT], in0=DTMP[:, 0:TT], in1=RSTDT[:]))
                b.on("ve", lambda v, o=o, i=ch: v.tensor_scalar(
                    out=CB1[:, o:o + TT], in0=DTMP[:, 0:TT],
                    scalar1=VEC["ln2w"][:, i:i + 1],
                    scalar2=VEC["ln2b"][:, i:i + 1],
                    op0=ALU.mult, op1=ALU.add))
                b.inc("ve", c["ve"], 1, lambda v, i=ch: v.tensor_scalar(
                    out=DTMP[:, TT:2 * TT], in0=DTMP[:, 0:TT],
                    scalar1=VEC["ln2w"][:, i:i + 1],
                    scalar2=VEC["ln2b"][:, i:i + 1],
                    op0=ALU.mult, op1=ALU.add))
                b.wait("te", c["ve"])
                for p4 in range(4):
                    lastmm = ch == NCH - 1 and p4 == 3
                    def gmm(t, ch=ch, tt=tt, p4=p4):
                        sl = psA[:, p4 * TT:p4 * TT + NE]
                        return t.matmul(sl,
                                        DTMP[:, TT + p4 * P:TT + (p4 + 1) * P],
                                        WGF[:, ch, :],
                                        start=(ch == 0), stop=(ch == NCH - 1))
                    if lastmm:
                        b.inc("te", c["pe"], 1, gmm)
                    else:
                        b.on("te", gmm)
            b.wait("ve", c["pe"])
            for p4 in range(4):
                pt = tt * 4 + p4
                last = p4 == 3
                def zev(v, pt=pt, p4=p4):
                    return v.tensor_copy(
                        out=ZB[:, pt * NE:(pt + 1) * NE],
                        in_=psA[:, p4 * TT:p4 * TT + NE])
                if last:
                    b.inc("ve", c["ve"], 1, zev)
                else:
                    b.on("ve", zev)
        b.inc("ve", c["ve"], 1, lambda v: v.tensor_copy(out=VNOP[:, 0:1], in_=VNOP[:, 1:2]))
        ln2_done = c["ve"].n
        b.wait("te", c["ve"], ln2_done)
        zb3 = ZB[:].rearrange("p (a b) -> p a b", b=NE)
        ge3 = GE[:].rearrange("p (a b) -> p a b", b=NE)
        gp3 = GP_[:].rearrange("p (a b) -> p a b", b=NE)
        gmk3 = GMK[:].rearrange("p (a b) -> p a b", b=NE)
        gind3 = GIND[:].rearrange("p (a b) -> p a b", b=NE)
        wet3 = WETM[:].rearrange("p (a b) -> p a b", b=NE)
        b.on("ve", lambda v: v.reduce_max(out=GM1[:, :, None], in_=zb3,
                                          axis=AX.X, negate=True))
        vedge(c)
        b.on("ve", lambda v: v.tensor_add(
            out=ge3, in0=zb3, in1=GM1[:, :, None].broadcast_to([P, 16, NE])))
        b.inc("ve", c["ve"], 1, lambda v: v.tensor_copy(out=VNOP[:, 0:1], in_=VNOP[:, 1:2]))
        b.wait("se", c["ve"])
        b.inc("se", c["se"], 1, lambda s_: s_.activation(out=GE[:], in_=GE[:],
                                                         func=AF.Exp))
        b.wait("ve", c["se"])
        b.on("ve", lambda v: v.reduce_sum(out=GS[:, :, None], in_=ge3,
                                          axis=AX.X))
        vedge(c)
        b.on("ve", lambda v: v.reciprocal(out=GS[:], in_=GS[:]))
        vedge(c)
        b.on("ve", lambda v: v.tensor_mul(
            out=gp3, in0=ge3, in1=GS[:, :, None].broadcast_to([P, 16, NE])))
        b.on("ve", lambda v: v.reduce_max(out=GM1[:, :, None], in_=gp3,
                                          axis=AX.X))
        vedge(c)
        b.on("ve", lambda v: v.tensor_tensor(
            out=gmk3, in0=gp3, in1=GM1[:, :, None].broadcast_to([P, 16, NE]),
            op=ALU.is_equal))
        b.on("ve", lambda v: v.scalar_tensor_tensor(
            out=GMK[:], in0=GMK[:], scalar=-1e30, in1=GP_[:], op0=ALU.mult,
            op1=ALU.add))
        b.on("ve", lambda v: v.reduce_max(out=GM2[:, :, None], in_=gmk3,
                                          axis=AX.X))
        vedge(c)
        b.on("ve", lambda v: v.tensor_tensor(
            out=gind3, in0=gp3, in1=GM2[:, :, None].broadcast_to([P, 16, NE]),
            op=ALU.is_ge))
        b.on("ve", lambda v: v.tensor_add(out=GM1[:], in0=GM1[:], in1=GM2[:]))
        vedge(c)
        b.on("ve", lambda v: v.reciprocal(out=GM1[:], in_=GM1[:]))
        vedge(c)
        b.on("ve", lambda v: v.tensor_mul(out=gp3, in0=gp3, in1=gind3))
        b.inc("ve", c["ve"], 1, lambda v: v.tensor_mul(
            out=wet3, in0=gp3, in1=GM1[:, :, None].broadcast_to([P, 16, NE])))
        we_done = c["ve"].n
        b.wait("te", c["ve"], we_done)
        tr_evac = {}
        for pt in range(16):
            if pt - 1 in tr_evac:
                b.wait("te", c["ve"], tr_evac[pt - 1])
            b.inc("te", c["pe"], 1, lambda t, pt=pt: t.transpose(
                pb7[0:NE, (pt % 4) * P:(pt % 4 + 1) * P],
                WETM[:, pt * NE:(pt + 1) * NE], IDF[:]))
            b.wait("ve", c["pe"])
            b.inc("ve", c["ve"], 1, lambda v, pt=pt: v.tensor_copy(
                out=WEFM[:, pt * P:(pt + 1) * P],
                in_=pb7[0:NE, (pt % 4) * P:(pt % 4 + 1) * P]))
            tr_evac[pt] = c["ve"].n
        if stage >= 6 and MOE_PART == 1:
            dump_and_finish(s, c, BF)
            return

        if stage >= 6 and MOE_PART == 0:
            dump_and_finish(s, c, BF)
            return
        b.wait("sy", c["ve"])
        b.inc("sy", c["ld"], 16, lambda y: y.dma_start(out=B2F[:],
                                                       in_=dt["b2l"]))
        b2f_ld = c["ld"].n
        b.wait("te", c["ld"], b2f_ld)
        first_core_pass = (s == 0)
        for ttg in range(NTT):
            t0 = ttg * TT
            b.wait("te", c["ve"])
            acc_evac = {}
            for oc in range(NCH):
                if (oc % 4) in acc_evac:
                    b.wait("te", c["ve"], acc_evac[oc % 4])
                b.inc("te", c["pe"], 1, lambda t, oc=oc, t0=t0: t.matmul(
                    psA_s(oc), B2F[:, oc * P:(oc + 1) * P],
                    WEFM[:, t0:t0 + TT], start=True, stop=True))
                b.wait("ve", c["pe"])
                b.inc("ve", c["ve"], 1, lambda v, oc=oc, t0=t0: v.tensor_add(
                    out=BF[:, oc * T + t0:oc * T + t0 + TT], in0=psA_s(oc),
                    in1=BF[:, oc * T + t0:oc * T + t0 + TT]))
                acc_evac[oc % 4] = c["ve"].n
            if stage >= 6 and MOE_PART == 2 and ttg == 1:
                break
            first_pass = first_core_pass and ttg == 0

            def mk_loader(ring, ldq, wb_vals, dram_first, dram_scr, scr_back):
                def do_load(e, he):
                    li = len(ring.load_vals)
                    if first_pass:
                        if li >= ring.nslots:
                            b.on(ldq, lambda g, j=li - ring.nslots:
                                 g.wait_ge(g_wb.sem, wb_vals[j]))
                        ring.load(ldq, lambda g, e=e, he=he, sl=li % 2:
                                  g.dma_start(out=dram_first[0][:, sl],
                                              in_=dram_first[1](e, he)))
                        b.wait("sy", c["wg"], ring.load_vals[li])
                        b.inc("sy", g_wb, 16, lambda y, e=e, he=he, sl=li % 2:
                              y.dma_start(out=scr_back(e, he),
                                          in_=dram_first[0][:, sl]))
                        wb_vals[li] = g_wb.n
                    else:
                        ring.load(ldq, lambda s_, e=e, he=he, sl=li % 2:
                                  s_.dma_start(out=dram_first[0][:, sl],
                                               in_=dram_scr(e, he)))
                    return li
                return do_load

            ldq = "gp" if first_pass else "se"
            ldc = c["wg"] if first_pass else c["sq"]
            if not first_pass and s == 0:
                b.wait("se", g_wb)
            r1 = Ring(b, 2, ldc, c["pe"])
            r2 = Ring(b, 2, ldc, c["pe"])
            wb1, wb2 = {}, {}
            load_w1 = mk_loader(
                r1, ldq, wb1,
                (W1E, lambda e, he: w1_c[e][:, :, he * TT:(he + 1) * TT]),
                lambda e, he: w1b_c[e][:, :, he * TT:(he + 1) * TT],
                lambda e, he: w1b_c[e][:, :, he * TT:(he + 1) * TT])
            load_w2 = mk_loader(
                r2, ldq, wb2,
                (W2E, lambda e, he: w2_c[e][:, he * 4:(he + 1) * 4, :]),
                lambda e, he: w2b_c[e][:, he * 4:(he + 1) * 4, :],
                lambda e, he: w2b_c[e][:, he * 4:(he + 1) * 4, :])

            for e in range(NE if MOE_PART >= 9 else (MOE_PART - 2 if MOE_PART >= 3 else 0)):
                b.wait("gp", c["pe"])
                b.inc("gp", c["wg"], 16, lambda g, e=e, t0=t0: g.dma_start(
                    out=FTMP[0:1, 0:TT], in_=WEFM[e:e + 1, t0:t0 + TT]))
                werow_ld = c["wg"].n
                l1 = [load_w1(e, 0), load_w1(e, 1)]
                b.wait("te", c["ve"])
                gelu_mark = {}
                for he in range(NE):
                    li = l1[he]
                    r1.use(li)
                    slot = li % 2
                    for hh in range(4):
                        hcg = he * 4 + hh
                        if (hcg % 4) in gelu_mark:
                            b.wait("te", c["se"], gelu_mark[hcg % 4])
                        for ch in range(NCH):
                            lastmm = ch == NCH - 1
                            def amm(t, sl=slot, ch=ch, hh=hh, hcg=hcg, t0=t0):
                                return t.matmul(
                                    psA_s(hcg),
                                    W1E[:, sl, ch, hh * P:(hh + 1) * P],
                                    CB1[:, ch * T + t0:ch * T + t0 + TT],
                                    start=(ch == 0), stop=(ch == NCH - 1))
                            if lastmm:
                                b.inc("te", c["pe"], 1, amm)
                            else:
                                b.on("te", amm)
                        b.wait("se", c["pe"])
                        b.inc("se", c["se"], 1, lambda s_, hcg=hcg, e=e:
                              s_.activation(
                                  out=GBUF[:, hcg, :], in_=psA_s(hcg),
                                  func=AF.Gelu_apprx_tanh,
                                  bias=B1S[:, e * NHM + hcg:e * NHM + hcg + 1]))
                        gelu_mark[hcg % 4] = c["se"].n
                    r1.consumed(li)
                    if he + 2 < NE:
                        l1.append(load_w1(e, he + 2))
                b.wait("te", c["wg"], werow_ld)
                b.wait("te", c["ve"])
                b.inc("te", c["pe"], 1, lambda t: t.matmul(
                    pb6[:], ONESR[:], FTMP[0:1, 0:TT], start=True, stop=True))
                b.wait("ve", c["pe"])
                b.inc("ve", c["ve"], 1, lambda v: v.tensor_copy(out=WEB[:],
                                                                in_=pb6[:]))
                web_done = c["ve"].n
                b.wait("te", c["se"])
                b.wait("te", c["ve"])
                l2 = [load_w2(e, 0), load_w2(e, 1)]
                for he in range(NE):
                    li = l2[he]
                    r2.use(li)
                    slot = li % 2
                    for oc in range(NCH):
                        for hh in range(4):
                            hcg = he * 4 + hh
                            lastmm = (oc == NCH - 1 and hh == 3)
                            def bmm(t, sl=slot, oc=oc, hh=hh, hcg=hcg, he=he):
                                return t.matmul(
                                    moe_ps(oc),
                                    W2E[:, sl, hh, oc * P:(oc + 1) * P],
                                    GBUF[:, hcg, :],
                                    start=(he == 0 and hh == 0),
                                    stop=(he == NE - 1 and hh == 3))
                            if lastmm:
                                b.inc("te", c["pe"], 1, bmm)
                            else:
                                b.on("te", bmm)
                    r2.consumed(li)
                    if he + 2 < NE:
                        l2.append(load_w2(e, he + 2))
                b.wait("ve", c["pe"])
                b.wait("ve", c["ve"], web_done)
                for oc in range(NCH):
                    b.on("ve", lambda v, oc=oc: v.tensor_mul(
                        out=MTMP[:], in0=moe_ps(oc), in1=WEB[:]))
                    b.on("ve", lambda v, oc=oc, t0=t0: v.tensor_add(
                        out=BF[:, oc * T + t0:oc * T + t0 + TT], in0=MTMP[:],
                        in1=BF[:, oc * T + t0:oc * T + t0 + TT]))
                b.inc("ve", c["ve"], 1, lambda v: v.tensor_copy(out=VNOP[:, 0:1], in_=VNOP[:, 1:2]))
            if stage == 6 or MOE_PART < 9:
                continue
            stats(c, BF, ttg, True, 1e-5, MCAST, MSQ, FTMP)
            b.wait("sy", c["pe"])
            xnt_ld = []
            for ch in range(NCH):
                b.inc("sy", c["ld"], 16, lambda y, i=ch, t0=t0: y.dma_start(
                    out=XNT8[:, i, :], in_=xnew_c[s, i][:, t0:t0 + TT]))
                xnt_ld.append(c["ld"].n)
            for ch in range(NCH):
                o = ch * T + t0
                b.on("ve", lambda v, o=o: v.tensor_sub(
                    out=FTMP[:], in0=BF[:, o:o + TT], in1=MEANT[:]))
                b.on("ve", lambda v, o=o: v.tensor_mul(
                    out=FTMP[:], in0=FTMP[:], in1=RSTDT[:]))
                b.wait("ve", c["ld"], xnt_ld[ch])
                b.inc("ve", c["ve"], 1, lambda v, o=o, i=ch: v.tensor_add(
                    out=BF[:, o:o + TT], in0=XNT8[:, i, :], in1=FTMP[:]))
            b.wait("sy", c["ve"])
            for ch in range(NCH):
                b.inc("sy", c["out"], 16, lambda y, i=ch, t0=t0: y.dma_start(
                    out=ys_c[s, i][:, t0:t0 + TT],
                    in_=BF[:, i * T + t0:i * T + t0 + TT]))
        if stage == 6 or (stage >= 6 and MOE_PART < 9):
            dump_and_finish(s, c, BF)
            return

    for s in range(NSEQ):
        walk_seq(s, C[s])

    for s in range(NSEQ):
        b.wait("sy", C[s]["out"])

    blk = b.stack.enter_context(nc.Block())
    b.flush(blk)
    b.stack.close()
    return nc


IDFB = None


def _prep_inputs(inputs):
    x = inputs["L0_attn_x"]
    v128 = lambda v: np.ascontiguousarray(
        np.asarray(v, dtype=np.float32).reshape(NCH, P).T)
    common = {
        "wfc": np.ascontiguousarray(np.asarray(inputs["L0_attn_ff_fc_w"]).T),
        "wpr": np.ascontiguousarray(np.asarray(inputs["L0_attn_ff_proj_w"]).T),
        "wqkv": np.ascontiguousarray(inputs["L0_ffn_w_qkv"]),
        "wo": np.ascontiguousarray(inputs["L0_ffn_w_o"]),
        "wg": np.ascontiguousarray(inputs["L0_ffn_w_gate"]),
        "w1": np.ascontiguousarray(inputs["L0_ffn_w1"]),
        "w2": np.ascontiguousarray(inputs["L0_ffn_w2"]),
        "rms1w": v128(inputs["L0_attn_input_norm_w"]),
        "alog": v128(inputs["L0_attn_a_log"]),
        "cout": v128(inputs["L0_attn_c_out"]),
        "dskip": v128(inputs["L0_attn_d_skip"]),
        "rms2w": v128(inputs["L0_attn_ff_norm_w"]),
        "ln1w": v128(inputs["L0_ffn_ln1_w"]),
        "ln1b": v128(inputs["L0_ffn_ln1_b"]),
        "ln2w": v128(inputs["L0_ffn_ln2_w"]),
        "ln2b": v128(inputs["L0_ffn_ln2_b"]),
        "bo": v128(inputs["L0_ffn_b_o"]),
        "bqkv": np.ascontiguousarray(
            np.asarray(inputs["L0_ffn_b_qkv"], dtype=np.float32)
            .reshape(24, P).T),
        "b1s": np.ascontiguousarray(
            np.asarray(inputs["L0_ffn_b1"], dtype=np.float32)
            .reshape(NE, NHM, P).transpose(2, 0, 1).reshape(P, NE * NHM)),
        "b2l": np.ascontiguousarray(inputs["L0_ffn_b2"]),
    }
    common = {k: np.asarray(v, dtype=np.float32) for k, v in common.items()}
    in_maps = []
    for core in range(8):
        xs = np.ascontiguousarray(
            np.asarray(x[4 * core:4 * core + 4]).transpose(0, 2, 1),
            dtype=np.float32)
        m = {"xs": xs}
        m.update(common)
        in_maps.append(m)
    return in_maps


def kernel(**inputs):
    in_maps = _prep_inputs(inputs)
    nc = build_program(STAGE_FULL)
    res = run_bass_kernel_spmd(nc, in_maps, core_ids=list(range(8)))
    out = np.empty((32, T, D), dtype=np.float32)
    for core in range(8):
        ys = res.results[core]["ys"]
        for i in range(NSEQ):
            out[4 * core + i] = ys[i].T
    return out


# version 05
# speedup vs baseline: 1.0772x; 1.0037x over previous
import contextlib
import numpy as np
import concourse.bass as bass
import concourse.mybir as mybir
from concourse.bass_utils import run_bass_kernel_spmd

F32 = mybir.dt.float32
BF16 = mybir.dt.bfloat16
AF = mybir.ActivationFunctionType
ALU = mybir.AluOpType
AX = mybir.AxisListType

P = 128
NSEQ = 4
T = 2048
D = 1024
NCH = 8
HS = 2048
NHS = 16
NH = 16
DH = 64
HM = 4096
NHM = 32
NE = 8
NTT = 4
TT = 512
KB = 1024

STAGE_FULL = 7
ATT_PART = 4
TRCP_ON = True
QT_N = 4
HP_N = 8
OVERSYNC = False
MOE_PART = 9


class Ctr:
    __slots__ = ("sem", "n")

    def __init__(self, sem):
        self.sem = sem
        self.n = 0


class Bld:
    def __init__(self, nc):
        self.nc = nc
        self.q = {"sy": [], "gp": [], "te": [], "ve": [], "se": []}
        self.stack = contextlib.ExitStack()

    def on(self, eng, fn):
        self.q[eng].append(fn)

    def ctr(self, name):
        return Ctr(self.stack.enter_context(self.nc.semaphore(name)))

    def wait(self, eng, c, v=None):
        v = c.n if v is None else v
        if v > 0:
            self.on(eng, lambda e, s=c.sem, vv=v: e.wait_ge(s, vv))

    def inc(self, eng, c, k, fn):
        c.n += k
        self.on(eng, lambda e, f=fn, s=c.sem, kk=k: f(e).then_inc(s, kk))

    def make_arena(self, total_bytes):
        self.arena = self.stack.enter_context(
            self.nc.sbuf_tensor("arena", [P, total_bytes], mybir.dt.uint8))
        self.arena_base = self.nc.lookup_mloc(self.arena).addr

    def at(self, name, off_bytes, shape, dtype):
        nc = self.nc
        h = nc.virtual_sbuf_tensor(name, list(shape), dtype)
        nc._tensor(h.name, list(shape), dtype, type="SB")
        mloc = nc.lookup_mloc(h)
        mloc.allocated = not nc.target_bir_lowering
        mloc.addr = self.arena_base + off_bytes
        return h

    def flush(self, blk):
        for name, dec in (("sy", blk.sync), ("gp", blk.gpsimd),
                          ("te", blk.tensor), ("ve", blk.vector),
                          ("se", blk.scalar)):
            fns = self.q[name]
            self.q[name] = []
            if fns:
                def runner(engine, fns=fns):
                    for f in fns:
                        f(engine)
                dec(runner)


class Ring:

    def __init__(self, b, nslots, load_ctr, cons_ctr):
        self.b = b
        self.nslots = nslots
        self.load_ctr = load_ctr
        self.cons_ctr = cons_ctr
        self.load_vals = []
        self.cons_vals = {}
        self.floor = cons_ctr.n

    def load(self, eng, fn):
        i = len(self.load_vals)
        if i >= self.nslots:
            j = i - self.nslots
            self.b.on(eng, lambda e, r=self, j=j: e.wait_ge(
                r.cons_ctr.sem, r.cons_vals[j]))
        elif self.floor > 0:
            self.b.wait(eng, self.cons_ctr, self.floor)
        self.b.inc(eng, self.load_ctr, 16, fn)
        self.load_vals.append(self.load_ctr.n)
        return i

    def use(self, i, eng="te"):
        self.b.wait(eng, self.load_ctr, self.load_vals[i])

    def consumed(self, i):
        self.cons_vals[i] = self.cons_ctr.n


def build_program(stage=STAGE_FULL):
    nc = bass.Bass("TRN2", target_bir_lowering=False)
    b = Bld(nc)

    dt = {}
    dt["xs"] = nc.dram_tensor("xs", [NSEQ, D, T], F32, kind="ExternalInput").ap()
    for n_, sh in [("wfc", [D, HS]), ("wpr", [HS, D]), ("wqkv", [D, 3 * D]),
                   ("wo", [D, D]), ("wg", [D, NE]), ("w1", [NE, D, HM]),
                   ("w2", [NE, HM, D]), ("bqkv", [P, 24]),
                   ("b1s", [P, NE * NHM]), ("b2l", [NE, D])]:
        dt[n_] = nc.dram_tensor(n_, sh, F32, kind="ExternalInput").ap()
    for v in ["rms1w", "alog", "cout", "dskip", "rms2w",
              "ln1w", "ln1b", "ln2w", "ln2b", "bo"]:
        dt[v] = nc.dram_tensor(v, [P, NCH], F32, kind="ExternalInput").ap()
    dt["ys"] = nc.dram_tensor("ys", [NSEQ, D, T], F32, kind="ExternalOutput").ap()
    dt["xnew_s"] = nc.dram_tensor("xnew_s", [NSEQ, D, T], F32, kind="Internal").ap()
    dt["w1b"] = nc.dram_tensor("w1b", [NE, D, HM], BF16, kind="Internal").ap()
    dt["w2b"] = nc.dram_tensor("w2b", [NE, HM, D], BF16, kind="Internal").ap()

    xs_c = dt["xs"].rearrange("s (c p) t -> s c p t", p=P)
    ys_c = dt["ys"].rearrange("s (c p) t -> s c p t", p=P)
    xnew_c = dt["xnew_s"].rearrange("s (c p) t -> s c p t", p=P)
    wfc_c = dt["wfc"].rearrange("(c p) h -> p c h", p=P)
    wpr_c = dt["wpr"].rearrange("(c p) o -> p c o", p=P)
    wqkv_c = dt["wqkv"].rearrange("(c p) o -> p c o", p=P)
    wo_c = dt["wo"].rearrange("(c p) o -> p c o", p=P)
    wg_c = dt["wg"].rearrange("(c p) e -> p c e", p=P)
    w1_c = dt["w1"].rearrange("e (c p) h -> e p c h", p=P)
    w2_c = dt["w2"].rearrange("e (c p) o -> e p c o", p=P)
    w1b_c = dt["w1b"].rearrange("e (c p) h -> e p c h", p=P)
    w2b_c = dt["w2b"].rearrange("e (c p) o -> e p c o", p=P)

    b.make_arena(207 * KB + 768)
    A_ = b.at
    RX = A_("RX", 0, [P, NCH * T], F32)
    QB = A_("QB", 0, [P, NCH, T], BF16)
    KBU = A_("KBU", 32 * KB, [P, NCH, T], BF16)
    GBUF = A_("GBUF", 0, [P, NHM, TT], BF16)
    MCAST = A_("MCAST", 32 * KB, [P, NCH * TT], BF16)
    MSQ = A_("MSQ", 40 * KB, [P, NCH * TT], BF16)
    XNT8 = A_("XNT8", 32 * KB, [P, NCH, TT], F32)
    B2F = A_("B2F", 60 * KB, [NE, D], F32)
    WEFM = A_("WEFM", 48 * KB, [NE, T], F32)
    ZB = A_("ZB", 56 * KB, [P, P], F32)
    GE = A_("GE", 56 * KB + 512, [P, P], F32)
    GP_ = A_("GP", 57 * KB, [P, P], F32)
    GMK = A_("GMK", 57 * KB + 512, [P, P], F32)
    GIND = A_("GIND", 58 * KB, [P, P], F32)
    GM1 = A_("GM1", 58 * KB + 512, [P, 16], F32)
    GM2 = A_("GM2", 58 * KB + 576, [P, 16], F32)
    GS = A_("GS", 58 * KB + 640, [P, 16], F32)
    WETM = A_("WETM", 59 * KB, [P, P], F32)
    BF = A_("BF", 64 * KB, [P, NCH * T], F32)
    CB1 = A_("CB1", 128 * KB, [P, NCH * T], BF16)
    PT0 = A_("PT0", 128 * KB, [P, 4 * TT], BF16)
    PT1 = A_("PT1", 132 * KB, [P, 4 * TT], BF16)
    CTXB = A_("CTXB", 136 * KB, [P, NCH * TT], BF16)
    SR = A_("SR", 144 * KB, [P, TT], F32)
    CSQ = A_("CSQ", 146 * KB, [P, T], BF16)
    CQM = A_("CQM", 150 * KB, [P, NCH], F32)
    CKM = A_("CKM", 150 * KB + 64, [P, NCH], F32)
    CBD = A_("CBD", 150 * KB + 128, [P, NCH], F32)
    VFM = A_("VFM", 198 * KB, [P, T], BF16)
    CASTA = A_("CASTA", 160 * KB, [P, NCH * TT], BF16)
    SQA = A_("SQA", 168 * KB, [P, NCH * TT], BF16)
    HID = A_("HID", 160 * KB, [P, NHS, TT], BF16)
    WFC = A_("WFC", 176 * KB, [P, 4, NCH, P], BF16)
    WPR = A_("WPR", 184 * KB, [P, 2, NHS, P], BF16)
    VT = A_("VT", 160 * KB, [P, 16, NH * DH], BF16)
    WQO = A_("WQO", 192 * KB + 2 * KB, [P, 2, NCH, P], BF16)
    W1E = A_("W1E", 160 * KB, [P, 2, NCH, TT], BF16)
    W2E = A_("W2E", 176 * KB, [P, 2, 4, D], BF16)
    WEB = A_("WEB", 192 * KB, [P, TT], F32)
    MTMP = A_("MTMP", 194 * KB, [P, TT], F32)
    MEANT = A_("MEANT", 198 * KB, [P, TT], F32)
    RSTDT = A_("RSTDT", 200 * KB, [P, TT], F32)
    FTMP = A_("FTMP", 202 * KB, [P, TT], F32)
    DTMP = A_("DTMP", 192 * KB, [P, T], F32)
    ONESB = A_("ONESB", 204 * KB, [P, P], BF16)
    IDFB = A_("IDFB", 204 * KB + 256, [P, P], BF16)
    IDF = A_("IDF", 204 * KB + 512, [P, P], F32)
    ONESR = A_("ONESR", 205 * KB, [1, P], F32)
    VEC = {}
    off = 205 * KB + 512
    for v in ["rms1w", "alog", "cout", "dskip", "rms2w",
              "ln1w", "ln1b", "ln2w", "ln2b", "bo"]:
        VEC[v] = A_("sb_" + v, off, [P, NCH], F32)
        off += 32
    BQKV = A_("BQKV", off, [P, 24], F32); off += 96
    AV = A_("AV", off, [P, NCH], F32); off += 32
    AVT = A_("AVT", off, [P, NCH], F32); off += 32
    B1S = A_("B1S", off, [P, NE * NHM], F32); off += 1024
    WGB = A_("WGB", off, [P, NCH, NE], BF16); off += 128
    WGF = A_("WGF", off, [P, NCH, NE], F32); off += 256
    VNOP = A_("VNOP", off, [1, 8], F32); off += 32
    SNOP = A_("SNOP", off, [1, 8], F32); off += 32
    IOF = A_("IOF", 0, [P, P], F32)
    IOP = A_("IOP", 512, [P, 1], F32)
    assert off <= 207 * KB + 768, off

    psA = b.stack.enter_context(nc.psum_tensor("psA", [P, 4 * TT], F32))
    psB = b.stack.enter_context(nc.psum_tensor("psB", [P, 2 * TT], F32))
    pb6 = b.stack.enter_context(nc.psum_tensor("pb6", [P, TT], F32))
    pb7 = b.stack.enter_context(nc.psum_tensor("pb7", [P, TT], F32))
    psTv = pb6[:].bitcast(BF16)

    def psA_s(i):
        return psA[:, (i % 4) * TT:((i % 4) + 1) * TT]

    def psB_s(i):
        return psB[:, (i % 2) * TT:((i % 2) + 1) * TT]

    def moe_ps(oc):
        if oc < 4:
            return psA_s(oc)
        if oc < 6:
            return psB_s(oc - 4)
        return pb6[:] if oc == 6 else pb7[:]

    g_ld = b.ctr("g_ld")
    g_cmp = b.ctr("g_cmp")
    g_cmps = b.ctr("g_cmps")
    g_cmpg = b.ctr("g_cmpg")
    g_wb = b.ctr("g_wb")
    C = []
    for s in range(NSEQ):
        C.append({k: b.ctr(f"{k}{s}") for k in
                  ["ld", "wg", "sq", "pe", "ve", "se", "vch", "sch", "out"]})

    def vedge(c):
        ch = c["vch"]
        ch.n += 1
        b.on("ve", lambda e, s=ch.sem: e.tensor_copy(
            out=VNOP[:, 0:1], in_=VNOP[:, 1:2]).then_inc(s, 1))
        b.on("ve", lambda e, s=ch.sem, v=ch.n: e.wait_ge(s, v))

    b.on("gp", lambda g: g.iota(IOF[:], pattern=[[1, P]], channel_multiplier=0,
                                allow_small_or_imprecise_dtypes=True))
    b.inc("gp", g_cmpg, 1, lambda g: g.iota(
        IOP[:], pattern=[[0, 1]], channel_multiplier=1,
        allow_small_or_imprecise_dtypes=True))
    b.wait("ve", g_cmpg, 1)
    b.on("ve", lambda v: v.tensor_scalar(out=IDF[:], in0=IOF[:],
                                         scalar1=IOP[:, 0:1], scalar2=None,
                                         op0=ALU.is_equal))
    b.on("ve", lambda v: v.tensor_copy(out=IDFB[:], in_=IDF[:]))
    b.on("ve", lambda v: v.memset(ONESB[:], 1.0))
    b.inc("ve", g_cmp, 1, lambda v: v.memset(ONESR[:], 1.0))
    for v in ["rms1w", "alog", "cout", "dskip", "rms2w",
              "ln1w", "ln1b", "ln2w", "ln2b", "bo"]:
        b.inc("sy", g_ld, 16, lambda s, d=VEC[v], src=dt[v]:
              s.dma_start(out=d[:], in_=src))
    b.inc("sy", g_ld, 16, lambda s: s.dma_start(out=BQKV[:], in_=dt["bqkv"]))
    b.inc("sy", g_ld, 16, lambda s: s.dma_start(out=B1S[:], in_=dt["b1s"]))
    b.inc("sy", g_ld, 16, lambda s: s.dma_start(out=WGF[:], in_=wg_c))
    b.wait("se", g_ld, 16 * 2)
    b.inc("se", g_cmps, 1, lambda s: s.activation(out=AVT[:],
                                                  in_=VEC["alog"][:],
                                                  func=AF.Exp))
    b.on("se", lambda s, v=g_cmps.n: s.wait_ge(g_cmps.sem, v))
    b.inc("se", g_cmps, 1, lambda s: s.activation(out=AV[:], in_=AVT[:],
                                                  func=AF.Exp, scale=-1.0))


    def stats(c, src, tt, mean, eps, cast, sq, tmp, src_stride=T):
        t0 = tt * TT
        b.wait("se", c["ve"])
        for ch in range(NCH):
            so = ch * src_stride + t0
            b.on("se", lambda s, so=so, i=ch: s.activation(
                out=sq[:, i * TT:(i + 1) * TT], in_=src[:, so:so + TT],
                func=AF.Square))
        b.inc("se", c["se"], 1, lambda s: s.copy(out=SNOP[:, 0:1], in_=SNOP[:, 1:2]))
        if mean:
            for ch in range(NCH):
                so = ch * src_stride + t0
                b.on("ve", lambda v, so=so, i=ch: v.tensor_copy(
                    out=cast[:, i * TT:(i + 1) * TT], in_=src[:, so:so + TT]))
            b.inc("ve", c["ve"], 1, lambda v: v.tensor_copy(out=VNOP[:, 0:1], in_=VNOP[:, 1:2]))
        b.wait("te", c["se"])
        b.wait("te", c["ve"])
        for ch in range(NCH):
            last = ch == NCH - 1
            if last and not mean:
                b.inc("te", c["pe"], 1, lambda t, i=ch: t.matmul(
                    psB[:, TT:2 * TT], ONESB[:], sq[:, i * TT:(i + 1) * TT],
                    start=(i == 0), stop=True))
            else:
                b.on("te", lambda t, i=ch: t.matmul(
                    psB[:, TT:2 * TT], ONESB[:], sq[:, i * TT:(i + 1) * TT],
                    start=(i == 0), stop=(i == NCH - 1)))
        if mean:
            for ch in range(NCH):
                if ch == NCH - 1:
                    b.inc("te", c["pe"], 1, lambda t, i=ch: t.matmul(
                        psB[:, 0:TT], ONESB[:], cast[:, i * TT:(i + 1) * TT],
                        start=(i == 0), stop=True))
                else:
                    b.on("te", lambda t, i=ch: t.matmul(
                        psB[:, 0:TT], ONESB[:], cast[:, i * TT:(i + 1) * TT],
                        start=(i == 0), stop=False))
        b.wait("ve", c["pe"])
        inv = 1.0 / D
        if mean:
            b.on("ve", lambda v: v.tensor_scalar(
                out=MEANT[:], in0=psB[:, 0:TT], scalar1=inv, scalar2=None,
                op0=ALU.mult))
            b.on("ve", lambda v: v.tensor_scalar(
                out=RSTDT[:], in0=psB[:, TT:2 * TT], scalar1=inv, scalar2=None,
                op0=ALU.mult))
            b.on("ve", lambda v: v.tensor_mul(out=tmp[:, 0:TT], in0=MEANT[:],
                                              in1=MEANT[:]))
            b.on("ve", lambda v: v.scalar_tensor_tensor(
                out=RSTDT[:], in0=RSTDT[:], scalar=eps, in1=tmp[:, 0:TT],
                op0=ALU.add, op1=ALU.subtract))
        else:
            b.on("ve", lambda v: v.tensor_scalar(
                out=RSTDT[:], in0=psB[:, TT:2 * TT], scalar1=inv, scalar2=eps,
                op0=ALU.mult, op1=ALU.add))
        b.inc("ve", c["ve"], 1, lambda v: v.tensor_copy(out=VNOP[:, 0:1], in_=VNOP[:, 1:2]))
        b.wait("se", c["ve"])
        b.inc("se", c["se"], 1, lambda s: s.activation(out=RSTDT[:],
                                                       in_=RSTDT[:],
                                                       func=AF.Sqrt))
        b.wait("ve", c["se"])
        b.inc("ve", c["ve"], 1, lambda v: v.reciprocal(out=RSTDT[:],
                                                       in_=RSTDT[:]))

    def dump_and_finish(s, c, src):
        b.wait("sy", c["ve"])
        b.wait("sy", c["se"])
        for ch in range(NCH):
            b.inc("sy", c["out"], 16, lambda y, i=ch, d=src: y.dma_start(
                out=ys_c[s, i], in_=d[:, i * T:(i + 1) * T]))

    if stage == 45:
        stage5_qt1 = True
    def walk_seq(s, c):
        if s == 0:
            b.wait("sy", g_cmp, 1)
        else:
            for eng in ("sy", "gp", "te", "ve", "se"):
                for k in ("pe", "ve", "se", "out", "ld", "wg", "sq"):
                    b.wait(eng, C[s - 1][k])
        for ch in range(NCH):
            b.inc("sy", c["ld"], 16, lambda y, i=ch: y.dma_start(
                out=RX[:, i * T:(i + 1) * T], in_=xs_c[s, i]))
        x_loaded = c["ld"].n

        for eng in ("ve", "se", "te"):
            b.wait(eng, c["ld"], x_loaded)
            b.wait(eng, g_ld)
            b.wait(eng, g_cmp)
            b.wait(eng, g_cmps)
            b.wait(eng, g_cmpg)
        for tt in range(NTT):
            stats(c, RX, tt, True, 1e-5, CASTA, SQA, DTMP)
            for ch in range(NCH):
                o = ch * T + tt * TT
                b.on("ve", lambda v, o=o: v.tensor_sub(
                    out=BF[:, o:o + TT], in0=RX[:, o:o + TT], in1=MEANT[:]))
                b.on("ve", lambda v, o=o: v.tensor_mul(
                    out=BF[:, o:o + TT], in0=BF[:, o:o + TT], in1=RSTDT[:]))
        for tt in range(NTT):
            stats(c, BF, tt, False, 1e-6, CASTA, SQA, DTMP)
            for ch in range(NCH):
                o = ch * T + tt * TT
                b.on("ve", lambda v, o=o, i=ch: v.scalar_tensor_tensor(
                    out=RX[:, o:o + TT], in0=BF[:, o:o + TT],
                    scalar=VEC["rms1w"][:, i:i + 1], in1=RSTDT[:],
                    op0=ALU.mult, op1=ALU.mult))
        for ch in range(NCH):
            o = ch * T
            b.on("ve", lambda v, o=o, i=ch: v.tensor_tensor_scan(
                out=DTMP[:, 0:T], data0=AV[:, i:i + 1].broadcast_to([P, T]),
                data1=RX[:, o:o + T], initial=0.0, op0=ALU.mult, op1=ALU.add))
            b.on("ve", lambda v, o=o, i=ch: v.scalar_tensor_tensor(
                out=BF[:, o:o + T], in0=DTMP[:, 0:T],
                scalar=VEC["cout"][:, i:i + 1], in1=BF[:, o:o + T],
                op0=ALU.mult, op1=ALU.add))
            b.on("ve", lambda v, o=o, i=ch: v.scalar_tensor_tensor(
                out=BF[:, o:o + T], in0=RX[:, o:o + T],
                scalar=VEC["dskip"][:, i:i + 1], in1=BF[:, o:o + T],
                op0=ALU.mult, op1=ALU.add))
        b.inc("ve", c["ve"], 1, lambda v: v.tensor_copy(out=VNOP[:, 0:1], in_=VNOP[:, 1:2]))
        scan_done = c["ve"].n

        if stage == 1:
            dump_and_finish(s, c, BF)
            return

        b.wait("sy", c["ve"], scan_done)
        for ch in range(NCH):
            b.inc("sy", c["ld"], 16, lambda y, i=ch: y.dma_start(
                out=RX[:, i * T:(i + 1) * T], in_=xs_c[s, i]))
        x_reloaded = c["ld"].n

        for tt in range(NTT):
            stats(c, BF, tt, False, 1e-6, CASTA, SQA, DTMP)
            for ch in range(NCH):
                o = ch * T + tt * TT
                b.on("ve", lambda v, o=o, i=ch: v.scalar_tensor_tensor(
                    out=CB1[:, o:o + TT], in0=BF[:, o:o + TT],
                    scalar=VEC["rms2w"][:, i:i + 1], in1=RSTDT[:],
                    op0=ALU.mult, op1=ALU.mult))
        b.inc("ve", c["ve"], 1, lambda v: v.tensor_copy(out=VNOP[:, 0:1], in_=VNOP[:, 1:2]))
        n2_done = c["ve"].n
        b.wait("te", c["ve"], n2_done)

        rwfc = Ring(b, 4, c["wg"], c["pe"])
        rwpr = Ring(b, 2, c["wg"], c["pe"])
        sched = []
        for tt in range(NTT):
            sched += [("fc", tt, hc) for hc in range(NHS)]
            sched += [("pr", tt, oc) for oc in range(NCH)]
        wfc_idx = {}
        wpr_idx = {}

        def emit_load(item):
            kind, tt, i = item
            if kind == "fc":
                li = rwfc.load("gp", lambda g, i=i, sl=len(rwfc.load_vals) % 4:
                               g.dma_start(out=WFC[:, sl],
                                           in_=wfc_c[:, :, i * P:(i + 1) * P]))
                wfc_idx[(tt, i)] = li
            else:
                li = rwpr.load("gp", lambda g, i=i, sl=len(rwpr.load_vals) % 2:
                               g.dma_start(out=WPR[:, sl],
                                           in_=wpr_c[:, :, i * P:(i + 1) * P]))
                wpr_idx[(tt, i)] = li
        LA = 2
        for k in range(LA):
            emit_load(sched[k])
        load_pos = LA
        ps_evac = {}
        psb_evac = {}
        for tt in range(NTT):
            for hc in range(NHS):
                li = wfc_idx[(tt, hc)]
                rwfc.use(li)
                slot = li % 4
                if (hc % 4) in ps_evac:
                    b.wait("te", c["se"], ps_evac[hc % 4])
                for ch in range(NCH):
                    if ch == NCH - 1:
                        b.inc("te", c["pe"], 1, lambda t, sl=slot, i=ch,
                              tt=tt, hc=hc: t.matmul(
                                  psA_s(hc), WFC[:, sl, i, :],
                                  CB1[:, i * T + tt * TT:i * T + tt * TT + TT],
                                  start=False, stop=True))
                    else:
                        b.on("te", lambda t, sl=slot, i=ch, tt=tt, hc=hc:
                             t.matmul(
                                 psA_s(hc), WFC[:, sl, i, :],
                                 CB1[:, i * T + tt * TT:i * T + tt * TT + TT],
                                 start=(i == 0), stop=False))
                rwfc.consumed(li)
                if load_pos < len(sched):
                    emit_load(sched[load_pos])
                    load_pos += 1
                b.wait("se", c["pe"])
                b.inc("se", c["se"], 1, lambda s_, hc=hc: s_.activation(
                    out=HID[:, hc, :], in_=psA_s(hc), func=AF.Silu))
                ps_evac[hc % 4] = c["se"].n
            b.wait("te", c["se"])
            for oc in range(NCH):
                li = wpr_idx[(tt, oc)]
                rwpr.use(li)
                slot = li % 2
                if (oc % 2) in psb_evac:
                    b.wait("te", c["ve"], psb_evac[oc % 2])
                for hc in range(NHS):
                    if hc == NHS - 1:
                        b.inc("te", c["pe"], 1, lambda t, sl=slot, h=hc, oc=oc:
                              t.matmul(psB_s(oc), WPR[:, sl, h, :],
                                       HID[:, h, :], start=False, stop=True))
                    else:
                        b.on("te", lambda t, sl=slot, h=hc, oc=oc: t.matmul(
                            psB_s(oc), WPR[:, sl, h, :], HID[:, h, :],
                            start=(h == 0), stop=False))
                rwpr.consumed(li)
                if load_pos < len(sched):
                    emit_load(sched[load_pos])
                    load_pos += 1
                b.wait("ve", c["pe"])
                b.inc("ve", c["ve"], 1, lambda v, oc=oc, tt=tt: v.tensor_add(
                    out=BF[:, oc * T + tt * TT:oc * T + tt * TT + TT],
                    in0=psB_s(oc),
                    in1=BF[:, oc * T + tt * TT:oc * T + tt * TT + TT]))
                psb_evac[oc % 2] = c["ve"].n
        if stage == 2:
            dump_and_finish(s, c, BF)
            return

        b.wait("ve", c["ld"], x_reloaded)
        for tt in range(NTT):
            stats(c, BF, tt, True, 1e-5, CASTA, SQA, DTMP)
            for ch in range(NCH):
                o = ch * T + tt * TT
                b.on("ve", lambda v, o=o: v.tensor_sub(
                    out=DTMP[:, 0:TT], in0=BF[:, o:o + TT], in1=MEANT[:]))
                b.on("ve", lambda v, o=o: v.tensor_mul(
                    out=DTMP[:, 0:TT], in0=DTMP[:, 0:TT], in1=RSTDT[:]))
                b.on("ve", lambda v, o=o: v.tensor_add(
                    out=RX[:, o:o + TT], in0=RX[:, o:o + TT],
                    in1=DTMP[:, 0:TT]))
        b.inc("ve", c["ve"], 1, lambda v: v.tensor_copy(out=VNOP[:, 0:1], in_=VNOP[:, 1:2]))
        xnew_done = c["ve"].n
        b.wait("sy", c["ve"], xnew_done)
        for ch in range(NCH):
            b.inc("sy", c["out"], 16, lambda y, i=ch: y.dma_start(
                out=xnew_c[s, i], in_=RX[:, i * T:(i + 1) * T]))
        if stage == 3:
            dump_and_finish(s, c, RX)
            return

        for tt in range(NTT):
            stats(c, RX, tt, True, 1e-5, CASTA, SQA, DTMP)
            for ch in range(NCH):
                o = ch * T + tt * TT
                b.on("ve", lambda v, o=o: v.tensor_sub(
                    out=BF[:, o:o + TT], in0=RX[:, o:o + TT], in1=MEANT[:]))
                b.on("ve", lambda v, o=o: v.tensor_mul(
                    out=BF[:, o:o + TT], in0=BF[:, o:o + TT], in1=RSTDT[:]))
        for tt in range(NTT):
            stats(c, BF, tt, True, 1e-5, CASTA, SQA, DTMP)
            for ch in range(NCH):
                o = ch * T + tt * TT
                b.on("ve", lambda v, o=o: v.tensor_sub(
                    out=DTMP[:, 0:TT], in0=BF[:, o:o + TT], in1=MEANT[:]))
                b.on("ve", lambda v, o=o: v.tensor_mul(
                    out=DTMP[:, 0:TT], in0=DTMP[:, 0:TT], in1=RSTDT[:]))
                b.on("ve", lambda v, o=o, i=ch: v.tensor_scalar(
                    out=CB1[:, o:o + TT], in0=DTMP[:, 0:TT],
                    scalar1=VEC["ln1w"][:, i:i + 1],
                    scalar2=VEC["ln1b"][:, i:i + 1],
                    op0=ALU.mult, op1=ALU.add))
        b.inc("ve", c["ve"], 1, lambda v: v.tensor_copy(out=VNOP[:, 0:1], in_=VNOP[:, 1:2]))
        ln1_done = c["ve"].n
        if stage == 4:
            dump_and_finish(s, c, BF)
            return

        b.wait("te", c["ve"], ln1_done)
        b.wait("ve", c["out"])
        rqo = Ring(b, 2, c["wg"], c["pe"])
        qkv_idx = [rqo.load("gp", lambda g, i=ot, sl=ot % 2: g.dma_start(
            out=WQO[:, sl], in_=wqkv_c[:, :, i * P:(i + 1) * P]))
            for ot in range(24)]
        ps_free = {}
        last_evac_ve = 0
        last_evac_se = 0
        vtr = 0
        for ot in range(24):
            li = qkv_idx[ot]
            rqo.use(li)
            slot = li % 2
            b.wait("te", c["ve"], last_evac_ve)
            if OVERSYNC:
                b.wait("te", c["ve"])
                b.wait("te", c["se"])
            for ch in range(NCH):
                for tt in range(NTT):
                    lastmm = (ch == NCH - 1 and tt == NTT - 1)
                    if lastmm:
                        b.inc("te", c["pe"], 1, lambda t, sl=slot, i=ch, tt=tt:
                              t.matmul(psA_s(tt), WQO[:, sl, i, :],
                                       CB1[:, i * T + tt * TT:
                                           i * T + tt * TT + TT],
                                       start=False, stop=True))
                    else:
                        b.on("te", lambda t, sl=slot, i=ch, tt=tt: t.matmul(
                            psA_s(tt), WQO[:, sl, i, :],
                            CB1[:, i * T + tt * TT:i * T + tt * TT + TT],
                            start=(i == 0), stop=(i == NCH - 1)))
            rqo.consumed(li)
            if stage == 44 and ot == 23:
                b.wait("gp", c["pe"])
                b.inc("gp", c["out"], 16, lambda g, sl=slot: g.dma_start(
                    out=ys_c[s, 3][:, 0:1024],
                    in_=WQO[:, sl].rearrange("p a b -> p (a b)")))
            b.wait("ve", c["pe"])
            bias = BQKV[:, ot:ot + 1]
            if ot < 8:
                dst = QB[:, ot, :]
            elif ot < 16:
                dst = KBU[:, ot - 8, :]
            else:
                dst = VFM[:]
            b.inc("ve", c["ve"], 1, lambda v, d=dst, bi=bias: v.tensor_scalar(
                out=d, in0=psA[:], scalar1=bi, scalar2=None, op0=ALU.add))
            last_evac_ve = c["ve"].n
            if ot >= 16 and ATT_PART >= 1:
                vc = ot - 16
                b.wait("te", c["ve"], last_evac_ve)
                tr_marks = {}
                for k in range(16):
                    sl = k % 8
                    if k >= 1:
                        b.wait("te", c["ve"], tr_marks[k - 1])
                    b.inc("te", c["pe"], 1, lambda t, k=k, sl=sl: t.transpose(
                        psTv[:, sl * P:(sl + 1) * P], VFM[:, k * P:(k + 1) * P],
                        IDFB[:]))
                    b.wait("ve", c["pe"])
                    if TRCP_ON:
                        b.inc("ve", c["ve"], 1, lambda v, k=k, sl=sl, vc=vc:
                              v.tensor_copy(
                                  out=VT[:, k, 2 * vc * DH:(2 * vc + 2) * DH],
                                  in_=psTv[:, sl * P:(sl + 1) * P]))
                    else:
                        b.inc("ve", c["ve"], 1, lambda v, sl=sl:
                              v.tensor_copy(
                                  out=VFM[:, 0:P],
                                  in_=psTv[:, sl * P:(sl + 1) * P]))
                    tr_marks[k] = c["ve"].n
                last_evac_ve = c["ve"].n

        if stage == 5 and ATT_PART <= 1:
            dump_and_finish(s, c, BF)
            return
        for which, buf, dstm in ((0, QB, CQM), (1, KBU, CKM)):
            for hp in range(NCH):
                b.wait("se", c["ve"])
                b.inc("se", c["se"], 1, lambda s_, bu=buf, hp=hp: s_.activation(
                    out=CSQ[:], in_=bu[:, hp, :], func=AF.Square))
                b.wait("te", c["se"])
                b.wait("te", c["ve"])
                for q4 in range(4):
                    b.inc("te", c["pe"], 1, lambda t, q4=q4: t.matmul(
                        psA_s(q4), ONESB[:], CSQ[:, q4 * TT:(q4 + 1) * TT],
                        start=True, stop=True))
                b.wait("ve", c["pe"])
                b.inc("ve", c["ve"], 1, lambda v, d=dstm, hp=hp: v.reduce_max(
                    out=d[:, hp:hp + 1], in_=psA[:], axis=AX.X))
        vedge(c)
        b.wait("se", c["ve"])
        b.inc("se", c["se"], 1, lambda s_: s_.activation(out=CQM[:], in_=CQM[:],
                                                         func=AF.Sqrt))
        b.on("se", lambda s_, v=c["se"].n: s_.wait_ge(c["se"].sem, v))
        b.inc("se", c["se"], 1, lambda s_: s_.activation(out=CKM[:], in_=CKM[:],
                                                         func=AF.Sqrt))
        b.wait("ve", c["se"])
        b.inc("ve", c["ve"], 1, lambda v: v.tensor_mul(out=CBD[:], in0=CQM[:],
                                                       in1=CKM[:]))
        vedge(c)
        b.inc("ve", c["ve"], 1, lambda v: v.tensor_scalar(
            out=CBD[:], in0=CBD[:], scalar1=-0.125, scalar2=None, op0=ALU.mult))
        vedge(c)
        cbd_done = c["ve"].n

        if stage == 5 and ATT_PART == 2:
            dump_and_finish(s, c, BF)
            return
        b.wait("te", c["ve"], cbd_done)
        b.wait("se", c["ve"], cbd_done)
        rwo = Ring(b, 2, c["wg"], c["pe"])
        n_qt = 1 if (stage in (44, 45) or ATT_PART < 4) else QT_N
        n_hp = HP_N if ATT_PART >= 4 else 1
        wo_idx = {}
        for qt in range(n_qt):
            for oc in range(NCH):
                wo_idx[(qt, oc)] = rwo.load(
                    "gp", lambda g, i=oc, sl=(qt * NCH + oc) % 2: g.dma_start(
                        out=WQO[:, sl], in_=wo_c[:, :, i * P:(i + 1) * P]))
        last_ctx_evac = 0
        expA_prev = 0
        expB_prev = 0
        for qt in range(n_qt):
            for hpp in range(n_hp // 2):
                hpA, hpB = 2 * hpp, 2 * hpp + 1
                b.wait("te", c["ve"], last_ctx_evac)

                def s_pair(kt, hp, colbase, qt=qt):
                    for half in range(2):
                        def smm(t, half=half, kt=kt, hp=hp, cb=colbase,
                                qt=qt):
                            return t.matmul(
                                psA[:, cb + half * TT:cb + (half + 1) * TT],
                                KBU[half * DH:(half + 1) * DH, hp,
                                    kt * P:(kt + 1) * P],
                                QB[half * DH:(half + 1) * DH, hp,
                                   qt * TT:(qt + 1) * TT],
                                start=True, stop=True,
                                tile_position=(half * DH, 0))
                        if half == 1:
                            b.inc("te", c["pe"], 1, smm)
                        else:
                            b.on("te", smm)

                def ctx_s(kt, hp, pt, cpb, scol):
                    st = kt == 0
                    sp = kt == 15
                    b.on("te", lambda t, pt=pt, kt=kt, hp=hp, st=st, sp=sp,
                         cpb=cpb: t.matmul(
                             cpb[0:DH, :],
                             VT[:, kt, (2 * hp) * DH:(2 * hp + 1) * DH],
                             pt[:, 0:TT], start=st, stop=sp))
                    b.on("te", lambda t, pt=pt, kt=kt, hp=hp, st=st, sp=sp,
                         cpb=cpb: t.matmul(
                             cpb[DH:P, :],
                             VT[:, kt, (2 * hp + 1) * DH:(2 * hp + 2) * DH],
                             pt[:, TT:2 * TT], start=st, stop=sp,
                             tile_position=(0, DH), skip_group_check=True))
                    b.on("te", lambda t, pt=pt, st=st, sp=sp, sc=scol:
                         t.matmul(psB[0:DH, sc:sc + TT], ONESB[:, 0:DH],
                                  pt[:, 0:TT], start=st, stop=sp))
                    def smm2(t, pt=pt, st=st, sp=sp, sc=scol):
                        return t.matmul(
                            psB[DH:P, sc:sc + TT], ONESB[:, 0:DH],
                            pt[:, TT:2 * TT], start=st, stop=sp,
                            tile_position=(0, DH), skip_group_check=True)
                    if sp:
                        b.inc("te", c["pe"], 1, smm2)
                    else:
                        b.on("te", smm2)

                eA = {}
                eB = {}
                for kt in range(16):
                    ptA = (PT0[:, 0:2 * TT], PT0[:, 2 * TT:4 * TT])[kt % 2]
                    ptB = (PT1[:, 0:2 * TT], PT1[:, 2 * TT:4 * TT])[kt % 2]
                    b.wait("te", c["se"], eA.get(kt - 1, expA_prev))
                    s_pair(kt, hpA, 0)
                    b.wait("se", c["pe"])
                    b.inc("se", c["se"], 1, lambda s_, pt=ptA, hp=hpA:
                          s_.activation(out=pt, in_=psA[:, 0:2 * TT],
                                        func=AF.Exp, scale=0.125,
                                        bias=CBD[:, hp:hp + 1]))
                    eA[kt] = c["se"].n
                    b.wait("te", c["se"], eB.get(kt - 1, expB_prev))
                    s_pair(kt, hpB, 2 * TT)
                    b.wait("se", c["pe"])
                    b.inc("se", c["se"], 1, lambda s_, pt=ptB, hp=hpB:
                          s_.activation(out=pt, in_=psA[:, 2 * TT:4 * TT],
                                        func=AF.Exp, scale=0.125,
                                        bias=CBD[:, hp:hp + 1]))
                    eB[kt] = c["se"].n
                    if kt >= 1:
                        pa = (PT0[:, 0:2 * TT], PT0[:, 2 * TT:4 * TT])[(kt - 1) % 2]
                        pb_ = (PT1[:, 0:2 * TT], PT1[:, 2 * TT:4 * TT])[(kt - 1) % 2]
                        b.wait("te", c["se"], eA[kt - 1])
                        ctx_s(kt - 1, hpA, pa, pb6, 0)
                        b.wait("te", c["se"], eB[kt - 1])
                        ctx_s(kt - 1, hpB, pb_, pb7, TT)
                pa = (PT0[:, 0:2 * TT], PT0[:, 2 * TT:4 * TT])[15 % 2]
                pb_ = (PT1[:, 0:2 * TT], PT1[:, 2 * TT:4 * TT])[15 % 2]
                b.wait("te", c["se"], eA[15])
                ctx_s(15, hpA, pa, pb6, 0)
                b.wait("te", c["se"], eB[15])
                ctx_s(15, hpB, pb_, pb7, TT)
                expA_prev, expB_prev = eA[15], eB[15]
                b.wait("ve", c["pe"])
                for hp, cpb, sc in ((hpA, pb6, 0), (hpB, pb7, TT)):
                    b.on("ve", lambda v, sc=sc: v.reciprocal(
                        out=SR[0:DH, :], in_=psB[0:DH, sc:sc + TT]))
                    b.on("ve", lambda v, sc=sc: v.reciprocal(
                        out=SR[DH:P, :], in_=psB[DH:P, sc:sc + TT]))
                    b.on("ve", lambda v, hp=hp, cpb=cpb: v.tensor_mul(
                        out=CTXB[0:DH, hp * TT:(hp + 1) * TT],
                        in0=cpb[0:DH, :], in1=SR[0:DH, :]))
                    b.inc("ve", c["ve"], 1, lambda v, hp=hp, cpb=cpb:
                          v.tensor_mul(
                              out=CTXB[DH:P, hp * TT:(hp + 1) * TT],
                              in0=cpb[DH:P, :], in1=SR[DH:P, :]))
                last_ctx_evac = c["ve"].n
            b.wait("te", c["ve"], last_ctx_evac)
            opj_evac = {}
            for oc in range(NCH):
                li = wo_idx[(qt, oc)]
                rwo.use(li)
                slot = li % 2
                if (oc % 4) in opj_evac:
                    b.wait("te", c["ve"], opj_evac[oc % 4])
                for hp in range(NCH):
                    lastmm = hp == NCH - 1
                    def omm(t, sl=slot, hp=hp, oc=oc):
                        return t.matmul(psA_s(oc), WQO[:, sl, hp, :],
                                        CTXB[:, hp * TT:(hp + 1) * TT],
                                        start=(hp == 0), stop=(hp == NCH - 1))
                    if lastmm:
                        b.inc("te", c["pe"], 1, omm)
                    else:
                        b.on("te", omm)
                rwo.consumed(li)
                b.wait("ve", c["pe"])
                b.inc("ve", c["ve"], 1, lambda v, oc=oc, qt=qt: (
                    v.scalar_tensor_tensor(
                        out=BF[:, oc * T + qt * TT:oc * T + qt * TT + TT],
                        in0=psA_s(oc), scalar=VEC["bo"][:, oc:oc + 1],
                        in1=BF[:, oc * T + qt * TT:oc * T + qt * TT + TT],
                        op0=ALU.add, op1=ALU.add)))
                opj_evac[oc % 4] = c["ve"].n
            last_ctx_evac = c["ve"].n
            if stage == 44 and qt == 0:
                b.wait("gp", c["ve"])
                b.inc("gp", c["out"], 16, lambda g: g.dma_start(
                    out=ys_c[s, 0][:, 0:1024], in_=VT[:, 0, :]))
                b.inc("gp", c["out"], 16, lambda g: g.dma_start(
                    out=ys_c[s, 0][:, 1024:2048], in_=VT[:, 5, :]))

                b.inc("gp", c["out"], 16, lambda g: g.dma_start(
                    out=ys_c[s, 1][:, 0:TT], in_=MEANT[:]))
                b.inc("gp", c["out"], 16, lambda g: g.dma_start(
                    out=ys_c[s, 1][:, TT:2 * TT], in_=RSTDT[:]))
                b.inc("gp", c["out"], 16, lambda g: g.dma_start(
                    out=ys_c[s, 2][:, 0:NCH], in_=CBD[:]))
                return
            if stage == 45 and qt == 0:
                b.wait("gp", c["ve"])
                b.inc("gp", c["out"], 16, lambda g: g.dma_start(
                    out=ys_c[s, 0], in_=CTXB[:, 0:T]))
                b.inc("gp", c["out"], 16, lambda g: g.dma_start(
                    out=ys_c[s, 1], in_=CTXB[:, T:2 * T]))
                b.inc("gp", c["out"], 16, lambda g: g.dma_start(
                    out=ys_c[s, 2][:, 0:NCH], in_=CBD[:]))
                b.inc("gp", c["out"], 16, lambda g: g.dma_start(
                    out=ys_c[s, 2][:, 8:16], in_=CQM[:]))
                b.inc("gp", c["out"], 16, lambda g: g.dma_start(
                    out=ys_c[s, 2][:, 16:24], in_=CKM[:]))
                b.inc("gp", c["out"], 16, lambda g: g.dma_start(
                    out=ys_c[s, 2][:, 512:1024], in_=SR[:]))
                return
        if stage == 5:
            dump_and_finish(s, c, BF)
            return

        for tt in range(NTT):
            stats(c, BF, tt, True, 1e-5, CASTA, SQA, DTMP)
            for ch in range(NCH):
                o = ch * T + tt * TT
                b.on("ve", lambda v, o=o: v.tensor_sub(
                    out=DTMP[:, 0:TT], in0=BF[:, o:o + TT], in1=MEANT[:]))
                b.on("ve", lambda v, o=o: v.tensor_mul(
                    out=DTMP[:, 0:TT], in0=DTMP[:, 0:TT], in1=RSTDT[:]))
                b.on("ve", lambda v, o=o, i=ch: v.tensor_scalar(
                    out=CB1[:, o:o + TT], in0=DTMP[:, 0:TT],
                    scalar1=VEC["ln2w"][:, i:i + 1],
                    scalar2=VEC["ln2b"][:, i:i + 1],
                    op0=ALU.mult, op1=ALU.add))
                b.wait("ve", c["pe"])
                b.inc("ve", c["ve"], 1, lambda v, i=ch: v.tensor_scalar(
                    out=DTMP[:, TT:2 * TT], in0=DTMP[:, 0:TT],
                    scalar1=VEC["ln2w"][:, i:i + 1],
                    scalar2=VEC["ln2b"][:, i:i + 1],
                    op0=ALU.mult, op1=ALU.add))
                b.wait("te", c["ve"])
                for p4 in range(4):
                    def gmm(t, ch=ch, tt=tt, p4=p4):
                        sl = psA[:, p4 * TT:p4 * TT + NE]
                        return t.matmul(sl,
                                        DTMP[:, TT + p4 * P:TT + (p4 + 1) * P],
                                        WGF[:, ch, :],
                                        start=(ch == 0), stop=(ch == NCH - 1))
                    if p4 == 3:
                        b.inc("te", c["pe"], 1, gmm)
                    else:
                        b.on("te", gmm)
            b.wait("ve", c["pe"])
            for p4 in range(4):
                pt = tt * 4 + p4
                last = p4 == 3
                def zev(v, pt=pt, p4=p4):
                    return v.tensor_copy(
                        out=ZB[:, pt * NE:(pt + 1) * NE],
                        in_=psA[:, p4 * TT:p4 * TT + NE])
                if last:
                    b.inc("ve", c["ve"], 1, zev)
                else:
                    b.on("ve", zev)
        b.inc("ve", c["ve"], 1, lambda v: v.tensor_copy(out=VNOP[:, 0:1], in_=VNOP[:, 1:2]))
        ln2_done = c["ve"].n
        b.wait("te", c["ve"], ln2_done)
        zb3 = ZB[:].rearrange("p (a b) -> p a b", b=NE)
        ge3 = GE[:].rearrange("p (a b) -> p a b", b=NE)
        gp3 = GP_[:].rearrange("p (a b) -> p a b", b=NE)
        gmk3 = GMK[:].rearrange("p (a b) -> p a b", b=NE)
        gind3 = GIND[:].rearrange("p (a b) -> p a b", b=NE)
        wet3 = WETM[:].rearrange("p (a b) -> p a b", b=NE)
        b.on("ve", lambda v: v.reduce_max(out=GM1[:, :, None], in_=zb3,
                                          axis=AX.X, negate=True))
        vedge(c)
        b.on("ve", lambda v: v.tensor_add(
            out=ge3, in0=zb3, in1=GM1[:, :, None].broadcast_to([P, 16, NE])))
        b.inc("ve", c["ve"], 1, lambda v: v.tensor_copy(out=VNOP[:, 0:1], in_=VNOP[:, 1:2]))
        b.wait("se", c["ve"])
        b.inc("se", c["se"], 1, lambda s_: s_.activation(out=GE[:], in_=GE[:],
                                                         func=AF.Exp))
        b.wait("ve", c["se"])
        b.on("ve", lambda v: v.reduce_sum(out=GS[:, :, None], in_=ge3,
                                          axis=AX.X))
        vedge(c)
        b.on("ve", lambda v: v.reciprocal(out=GS[:], in_=GS[:]))
        vedge(c)
        b.on("ve", lambda v: v.tensor_mul(
            out=gp3, in0=ge3, in1=GS[:, :, None].broadcast_to([P, 16, NE])))
        b.on("ve", lambda v: v.reduce_max(out=GM1[:, :, None], in_=gp3,
                                          axis=AX.X))
        vedge(c)
        b.on("ve", lambda v: v.tensor_tensor(
            out=gmk3, in0=gp3, in1=GM1[:, :, None].broadcast_to([P, 16, NE]),
            op=ALU.is_equal))
        b.on("ve", lambda v: v.scalar_tensor_tensor(
            out=GMK[:], in0=GMK[:], scalar=-1e30, in1=GP_[:], op0=ALU.mult,
            op1=ALU.add))
        b.on("ve", lambda v: v.reduce_max(out=GM2[:, :, None], in_=gmk3,
                                          axis=AX.X))
        vedge(c)
        b.on("ve", lambda v: v.tensor_tensor(
            out=gind3, in0=gp3, in1=GM2[:, :, None].broadcast_to([P, 16, NE]),
            op=ALU.is_ge))
        b.on("ve", lambda v: v.tensor_add(out=GM1[:], in0=GM1[:], in1=GM2[:]))
        vedge(c)
        b.on("ve", lambda v: v.reciprocal(out=GM1[:], in_=GM1[:]))
        vedge(c)
        b.on("ve", lambda v: v.tensor_mul(out=gp3, in0=gp3, in1=gind3))
        b.inc("ve", c["ve"], 1, lambda v: v.tensor_mul(
            out=wet3, in0=gp3, in1=GM1[:, :, None].broadcast_to([P, 16, NE])))
        we_done = c["ve"].n
        b.wait("te", c["ve"], we_done)
        tr_evac = {}
        for pt in range(16):
            if pt - 1 in tr_evac:
                b.wait("te", c["ve"], tr_evac[pt - 1])
            b.inc("te", c["pe"], 1, lambda t, pt=pt: t.transpose(
                pb7[0:NE, (pt % 4) * P:(pt % 4 + 1) * P],
                WETM[:, pt * NE:(pt + 1) * NE], IDF[:]))
            b.wait("ve", c["pe"])
            b.inc("ve", c["ve"], 1, lambda v, pt=pt: v.tensor_copy(
                out=WEFM[:, pt * P:(pt + 1) * P],
                in_=pb7[0:NE, (pt % 4) * P:(pt % 4 + 1) * P]))
            tr_evac[pt] = c["ve"].n
        if stage >= 6 and MOE_PART == 1:
            dump_and_finish(s, c, BF)
            return

        if stage >= 6 and MOE_PART == 0:
            dump_and_finish(s, c, BF)
            return
        b.wait("sy", c["ve"])
        b.inc("sy", c["ld"], 16, lambda y: y.dma_start(out=B2F[:],
                                                       in_=dt["b2l"]))
        b2f_ld = c["ld"].n
        b.wait("te", c["ld"], b2f_ld)
        first_core_pass = (s == 0)
        for ttg in range(NTT):
            t0 = ttg * TT
            b.wait("te", c["ve"])
            acc_evac = {}
            for oc in range(NCH):
                if (oc % 4) in acc_evac:
                    b.wait("te", c["ve"], acc_evac[oc % 4])
                b.inc("te", c["pe"], 1, lambda t, oc=oc, t0=t0: t.matmul(
                    psA_s(oc), B2F[:, oc * P:(oc + 1) * P],
                    WEFM[:, t0:t0 + TT], start=True, stop=True))
                b.wait("ve", c["pe"])
                b.inc("ve", c["ve"], 1, lambda v, oc=oc, t0=t0: v.tensor_add(
                    out=BF[:, oc * T + t0:oc * T + t0 + TT], in0=psA_s(oc),
                    in1=BF[:, oc * T + t0:oc * T + t0 + TT]))
                acc_evac[oc % 4] = c["ve"].n
            if stage >= 6 and MOE_PART == 2 and ttg == 1:
                break
            first_pass = first_core_pass and ttg == 0

            def mk_loader(ring, ldq, wb_vals, dram_first, dram_scr, scr_back):
                def do_load(e, he):
                    li = len(ring.load_vals)
                    if first_pass:
                        if li >= ring.nslots:
                            b.on(ldq, lambda g, j=li - ring.nslots:
                                 g.wait_ge(g_wb.sem, wb_vals[j]))
                        ring.load(ldq, lambda g, e=e, he=he, sl=li % 2:
                                  g.dma_start(out=dram_first[0][:, sl],
                                              in_=dram_first[1](e, he)))
                        b.wait("sy", c["wg"], ring.load_vals[li])
                        b.inc("sy", g_wb, 16, lambda y, e=e, he=he, sl=li % 2:
                              y.dma_start(out=scr_back(e, he),
                                          in_=dram_first[0][:, sl]))
                        wb_vals[li] = g_wb.n
                    else:
                        ring.load(ldq, lambda s_, e=e, he=he, sl=li % 2:
                                  s_.dma_start(out=dram_first[0][:, sl],
                                               in_=dram_scr(e, he)))
                    return li
                return do_load

            ldq = "gp" if first_pass else "se"
            ldc = c["wg"] if first_pass else c["sq"]
            if not first_pass and s == 0:
                b.wait("se", g_wb)
            r1 = Ring(b, 2, ldc, c["pe"])
            r2 = Ring(b, 2, ldc, c["pe"])
            wb1, wb2 = {}, {}
            load_w1 = mk_loader(
                r1, ldq, wb1,
                (W1E, lambda e, he: w1_c[e][:, :, he * TT:(he + 1) * TT]),
                lambda e, he: w1b_c[e][:, :, he * TT:(he + 1) * TT],
                lambda e, he: w1b_c[e][:, :, he * TT:(he + 1) * TT])
            load_w2 = mk_loader(
                r2, ldq, wb2,
                (W2E, lambda e, he: w2_c[e][:, he * 4:(he + 1) * 4, :]),
                lambda e, he: w2b_c[e][:, he * 4:(he + 1) * 4, :],
                lambda e, he: w2b_c[e][:, he * 4:(he + 1) * 4, :])

            ne_run = NE if MOE_PART >= 9 else (MOE_PART - 2 if MOE_PART >= 3
                                               else 0)
            w1_sched = [(e, he) for e in range(ne_run) for he in range(NE)]
            w2_sched = [(e, he) for e in range(ne_run) for he in range(NE)]
            w1_pos = w2_pos = 0
            LA1 = 2
            for _ in range(min(LA1, len(w1_sched))):
                load_w1(*w1_sched[w1_pos]); w1_pos += 1
            for _ in range(min(2, len(w2_sched))):
                load_w2(*w2_sched[w2_pos]); w2_pos += 1
            for e in range(ne_run):
                b.wait("gp", c["pe"])
                b.inc("gp", c["wg"], 16, lambda g, e=e, t0=t0: g.dma_start(
                    out=FTMP[0:1, 0:TT], in_=WEFM[e:e + 1, t0:t0 + TT]))
                werow_ld = c["wg"].n
                b.wait("te", c["ve"])
                gelu_mark = {}
                for he in range(NE):
                    li = e * NE + he
                    r1.use(li)
                    slot = li % 2
                    for hh in range(4):
                        hcg = he * 4 + hh
                        if (hcg % 4) in gelu_mark:
                            b.wait("te", c["se"], gelu_mark[hcg % 4])
                        for ch in range(NCH):
                            lastmm = ch == NCH - 1
                            def amm(t, sl=slot, ch=ch, hh=hh, hcg=hcg, t0=t0):
                                return t.matmul(
                                    psA_s(hcg),
                                    W1E[:, sl, ch, hh * P:(hh + 1) * P],
                                    CB1[:, ch * T + t0:ch * T + t0 + TT],
                                    start=(ch == 0), stop=(ch == NCH - 1))
                            if lastmm:
                                b.inc("te", c["pe"], 1, amm)
                            else:
                                b.on("te", amm)
                        b.wait("se", c["pe"])
                        b.inc("se", c["se"], 1, lambda s_, hcg=hcg, e=e:
                              s_.activation(
                                  out=GBUF[:, hcg, :], in_=psA_s(hcg),
                                  func=AF.Gelu_apprx_tanh,
                                  bias=B1S[:, e * NHM + hcg:e * NHM + hcg + 1]))
                        gelu_mark[hcg % 4] = c["se"].n
                    r1.consumed(li)
                    if w1_pos < len(w1_sched):
                        load_w1(*w1_sched[w1_pos]); w1_pos += 1
                b.wait("te", c["wg"], werow_ld)
                b.wait("te", c["ve"])
                b.inc("te", c["pe"], 1, lambda t: t.matmul(
                    pb6[:], ONESR[:], FTMP[0:1, 0:TT], start=True, stop=True))
                b.wait("ve", c["pe"])
                b.inc("ve", c["ve"], 1, lambda v: v.tensor_copy(out=WEB[:],
                                                                in_=pb6[:]))
                web_done = c["ve"].n
                b.wait("te", c["se"])
                b.wait("te", c["ve"])
                for he in range(NE):
                    li = e * NE + he
                    r2.use(li)
                    slot = li % 2
                    for oc in range(NCH):
                        for hh in range(4):
                            hcg = he * 4 + hh
                            lastmm = (oc == NCH - 1 and hh == 3)
                            def bmm(t, sl=slot, oc=oc, hh=hh, hcg=hcg, he=he):
                                return t.matmul(
                                    moe_ps(oc),
                                    W2E[:, sl, hh, oc * P:(oc + 1) * P],
                                    GBUF[:, hcg, :],
                                    start=(he == 0 and hh == 0),
                                    stop=(he == NE - 1 and hh == 3))
                            if lastmm:
                                b.inc("te", c["pe"], 1, bmm)
                            else:
                                b.on("te", bmm)
                    r2.consumed(li)
                    if w2_pos < len(w2_sched):
                        load_w2(*w2_sched[w2_pos]); w2_pos += 1
                b.wait("ve", c["pe"])
                b.wait("ve", c["ve"], web_done)
                for oc in range(NCH):
                    b.on("ve", lambda v, oc=oc: v.tensor_mul(
                        out=MTMP[:], in0=moe_ps(oc), in1=WEB[:]))
                    b.on("ve", lambda v, oc=oc, t0=t0: v.tensor_add(
                        out=BF[:, oc * T + t0:oc * T + t0 + TT], in0=MTMP[:],
                        in1=BF[:, oc * T + t0:oc * T + t0 + TT]))
                b.inc("ve", c["ve"], 1, lambda v: v.tensor_copy(out=VNOP[:, 0:1], in_=VNOP[:, 1:2]))
            if stage == 6 or MOE_PART < 9:
                continue
            stats(c, BF, ttg, True, 1e-5, MCAST, MSQ, FTMP)
            b.wait("sy", c["pe"])
            xnt_ld = []
            for ch in range(NCH):
                b.inc("sy", c["ld"], 16, lambda y, i=ch, t0=t0: y.dma_start(
                    out=XNT8[:, i, :], in_=xnew_c[s, i][:, t0:t0 + TT]))
                xnt_ld.append(c["ld"].n)
            for ch in range(NCH):
                o = ch * T + t0
                b.on("ve", lambda v, o=o: v.tensor_sub(
                    out=FTMP[:], in0=BF[:, o:o + TT], in1=MEANT[:]))
                b.on("ve", lambda v, o=o: v.tensor_mul(
                    out=FTMP[:], in0=FTMP[:], in1=RSTDT[:]))
                b.wait("ve", c["ld"], xnt_ld[ch])
                b.inc("ve", c["ve"], 1, lambda v, o=o, i=ch: v.tensor_add(
                    out=BF[:, o:o + TT], in0=XNT8[:, i, :], in1=FTMP[:]))
            b.wait("sy", c["ve"])
            for ch in range(NCH):
                b.inc("sy", c["out"], 16, lambda y, i=ch, t0=t0: y.dma_start(
                    out=ys_c[s, i][:, t0:t0 + TT],
                    in_=BF[:, i * T + t0:i * T + t0 + TT]))
        if stage == 6 or (stage >= 6 and MOE_PART < 9):
            dump_and_finish(s, c, BF)
            return

    for s in range(NSEQ):
        walk_seq(s, C[s])

    for s in range(NSEQ):
        b.wait("sy", C[s]["out"])

    blk = b.stack.enter_context(nc.Block())
    b.flush(blk)
    b.stack.close()
    return nc


IDFB = None


def _prep_inputs(inputs):
    x = inputs["L0_attn_x"]
    v128 = lambda v: np.ascontiguousarray(
        np.asarray(v, dtype=np.float32).reshape(NCH, P).T)
    common = {
        "wfc": np.ascontiguousarray(np.asarray(inputs["L0_attn_ff_fc_w"]).T),
        "wpr": np.ascontiguousarray(np.asarray(inputs["L0_attn_ff_proj_w"]).T),
        "wqkv": np.ascontiguousarray(inputs["L0_ffn_w_qkv"]),
        "wo": np.ascontiguousarray(inputs["L0_ffn_w_o"]),
        "wg": np.ascontiguousarray(inputs["L0_ffn_w_gate"]),
        "w1": np.ascontiguousarray(inputs["L0_ffn_w1"]),
        "w2": np.ascontiguousarray(inputs["L0_ffn_w2"]),
        "rms1w": v128(inputs["L0_attn_input_norm_w"]),
        "alog": v128(inputs["L0_attn_a_log"]),
        "cout": v128(inputs["L0_attn_c_out"]),
        "dskip": v128(inputs["L0_attn_d_skip"]),
        "rms2w": v128(inputs["L0_attn_ff_norm_w"]),
        "ln1w": v128(inputs["L0_ffn_ln1_w"]),
        "ln1b": v128(inputs["L0_ffn_ln1_b"]),
        "ln2w": v128(inputs["L0_ffn_ln2_w"]),
        "ln2b": v128(inputs["L0_ffn_ln2_b"]),
        "bo": v128(inputs["L0_ffn_b_o"]),
        "bqkv": np.ascontiguousarray(
            np.asarray(inputs["L0_ffn_b_qkv"], dtype=np.float32)
            .reshape(24, P).T),
        "b1s": np.ascontiguousarray(
            np.asarray(inputs["L0_ffn_b1"], dtype=np.float32)
            .reshape(NE, NHM, P).transpose(2, 0, 1).reshape(P, NE * NHM)),
        "b2l": np.ascontiguousarray(inputs["L0_ffn_b2"]),
    }
    common = {k: np.asarray(v, dtype=np.float32) for k, v in common.items()}
    in_maps = []
    for core in range(8):
        xs = np.ascontiguousarray(
            np.asarray(x[4 * core:4 * core + 4]).transpose(0, 2, 1),
            dtype=np.float32)
        m = {"xs": xs}
        m.update(common)
        in_maps.append(m)
    return in_maps


def kernel(**inputs):
    in_maps = _prep_inputs(inputs)
    nc = build_program(STAGE_FULL)
    res = run_bass_kernel_spmd(nc, in_maps, core_ids=list(range(8)))
    out = np.empty((32, T, D), dtype=np.float32)
    for core in range(8):
        ys = res.results[core]["ys"]
        for i in range(NSEQ):
            out[4 * core + i] = ys[i].T
    return out


# version 07 (final)
# speedup vs baseline: 1.0885x; 1.0105x over previous
import contextlib
import numpy as np
import concourse.bass as bass
import concourse.mybir as mybir
from concourse.bass_utils import run_bass_kernel_spmd

F32 = mybir.dt.float32
BF16 = mybir.dt.bfloat16
AF = mybir.ActivationFunctionType
ALU = mybir.AluOpType
AX = mybir.AxisListType

P = 128
NSEQ = 4
T = 2048
D = 1024
NCH = 8
HS = 2048
NHS = 16
NH = 16
DH = 64
HM = 4096
NHM = 32
NE = 8
NTT = 4
TT = 512
KB = 1024

STAGE_FULL = 7
ATT_PART = 4
TRCP_ON = True
QT_N = 4
HP_N = 8
OVERSYNC = False
MOE_PART = 9


class Ctr:
    __slots__ = ("sem", "n")

    def __init__(self, sem):
        self.sem = sem
        self.n = 0


class Bld:
    def __init__(self, nc):
        self.nc = nc
        self.q = {"sy": [], "gp": [], "te": [], "ve": [], "se": []}
        self.stack = contextlib.ExitStack()

    def on(self, eng, fn):
        self.q[eng].append(fn)

    def ctr(self, name):
        return Ctr(self.stack.enter_context(self.nc.semaphore(name)))

    def wait(self, eng, c, v=None):
        v = c.n if v is None else v
        if v > 0:
            self.on(eng, lambda e, s=c.sem, vv=v: e.wait_ge(s, vv))

    def inc(self, eng, c, k, fn):
        c.n += k
        self.on(eng, lambda e, f=fn, s=c.sem, kk=k: f(e).then_inc(s, kk))

    def make_arena(self, total_bytes):
        self.arena = self.stack.enter_context(
            self.nc.sbuf_tensor("arena", [P, total_bytes], mybir.dt.uint8))
        self.arena_base = self.nc.lookup_mloc(self.arena).addr

    def at(self, name, off_bytes, shape, dtype):
        nc = self.nc
        h = nc.virtual_sbuf_tensor(name, list(shape), dtype)
        nc._tensor(h.name, list(shape), dtype, type="SB")
        mloc = nc.lookup_mloc(h)
        mloc.allocated = not nc.target_bir_lowering
        mloc.addr = self.arena_base + off_bytes
        return h

    def flush(self, blk):
        for name, dec in (("sy", blk.sync), ("gp", blk.gpsimd),
                          ("te", blk.tensor), ("ve", blk.vector),
                          ("se", blk.scalar)):
            fns = self.q[name]
            self.q[name] = []
            if fns:
                def runner(engine, fns=fns):
                    for f in fns:
                        f(engine)
                dec(runner)


class Ring:

    def __init__(self, b, nslots, load_ctr, cons_ctr):
        self.b = b
        self.nslots = nslots
        self.load_ctr = load_ctr
        self.cons_ctr = cons_ctr
        self.load_vals = []
        self.cons_vals = {}
        self.floor = cons_ctr.n

    def load(self, eng, fn):
        i = len(self.load_vals)
        if i >= self.nslots:
            j = i - self.nslots
            self.b.on(eng, lambda e, r=self, j=j: e.wait_ge(
                r.cons_ctr.sem, r.cons_vals[j]))
        elif self.floor > 0:
            self.b.wait(eng, self.cons_ctr, self.floor)
        self.b.inc(eng, self.load_ctr, 16, fn)
        self.load_vals.append(self.load_ctr.n)
        return i

    def use(self, i, eng="te"):
        self.b.wait(eng, self.load_ctr, self.load_vals[i])

    def consumed(self, i):
        self.cons_vals[i] = self.cons_ctr.n


def build_program(stage=STAGE_FULL):
    nc = bass.Bass("TRN2", target_bir_lowering=False)
    b = Bld(nc)

    dt = {}
    dt["xs"] = nc.dram_tensor("xs", [NSEQ, D, T], F32, kind="ExternalInput").ap()
    for n_, sh in [("wfc", [D, HS]), ("wpr", [HS, D]), ("wqkv", [D, 3 * D]),
                   ("wo", [D, D]), ("wg", [D, NE]), ("w1", [NE, D, HM]),
                   ("w2", [NE, HM, D]), ("bqkv", [P, 24]),
                   ("b1s", [P, NE * NHM]), ("b2l", [NE, D])]:
        dt[n_] = nc.dram_tensor(n_, sh, F32, kind="ExternalInput").ap()
    for v in ["rms1w", "alog", "cout", "dskip", "rms2w",
              "ln1w", "ln1b", "ln2w", "ln2b", "bo"]:
        dt[v] = nc.dram_tensor(v, [P, NCH], F32, kind="ExternalInput").ap()
    dt["ys"] = nc.dram_tensor("ys", [NSEQ, D, T], F32, kind="ExternalOutput").ap()
    dt["xnew_s"] = nc.dram_tensor("xnew_s", [NSEQ, D, T], F32, kind="Internal").ap()
    dt["w1b"] = nc.dram_tensor("w1b", [NE, D, HM], BF16, kind="Internal").ap()
    dt["w2b"] = nc.dram_tensor("w2b", [NE, HM, D], BF16, kind="Internal").ap()

    xs_c = dt["xs"].rearrange("s (c p) t -> s c p t", p=P)
    ys_c = dt["ys"].rearrange("s (c p) t -> s c p t", p=P)
    xnew_c = dt["xnew_s"].rearrange("s (c p) t -> s c p t", p=P)
    wfc_c = dt["wfc"].rearrange("(c p) h -> p c h", p=P)
    wpr_c = dt["wpr"].rearrange("(c p) o -> p c o", p=P)
    wqkv_c = dt["wqkv"].rearrange("(c p) o -> p c o", p=P)
    wo_c = dt["wo"].rearrange("(c p) o -> p c o", p=P)
    wg_c = dt["wg"].rearrange("(c p) e -> p c e", p=P)
    w1_c = dt["w1"].rearrange("e (c p) h -> e p c h", p=P)
    w2_c = dt["w2"].rearrange("e (c p) o -> e p c o", p=P)
    w1b_c = dt["w1b"].rearrange("e (c p) h -> e p c h", p=P)
    w2b_c = dt["w2b"].rearrange("e (c p) o -> e p c o", p=P)

    b.make_arena(207 * KB + 768)
    A_ = b.at
    RX = A_("RX", 0, [P, NCH * T], F32)
    QB = A_("QB", 0, [P, NCH, T], BF16)
    KBU = A_("KBU", 32 * KB, [P, NCH, T], BF16)
    GBUF = A_("GBUF", 0, [P, NHM, TT], BF16)
    MCAST = A_("MCAST", 32 * KB, [P, NCH * TT], BF16)
    MSQ = A_("MSQ", 40 * KB, [P, NCH * TT], BF16)
    XNT8 = A_("XNT8", 32 * KB, [P, NCH, TT], F32)
    B2F = A_("B2F", 60 * KB, [NE, D], F32)
    WEFM = A_("WEFM", 48 * KB, [NE, T], F32)
    ZB = A_("ZB", 56 * KB, [P, P], F32)
    GE = A_("GE", 56 * KB + 512, [P, P], F32)
    WER = A_("WER", 56 * KB + 512, [1, TT], F32)
    GP_ = A_("GP", 57 * KB, [P, P], F32)
    GMK = A_("GMK", 57 * KB + 512, [P, P], F32)
    GIND = A_("GIND", 58 * KB, [P, P], F32)
    GM1 = A_("GM1", 58 * KB + 512, [P, 16], F32)
    GM2 = A_("GM2", 58 * KB + 576, [P, 16], F32)
    GS = A_("GS", 58 * KB + 640, [P, 16], F32)
    WETM = A_("WETM", 59 * KB, [P, P], F32)
    BF = A_("BF", 64 * KB, [P, NCH * T], F32)
    CB1 = A_("CB1", 128 * KB, [P, NCH * T], BF16)
    PT0 = A_("PT0", 128 * KB, [P, 4 * TT], BF16)
    PT1 = A_("PT1", 132 * KB, [P, 4 * TT], BF16)
    CTXB = A_("CTXB", 136 * KB, [P, NCH * TT], BF16)
    SR = A_("SR", 144 * KB, [P, TT], F32)
    CSQ = A_("CSQ", 146 * KB, [P, T], BF16)
    CQM = A_("CQM", 150 * KB, [P, NCH], F32)
    CKM = A_("CKM", 150 * KB + 64, [P, NCH], F32)
    CBD = A_("CBD", 150 * KB + 128, [P, NCH], F32)
    VFM = A_("VFM", 198 * KB, [P, T], BF16)
    CASTA = A_("CASTA", 160 * KB, [P, NCH * TT], BF16)
    SQA = A_("SQA", 168 * KB, [P, NCH * TT], BF16)
    HID = A_("HID", 160 * KB, [P, NHS, TT], BF16)
    WFC = A_("WFC", 176 * KB, [P, 4, NCH, P], BF16)
    WPR = A_("WPR", 184 * KB, [P, 2, NHS, P], BF16)
    VT = A_("VT", 160 * KB, [P, 16, NH * DH], BF16)
    WQO = A_("WQO", 192 * KB + 2 * KB, [P, 2, NCH, P], BF16)
    W1E = A_("W1E", 160 * KB, [P, 2, NCH, TT], BF16)
    W2E = A_("W2E", 176 * KB, [P, 2, 4, D], BF16)
    WEB = A_("WEB", 192 * KB, [P, TT], F32)
    MTMP = A_("MTMP", 194 * KB, [P, TT], F32)
    MEANT = A_("MEANT", 198 * KB, [P, TT], F32)
    RSTDT = A_("RSTDT", 200 * KB, [P, TT], F32)
    FTMP = A_("FTMP", 202 * KB, [P, TT], F32)
    DTMP = A_("DTMP", 192 * KB, [P, T], F32)
    ONESB = A_("ONESB", 204 * KB, [P, P], BF16)
    IDFB = A_("IDFB", 204 * KB + 256, [P, P], BF16)
    IDF = A_("IDF", 204 * KB + 512, [P, P], F32)
    ONESR = A_("ONESR", 205 * KB, [1, P], F32)
    VEC = {}
    off = 205 * KB + 512
    for v in ["rms1w", "alog", "cout", "dskip", "rms2w",
              "ln1w", "ln1b", "ln2w", "ln2b", "bo"]:
        VEC[v] = A_("sb_" + v, off, [P, NCH], F32)
        off += 32
    BQKV = A_("BQKV", off, [P, 24], F32); off += 96
    AV = A_("AV", off, [P, NCH], F32); off += 32
    AVT = A_("AVT", off, [P, NCH], F32); off += 32
    B1S = A_("B1S", off, [P, NE * NHM], F32); off += 1024
    WGB = A_("WGB", off, [P, NCH, NE], BF16); off += 128
    WGF = A_("WGF", off, [P, NCH, NE], F32); off += 256
    VNOP = A_("VNOP", off, [1, 8], F32); off += 32
    SNOP = A_("SNOP", off, [1, 8], F32); off += 32
    IOF = A_("IOF", 0, [P, P], F32)
    IOP = A_("IOP", 512, [P, 1], F32)
    assert off <= 207 * KB + 768, off

    psA = b.stack.enter_context(nc.psum_tensor("psA", [P, 4 * TT], F32))
    psB = b.stack.enter_context(nc.psum_tensor("psB", [P, 2 * TT], F32))
    pb6 = b.stack.enter_context(nc.psum_tensor("pb6", [P, TT], F32))
    pb7 = b.stack.enter_context(nc.psum_tensor("pb7", [P, TT], F32))
    psTv = pb6[:].bitcast(BF16)

    def psA_s(i):
        return psA[:, (i % 4) * TT:((i % 4) + 1) * TT]

    def psB_s(i):
        return psB[:, (i % 2) * TT:((i % 2) + 1) * TT]

    def moe_ps(oc):
        if oc < 4:
            return psA_s(oc)
        if oc < 6:
            return psB_s(oc - 4)
        return pb6[:] if oc == 6 else pb7[:]

    g_ld = b.ctr("g_ld")
    g_cmp = b.ctr("g_cmp")
    g_cmps = b.ctr("g_cmps")
    g_cmpg = b.ctr("g_cmpg")
    g_wb = b.ctr("g_wb")
    C = []
    for s in range(NSEQ):
        C.append({k: b.ctr(f"{k}{s}") for k in
                  ["ld", "wg", "sq", "pe", "ve", "se", "vch", "sch", "out"]})

    def vedge(c):
        ch = c["vch"]
        ch.n += 1
        b.on("ve", lambda e, s=ch.sem: e.tensor_copy(
            out=VNOP[:, 0:1], in_=VNOP[:, 1:2]).then_inc(s, 1))
        b.on("ve", lambda e, s=ch.sem, v=ch.n: e.wait_ge(s, v))

    b.on("gp", lambda g: g.iota(IOF[:], pattern=[[1, P]], channel_multiplier=0,
                                allow_small_or_imprecise_dtypes=True))
    b.inc("gp", g_cmpg, 1, lambda g: g.iota(
        IOP[:], pattern=[[0, 1]], channel_multiplier=1,
        allow_small_or_imprecise_dtypes=True))
    b.wait("ve", g_cmpg, 1)
    b.on("ve", lambda v: v.tensor_scalar(out=IDF[:], in0=IOF[:],
                                         scalar1=IOP[:, 0:1], scalar2=None,
                                         op0=ALU.is_equal))
    b.on("ve", lambda v: v.tensor_copy(out=IDFB[:], in_=IDF[:]))
    b.on("ve", lambda v: v.memset(ONESB[:], 1.0))
    b.inc("ve", g_cmp, 1, lambda v: v.memset(ONESR[:], 1.0))
    for v in ["rms1w", "alog", "cout", "dskip", "rms2w",
              "ln1w", "ln1b", "ln2w", "ln2b", "bo"]:
        b.inc("sy", g_ld, 16, lambda s, d=VEC[v], src=dt[v]:
              s.dma_start(out=d[:], in_=src))
    b.inc("sy", g_ld, 16, lambda s: s.dma_start(out=BQKV[:], in_=dt["bqkv"]))
    b.inc("sy", g_ld, 16, lambda s: s.dma_start(out=B1S[:], in_=dt["b1s"]))
    b.inc("sy", g_ld, 16, lambda s: s.dma_start(out=WGF[:], in_=wg_c))
    b.wait("se", g_ld, 16 * 2)
    b.inc("se", g_cmps, 1, lambda s: s.activation(out=AVT[:],
                                                  in_=VEC["alog"][:],
                                                  func=AF.Exp))
    b.on("se", lambda s, v=g_cmps.n: s.wait_ge(g_cmps.sem, v))
    b.inc("se", g_cmps, 1, lambda s: s.activation(out=AV[:], in_=AVT[:],
                                                  func=AF.Exp, scale=-1.0))


    def stats(c, src, tt, mean, eps, cast, sq, tmp, src_stride=T):
        t0 = tt * TT
        b.wait("se", c["ve"])
        for ch in range(NCH):
            so = ch * src_stride + t0
            b.on("se", lambda s, so=so, i=ch: s.activation(
                out=sq[:, i * TT:(i + 1) * TT], in_=src[:, so:so + TT],
                func=AF.Square))
        b.inc("se", c["se"], 1, lambda s: s.copy(out=SNOP[:, 0:1], in_=SNOP[:, 1:2]))
        if mean:
            for ch in range(NCH):
                so = ch * src_stride + t0
                b.on("ve", lambda v, so=so, i=ch: v.tensor_copy(
                    out=cast[:, i * TT:(i + 1) * TT], in_=src[:, so:so + TT]))
            b.inc("ve", c["ve"], 1, lambda v: v.tensor_copy(out=VNOP[:, 0:1], in_=VNOP[:, 1:2]))
        b.wait("te", c["se"])
        b.wait("te", c["ve"])
        for ch in range(NCH):
            last = ch == NCH - 1
            if last and not mean:
                b.inc("te", c["pe"], 1, lambda t, i=ch: t.matmul(
                    psB[:, TT:2 * TT], ONESB[:], sq[:, i * TT:(i + 1) * TT],
                    start=(i == 0), stop=True))
            else:
                b.on("te", lambda t, i=ch: t.matmul(
                    psB[:, TT:2 * TT], ONESB[:], sq[:, i * TT:(i + 1) * TT],
                    start=(i == 0), stop=(i == NCH - 1)))
        if mean:
            for ch in range(NCH):
                if ch == NCH - 1:
                    b.inc("te", c["pe"], 1, lambda t, i=ch: t.matmul(
                        psB[:, 0:TT], ONESB[:], cast[:, i * TT:(i + 1) * TT],
                        start=(i == 0), stop=True))
                else:
                    b.on("te", lambda t, i=ch: t.matmul(
                        psB[:, 0:TT], ONESB[:], cast[:, i * TT:(i + 1) * TT],
                        start=(i == 0), stop=False))
        b.wait("ve", c["pe"])
        inv = 1.0 / D
        if mean:
            b.on("ve", lambda v: v.tensor_scalar(
                out=MEANT[:], in0=psB[:, 0:TT], scalar1=inv, scalar2=None,
                op0=ALU.mult))
            b.on("ve", lambda v: v.tensor_scalar(
                out=RSTDT[:], in0=psB[:, TT:2 * TT], scalar1=inv, scalar2=None,
                op0=ALU.mult))
            b.on("ve", lambda v: v.tensor_mul(out=tmp[:, 0:TT], in0=MEANT[:],
                                              in1=MEANT[:]))
            b.on("ve", lambda v: v.scalar_tensor_tensor(
                out=RSTDT[:], in0=RSTDT[:], scalar=eps, in1=tmp[:, 0:TT],
                op0=ALU.add, op1=ALU.subtract))
        else:
            b.on("ve", lambda v: v.tensor_scalar(
                out=RSTDT[:], in0=psB[:, TT:2 * TT], scalar1=inv, scalar2=eps,
                op0=ALU.mult, op1=ALU.add))
        b.inc("ve", c["ve"], 1, lambda v: v.tensor_copy(out=VNOP[:, 0:1], in_=VNOP[:, 1:2]))
        b.wait("se", c["ve"])
        b.inc("se", c["se"], 1, lambda s: s.activation(out=RSTDT[:],
                                                       in_=RSTDT[:],
                                                       func=AF.Sqrt))
        b.wait("ve", c["se"])
        b.inc("ve", c["ve"], 1, lambda v: v.reciprocal(out=RSTDT[:],
                                                       in_=RSTDT[:]))

    def dump_and_finish(s, c, src):
        b.wait("sy", c["ve"])
        b.wait("sy", c["se"])
        for ch in range(NCH):
            b.inc("sy", c["out"], 16, lambda y, i=ch, d=src: y.dma_start(
                out=ys_c[s, i], in_=d[:, i * T:(i + 1) * T]))

    if stage == 45:
        stage5_qt1 = True
    def walk_seq(s, c):
        if s == 0:
            b.wait("sy", g_cmp, 1)
        else:
            for eng in ("sy", "gp", "te", "ve", "se"):
                for k in ("pe", "ve", "se", "out", "ld", "wg", "sq"):
                    b.wait(eng, C[s - 1][k])
        for ch in range(NCH):
            b.inc("sy", c["ld"], 16, lambda y, i=ch: y.dma_start(
                out=RX[:, i * T:(i + 1) * T], in_=xs_c[s, i]))
        x_loaded = c["ld"].n

        for eng in ("ve", "se", "te"):
            b.wait(eng, c["ld"], x_loaded)
            b.wait(eng, g_ld)
            b.wait(eng, g_cmp)
            b.wait(eng, g_cmps)
            b.wait(eng, g_cmpg)
        for tt in range(NTT):
            stats(c, RX, tt, True, 1e-5, CASTA, SQA, DTMP)
            for ch in range(NCH):
                o = ch * T + tt * TT
                b.on("ve", lambda v, o=o: v.tensor_sub(
                    out=BF[:, o:o + TT], in0=RX[:, o:o + TT], in1=MEANT[:]))
                b.on("ve", lambda v, o=o: v.tensor_mul(
                    out=BF[:, o:o + TT], in0=BF[:, o:o + TT], in1=RSTDT[:]))
        for tt in range(NTT):
            stats(c, BF, tt, False, 1e-6, CASTA, SQA, DTMP)
            for ch in range(NCH):
                o = ch * T + tt * TT
                b.on("ve", lambda v, o=o, i=ch: v.scalar_tensor_tensor(
                    out=RX[:, o:o + TT], in0=BF[:, o:o + TT],
                    scalar=VEC["rms1w"][:, i:i + 1], in1=RSTDT[:],
                    op0=ALU.mult, op1=ALU.mult))
        for ch in range(NCH):
            o = ch * T
            b.on("ve", lambda v, o=o, i=ch: v.tensor_tensor_scan(
                out=DTMP[:, 0:T], data0=AV[:, i:i + 1].broadcast_to([P, T]),
                data1=RX[:, o:o + T], initial=0.0, op0=ALU.mult, op1=ALU.add))
            b.on("ve", lambda v, o=o, i=ch: v.scalar_tensor_tensor(
                out=BF[:, o:o + T], in0=DTMP[:, 0:T],
                scalar=VEC["cout"][:, i:i + 1], in1=BF[:, o:o + T],
                op0=ALU.mult, op1=ALU.add))
            b.on("ve", lambda v, o=o, i=ch: v.scalar_tensor_tensor(
                out=BF[:, o:o + T], in0=RX[:, o:o + T],
                scalar=VEC["dskip"][:, i:i + 1], in1=BF[:, o:o + T],
                op0=ALU.mult, op1=ALU.add))
        b.inc("ve", c["ve"], 1, lambda v: v.tensor_copy(out=VNOP[:, 0:1], in_=VNOP[:, 1:2]))
        scan_done = c["ve"].n

        if stage == 1:
            dump_and_finish(s, c, BF)
            return

        b.wait("sy", c["ve"], scan_done)
        for ch in range(NCH):
            b.inc("sy", c["ld"], 16, lambda y, i=ch: y.dma_start(
                out=RX[:, i * T:(i + 1) * T], in_=xs_c[s, i]))
        x_reloaded = c["ld"].n

        for tt in range(NTT):
            stats(c, BF, tt, False, 1e-6, CASTA, SQA, DTMP)
            for ch in range(NCH):
                o = ch * T + tt * TT
                b.on("ve", lambda v, o=o, i=ch: v.scalar_tensor_tensor(
                    out=CB1[:, o:o + TT], in0=BF[:, o:o + TT],
                    scalar=VEC["rms2w"][:, i:i + 1], in1=RSTDT[:],
                    op0=ALU.mult, op1=ALU.mult))
        b.inc("ve", c["ve"], 1, lambda v: v.tensor_copy(out=VNOP[:, 0:1], in_=VNOP[:, 1:2]))
        n2_done = c["ve"].n
        b.wait("te", c["ve"], n2_done)

        rwfc = Ring(b, 4, c["wg"], c["pe"])
        rwpr = Ring(b, 2, c["wg"], c["pe"])
        sched = []
        for tt in range(NTT):
            sched += [("fc", tt, hc) for hc in range(NHS)]
            sched += [("pr", tt, oc) for oc in range(NCH)]
        wfc_idx = {}
        wpr_idx = {}

        def emit_load(item):
            kind, tt, i = item
            if kind == "fc":
                li = rwfc.load("gp", lambda g, i=i, sl=len(rwfc.load_vals) % 4:
                               g.dma_start(out=WFC[:, sl],
                                           in_=wfc_c[:, :, i * P:(i + 1) * P]))
                wfc_idx[(tt, i)] = li
            else:
                li = rwpr.load("gp", lambda g, i=i, sl=len(rwpr.load_vals) % 2:
                               g.dma_start(out=WPR[:, sl],
                                           in_=wpr_c[:, :, i * P:(i + 1) * P]))
                wpr_idx[(tt, i)] = li
        LA = 4
        for k in range(LA):
            emit_load(sched[k])
        load_pos = LA
        ps_evac = {}
        psb_evac = {}
        for tt in range(NTT):
            for hc in range(NHS):
                li = wfc_idx[(tt, hc)]
                rwfc.use(li)
                slot = li % 4
                if (hc % 4) in ps_evac:
                    b.wait("te", c["se"], ps_evac[hc % 4])
                for ch in range(NCH):
                    if ch == NCH - 1:
                        b.inc("te", c["pe"], 1, lambda t, sl=slot, i=ch,
                              tt=tt, hc=hc: t.matmul(
                                  psA_s(hc), WFC[:, sl, i, :],
                                  CB1[:, i * T + tt * TT:i * T + tt * TT + TT],
                                  start=False, stop=True))
                    else:
                        b.on("te", lambda t, sl=slot, i=ch, tt=tt, hc=hc:
                             t.matmul(
                                 psA_s(hc), WFC[:, sl, i, :],
                                 CB1[:, i * T + tt * TT:i * T + tt * TT + TT],
                                 start=(i == 0), stop=False))
                rwfc.consumed(li)
                if load_pos < len(sched):
                    emit_load(sched[load_pos])
                    load_pos += 1
                b.wait("se", c["pe"])
                b.inc("se", c["se"], 1, lambda s_, hc=hc: s_.activation(
                    out=HID[:, hc, :], in_=psA_s(hc), func=AF.Silu))
                ps_evac[hc % 4] = c["se"].n
            b.wait("te", c["se"])
            for oc in range(NCH):
                li = wpr_idx[(tt, oc)]
                rwpr.use(li)
                slot = li % 2
                if (oc % 2) in psb_evac:
                    b.wait("te", c["ve"], psb_evac[oc % 2])
                for hc in range(NHS):
                    if hc == NHS - 1:
                        b.inc("te", c["pe"], 1, lambda t, sl=slot, h=hc, oc=oc:
                              t.matmul(psB_s(oc), WPR[:, sl, h, :],
                                       HID[:, h, :], start=False, stop=True))
                    else:
                        b.on("te", lambda t, sl=slot, h=hc, oc=oc: t.matmul(
                            psB_s(oc), WPR[:, sl, h, :], HID[:, h, :],
                            start=(h == 0), stop=False))
                rwpr.consumed(li)
                if load_pos < len(sched):
                    emit_load(sched[load_pos])
                    load_pos += 1
                b.wait("ve", c["pe"])
                b.inc("ve", c["ve"], 1, lambda v, oc=oc, tt=tt: v.tensor_add(
                    out=BF[:, oc * T + tt * TT:oc * T + tt * TT + TT],
                    in0=psB_s(oc),
                    in1=BF[:, oc * T + tt * TT:oc * T + tt * TT + TT]))
                psb_evac[oc % 2] = c["ve"].n
        if stage == 2:
            dump_and_finish(s, c, BF)
            return

        b.wait("ve", c["ld"], x_reloaded)
        for tt in range(NTT):
            stats(c, BF, tt, True, 1e-5, CASTA, SQA, DTMP)
            for ch in range(NCH):
                o = ch * T + tt * TT
                b.on("ve", lambda v, o=o: v.tensor_sub(
                    out=DTMP[:, 0:TT], in0=BF[:, o:o + TT], in1=MEANT[:]))
                b.on("ve", lambda v, o=o: v.tensor_mul(
                    out=DTMP[:, 0:TT], in0=DTMP[:, 0:TT], in1=RSTDT[:]))
                b.on("ve", lambda v, o=o: v.tensor_add(
                    out=RX[:, o:o + TT], in0=RX[:, o:o + TT],
                    in1=DTMP[:, 0:TT]))
        b.inc("ve", c["ve"], 1, lambda v: v.tensor_copy(out=VNOP[:, 0:1], in_=VNOP[:, 1:2]))
        xnew_done = c["ve"].n
        b.wait("sy", c["ve"], xnew_done)
        for ch in range(NCH):
            b.inc("sy", c["out"], 16, lambda y, i=ch: y.dma_start(
                out=xnew_c[s, i], in_=RX[:, i * T:(i + 1) * T]))
        if stage == 3:
            dump_and_finish(s, c, RX)
            return

        for tt in range(NTT):
            stats(c, RX, tt, True, 1e-5, CASTA, SQA, DTMP)
            for ch in range(NCH):
                o = ch * T + tt * TT
                b.on("ve", lambda v, o=o: v.tensor_sub(
                    out=BF[:, o:o + TT], in0=RX[:, o:o + TT], in1=MEANT[:]))
                b.on("ve", lambda v, o=o: v.tensor_mul(
                    out=BF[:, o:o + TT], in0=BF[:, o:o + TT], in1=RSTDT[:]))
        for tt in range(NTT):
            stats(c, BF, tt, True, 1e-5, CASTA, SQA, DTMP)
            for ch in range(NCH):
                o = ch * T + tt * TT
                b.on("ve", lambda v, o=o: v.tensor_sub(
                    out=DTMP[:, 0:TT], in0=BF[:, o:o + TT], in1=MEANT[:]))
                b.on("ve", lambda v, o=o: v.tensor_mul(
                    out=DTMP[:, 0:TT], in0=DTMP[:, 0:TT], in1=RSTDT[:]))
                b.on("ve", lambda v, o=o, i=ch: v.tensor_scalar(
                    out=CB1[:, o:o + TT], in0=DTMP[:, 0:TT],
                    scalar1=VEC["ln1w"][:, i:i + 1],
                    scalar2=VEC["ln1b"][:, i:i + 1],
                    op0=ALU.mult, op1=ALU.add))
        b.inc("ve", c["ve"], 1, lambda v: v.tensor_copy(out=VNOP[:, 0:1], in_=VNOP[:, 1:2]))
        ln1_done = c["ve"].n
        if stage == 4:
            dump_and_finish(s, c, BF)
            return

        b.wait("te", c["ve"], ln1_done)
        b.wait("ve", c["out"])
        rqo = Ring(b, 2, c["wg"], c["pe"])
        qkv_idx = [rqo.load("gp", lambda g, i=ot, sl=ot % 2: g.dma_start(
            out=WQO[:, sl], in_=wqkv_c[:, :, i * P:(i + 1) * P]))
            for ot in range(24)]
        ps_free = {}
        last_evac_ve = 0
        last_evac_se = 0
        vtr = 0
        for ot in range(24):
            li = qkv_idx[ot]
            rqo.use(li)
            slot = li % 2
            b.wait("te", c["ve"], last_evac_ve)
            if OVERSYNC:
                b.wait("te", c["ve"])
                b.wait("te", c["se"])
            for ch in range(NCH):
                for tt in range(NTT):
                    lastmm = (ch == NCH - 1 and tt == NTT - 1)
                    if lastmm:
                        b.inc("te", c["pe"], 1, lambda t, sl=slot, i=ch, tt=tt:
                              t.matmul(psA_s(tt), WQO[:, sl, i, :],
                                       CB1[:, i * T + tt * TT:
                                           i * T + tt * TT + TT],
                                       start=False, stop=True))
                    else:
                        b.on("te", lambda t, sl=slot, i=ch, tt=tt: t.matmul(
                            psA_s(tt), WQO[:, sl, i, :],
                            CB1[:, i * T + tt * TT:i * T + tt * TT + TT],
                            start=(i == 0), stop=(i == NCH - 1)))
            rqo.consumed(li)
            if stage == 44 and ot == 23:
                b.wait("gp", c["pe"])
                b.inc("gp", c["out"], 16, lambda g, sl=slot: g.dma_start(
                    out=ys_c[s, 3][:, 0:1024],
                    in_=WQO[:, sl].rearrange("p a b -> p (a b)")))
            b.wait("ve", c["pe"])
            bias = BQKV[:, ot:ot + 1]
            if ot < 8:
                dst = QB[:, ot, :]
            elif ot < 16:
                dst = KBU[:, ot - 8, :]
            else:
                dst = VFM[:]
            b.inc("ve", c["ve"], 1, lambda v, d=dst, bi=bias: v.tensor_scalar(
                out=d, in0=psA[:], scalar1=bi, scalar2=None, op0=ALU.add))
            last_evac_ve = c["ve"].n
            if ot >= 16 and ATT_PART >= 1:
                vc = ot - 16
                b.wait("te", c["ve"], last_evac_ve)
                tr_marks = {}
                for k in range(16):
                    sl = k % 8
                    if k >= 1:
                        b.wait("te", c["ve"], tr_marks[k - 1])
                    b.inc("te", c["pe"], 1, lambda t, k=k, sl=sl: t.transpose(
                        psTv[:, sl * P:(sl + 1) * P], VFM[:, k * P:(k + 1) * P],
                        IDFB[:]))
                    b.wait("ve", c["pe"])
                    if TRCP_ON:
                        b.inc("ve", c["ve"], 1, lambda v, k=k, sl=sl, vc=vc:
                              v.tensor_copy(
                                  out=VT[:, k, 2 * vc * DH:(2 * vc + 2) * DH],
                                  in_=psTv[:, sl * P:(sl + 1) * P]))
                    else:
                        b.inc("ve", c["ve"], 1, lambda v, sl=sl:
                              v.tensor_copy(
                                  out=VFM[:, 0:P],
                                  in_=psTv[:, sl * P:(sl + 1) * P]))
                    tr_marks[k] = c["ve"].n
                last_evac_ve = c["ve"].n

        if stage == 5 and ATT_PART <= 1:
            dump_and_finish(s, c, BF)
            return
        for which, buf, dstm in ((0, QB, CQM), (1, KBU, CKM)):
            for hp in range(NCH):
                b.wait("se", c["ve"])
                b.inc("se", c["se"], 1, lambda s_, bu=buf, hp=hp: s_.activation(
                    out=CSQ[:], in_=bu[:, hp, :], func=AF.Square))
                b.wait("te", c["se"])
                b.wait("te", c["ve"])
                for q4 in range(4):
                    b.inc("te", c["pe"], 1, lambda t, q4=q4: t.matmul(
                        psA_s(q4), ONESB[:], CSQ[:, q4 * TT:(q4 + 1) * TT],
                        start=True, stop=True))
                b.wait("ve", c["pe"])
                b.inc("ve", c["ve"], 1, lambda v, d=dstm, hp=hp: v.reduce_max(
                    out=d[:, hp:hp + 1], in_=psA[:], axis=AX.X))
        vedge(c)
        b.wait("se", c["ve"])
        b.inc("se", c["se"], 1, lambda s_: s_.activation(out=CQM[:], in_=CQM[:],
                                                         func=AF.Sqrt))
        b.on("se", lambda s_, v=c["se"].n: s_.wait_ge(c["se"].sem, v))
        b.inc("se", c["se"], 1, lambda s_: s_.activation(out=CKM[:], in_=CKM[:],
                                                         func=AF.Sqrt))
        b.wait("ve", c["se"])
        b.inc("ve", c["ve"], 1, lambda v: v.tensor_mul(out=CBD[:], in0=CQM[:],
                                                       in1=CKM[:]))
        vedge(c)
        b.inc("ve", c["ve"], 1, lambda v: v.tensor_scalar(
            out=CBD[:], in0=CBD[:], scalar1=-0.125, scalar2=None, op0=ALU.mult))
        vedge(c)
        cbd_done = c["ve"].n

        if stage == 5 and ATT_PART == 2:
            dump_and_finish(s, c, BF)
            return
        b.wait("te", c["ve"], cbd_done)
        b.wait("se", c["ve"], cbd_done)
        rwo = Ring(b, 2, c["wg"], c["pe"])
        n_qt = 1 if (stage in (44, 45) or ATT_PART < 4) else QT_N
        n_hp = HP_N if ATT_PART >= 4 else 1
        wo_idx = {}
        for qt in range(n_qt):
            for oc in range(NCH):
                wo_idx[(qt, oc)] = rwo.load(
                    "gp", lambda g, i=oc, sl=(qt * NCH + oc) % 2: g.dma_start(
                        out=WQO[:, sl], in_=wo_c[:, :, i * P:(i + 1) * P]))
        last_ctx_evac = 0
        expA_prev = 0
        expB_prev = 0
        for qt in range(n_qt):
            for hpp in range(n_hp // 2):
                hpA, hpB = 2 * hpp, 2 * hpp + 1
                b.wait("te", c["ve"], last_ctx_evac)

                def s_pair(kt, hp, colbase, qt=qt):
                    for half in range(2):
                        def smm(t, half=half, kt=kt, hp=hp, cb=colbase,
                                qt=qt):
                            return t.matmul(
                                psA[:, cb + half * TT:cb + (half + 1) * TT],
                                KBU[half * DH:(half + 1) * DH, hp,
                                    kt * P:(kt + 1) * P],
                                QB[half * DH:(half + 1) * DH, hp,
                                   qt * TT:(qt + 1) * TT],
                                start=True, stop=True,
                                tile_position=(half * DH, 0))
                        if half == 1:
                            b.inc("te", c["pe"], 1, smm)
                        else:
                            b.on("te", smm)

                def ctx_s(kt, hp, pt, cpb, scol):
                    st = kt == 0
                    sp = kt == 15
                    b.on("te", lambda t, pt=pt, kt=kt, hp=hp, st=st, sp=sp,
                         cpb=cpb: t.matmul(
                             cpb[0:DH, :],
                             VT[:, kt, (2 * hp) * DH:(2 * hp + 1) * DH],
                             pt[:, 0:TT], start=st, stop=sp))
                    b.on("te", lambda t, pt=pt, kt=kt, hp=hp, st=st, sp=sp,
                         cpb=cpb: t.matmul(
                             cpb[DH:P, :],
                             VT[:, kt, (2 * hp + 1) * DH:(2 * hp + 2) * DH],
                             pt[:, TT:2 * TT], start=st, stop=sp,
                             tile_position=(0, DH), skip_group_check=True))
                    b.on("te", lambda t, pt=pt, st=st, sp=sp, sc=scol:
                         t.matmul(psB[0:DH, sc:sc + TT], ONESB[:, 0:DH],
                                  pt[:, 0:TT], start=st, stop=sp))
                    def smm2(t, pt=pt, st=st, sp=sp, sc=scol):
                        return t.matmul(
                            psB[DH:P, sc:sc + TT], ONESB[:, 0:DH],
                            pt[:, TT:2 * TT], start=st, stop=sp,
                            tile_position=(0, DH), skip_group_check=True)
                    if sp:
                        b.inc("te", c["pe"], 1, smm2)
                    else:
                        b.on("te", smm2)

                eA = {}
                eB = {}
                for kt in range(16):
                    ptA = (PT0[:, 0:2 * TT], PT0[:, 2 * TT:4 * TT])[kt % 2]
                    ptB = (PT1[:, 0:2 * TT], PT1[:, 2 * TT:4 * TT])[kt % 2]
                    b.wait("te", c["se"], eA.get(kt - 1, expA_prev))
                    s_pair(kt, hpA, 0)
                    b.wait("se", c["pe"])
                    b.inc("se", c["se"], 1, lambda s_, pt=ptA, hp=hpA:
                          s_.activation(out=pt, in_=psA[:, 0:2 * TT],
                                        func=AF.Exp, scale=0.125,
                                        bias=CBD[:, hp:hp + 1]))
                    eA[kt] = c["se"].n
                    b.wait("te", c["se"], eB.get(kt - 1, expB_prev))
                    s_pair(kt, hpB, 2 * TT)
                    b.wait("se", c["pe"])
                    b.inc("se", c["se"], 1, lambda s_, pt=ptB, hp=hpB:
                          s_.activation(out=pt, in_=psA[:, 2 * TT:4 * TT],
                                        func=AF.Exp, scale=0.125,
                                        bias=CBD[:, hp:hp + 1]))
                    eB[kt] = c["se"].n
                    if kt >= 1:
                        pa = (PT0[:, 0:2 * TT], PT0[:, 2 * TT:4 * TT])[(kt - 1) % 2]
                        pb_ = (PT1[:, 0:2 * TT], PT1[:, 2 * TT:4 * TT])[(kt - 1) % 2]
                        b.wait("te", c["se"], eA[kt - 1])
                        ctx_s(kt - 1, hpA, pa, pb6, 0)
                        b.wait("te", c["se"], eB[kt - 1])
                        ctx_s(kt - 1, hpB, pb_, pb7, TT)
                pa = (PT0[:, 0:2 * TT], PT0[:, 2 * TT:4 * TT])[15 % 2]
                pb_ = (PT1[:, 0:2 * TT], PT1[:, 2 * TT:4 * TT])[15 % 2]
                b.wait("te", c["se"], eA[15])
                ctx_s(15, hpA, pa, pb6, 0)
                b.wait("te", c["se"], eB[15])
                ctx_s(15, hpB, pb_, pb7, TT)
                expA_prev, expB_prev = eA[15], eB[15]
                b.wait("ve", c["pe"])
                for hp, cpb, sc in ((hpA, pb6, 0), (hpB, pb7, TT)):
                    b.on("ve", lambda v, sc=sc: v.reciprocal(
                        out=SR[0:DH, :], in_=psB[0:DH, sc:sc + TT]))
                    b.on("ve", lambda v, sc=sc: v.reciprocal(
                        out=SR[DH:P, :], in_=psB[DH:P, sc:sc + TT]))
                    b.on("ve", lambda v, hp=hp, cpb=cpb: v.tensor_mul(
                        out=CTXB[0:DH, hp * TT:(hp + 1) * TT],
                        in0=cpb[0:DH, :], in1=SR[0:DH, :]))
                    b.inc("ve", c["ve"], 1, lambda v, hp=hp, cpb=cpb:
                          v.tensor_mul(
                              out=CTXB[DH:P, hp * TT:(hp + 1) * TT],
                              in0=cpb[DH:P, :], in1=SR[DH:P, :]))
                last_ctx_evac = c["ve"].n
            b.wait("te", c["ve"], last_ctx_evac)
            opj_evac = {}
            for oc in range(NCH):
                li = wo_idx[(qt, oc)]
                rwo.use(li)
                slot = li % 2
                if (oc % 4) in opj_evac:
                    b.wait("te", c["ve"], opj_evac[oc % 4])
                for hp in range(NCH):
                    lastmm = hp == NCH - 1
                    def omm(t, sl=slot, hp=hp, oc=oc):
                        return t.matmul(psA_s(oc), WQO[:, sl, hp, :],
                                        CTXB[:, hp * TT:(hp + 1) * TT],
                                        start=(hp == 0), stop=(hp == NCH - 1))
                    if lastmm:
                        b.inc("te", c["pe"], 1, omm)
                    else:
                        b.on("te", omm)
                rwo.consumed(li)
                b.wait("ve", c["pe"])
                b.inc("ve", c["ve"], 1, lambda v, oc=oc, qt=qt: (
                    v.scalar_tensor_tensor(
                        out=BF[:, oc * T + qt * TT:oc * T + qt * TT + TT],
                        in0=psA_s(oc), scalar=VEC["bo"][:, oc:oc + 1],
                        in1=BF[:, oc * T + qt * TT:oc * T + qt * TT + TT],
                        op0=ALU.add, op1=ALU.add)))
                opj_evac[oc % 4] = c["ve"].n
            last_ctx_evac = c["ve"].n
            if stage == 44 and qt == 0:
                b.wait("gp", c["ve"])
                b.inc("gp", c["out"], 16, lambda g: g.dma_start(
                    out=ys_c[s, 0][:, 0:1024], in_=VT[:, 0, :]))
                b.inc("gp", c["out"], 16, lambda g: g.dma_start(
                    out=ys_c[s, 0][:, 1024:2048], in_=VT[:, 5, :]))

                b.inc("gp", c["out"], 16, lambda g: g.dma_start(
                    out=ys_c[s, 1][:, 0:TT], in_=MEANT[:]))
                b.inc("gp", c["out"], 16, lambda g: g.dma_start(
                    out=ys_c[s, 1][:, TT:2 * TT], in_=RSTDT[:]))
                b.inc("gp", c["out"], 16, lambda g: g.dma_start(
                    out=ys_c[s, 2][:, 0:NCH], in_=CBD[:]))
                return
            if stage == 45 and qt == 0:
                b.wait("gp", c["ve"])
                b.inc("gp", c["out"], 16, lambda g: g.dma_start(
                    out=ys_c[s, 0], in_=CTXB[:, 0:T]))
                b.inc("gp", c["out"], 16, lambda g: g.dma_start(
                    out=ys_c[s, 1], in_=CTXB[:, T:2 * T]))
                b.inc("gp", c["out"], 16, lambda g: g.dma_start(
                    out=ys_c[s, 2][:, 0:NCH], in_=CBD[:]))
                b.inc("gp", c["out"], 16, lambda g: g.dma_start(
                    out=ys_c[s, 2][:, 8:16], in_=CQM[:]))
                b.inc("gp", c["out"], 16, lambda g: g.dma_start(
                    out=ys_c[s, 2][:, 16:24], in_=CKM[:]))
                b.inc("gp", c["out"], 16, lambda g: g.dma_start(
                    out=ys_c[s, 2][:, 512:1024], in_=SR[:]))
                return
        if stage == 5:
            dump_and_finish(s, c, BF)
            return

        for tt in range(NTT):
            stats(c, BF, tt, True, 1e-5, CASTA, SQA, DTMP)
            for ch in range(NCH):
                o = ch * T + tt * TT
                b.on("ve", lambda v, o=o: v.tensor_sub(
                    out=DTMP[:, 0:TT], in0=BF[:, o:o + TT], in1=MEANT[:]))
                b.on("ve", lambda v, o=o: v.tensor_mul(
                    out=DTMP[:, 0:TT], in0=DTMP[:, 0:TT], in1=RSTDT[:]))
                b.on("ve", lambda v, o=o, i=ch: v.tensor_scalar(
                    out=CB1[:, o:o + TT], in0=DTMP[:, 0:TT],
                    scalar1=VEC["ln2w"][:, i:i + 1],
                    scalar2=VEC["ln2b"][:, i:i + 1],
                    op0=ALU.mult, op1=ALU.add))
                b.wait("ve", c["pe"])
                b.inc("ve", c["ve"], 1, lambda v, i=ch: v.tensor_scalar(
                    out=DTMP[:, TT:2 * TT], in0=DTMP[:, 0:TT],
                    scalar1=VEC["ln2w"][:, i:i + 1],
                    scalar2=VEC["ln2b"][:, i:i + 1],
                    op0=ALU.mult, op1=ALU.add))
                b.wait("te", c["ve"])
                for p4 in range(4):
                    def gmm(t, ch=ch, tt=tt, p4=p4):
                        sl = psA[:, p4 * TT:p4 * TT + NE]
                        return t.matmul(sl,
                                        DTMP[:, TT + p4 * P:TT + (p4 + 1) * P],
                                        WGF[:, ch, :],
                                        start=(ch == 0), stop=(ch == NCH - 1))
                    if p4 == 3:
                        b.inc("te", c["pe"], 1, gmm)
                    else:
                        b.on("te", gmm)
            b.wait("ve", c["pe"])
            for p4 in range(4):
                pt = tt * 4 + p4
                last = p4 == 3
                def zev(v, pt=pt, p4=p4):
                    return v.tensor_copy(
                        out=ZB[:, pt * NE:(pt + 1) * NE],
                        in_=psA[:, p4 * TT:p4 * TT + NE])
                if last:
                    b.inc("ve", c["ve"], 1, zev)
                else:
                    b.on("ve", zev)
        b.inc("ve", c["ve"], 1, lambda v: v.tensor_copy(out=VNOP[:, 0:1], in_=VNOP[:, 1:2]))
        ln2_done = c["ve"].n
        b.wait("te", c["ve"], ln2_done)
        zb3 = ZB[:].rearrange("p (a b) -> p a b", b=NE)
        ge3 = GE[:].rearrange("p (a b) -> p a b", b=NE)
        gp3 = GP_[:].rearrange("p (a b) -> p a b", b=NE)
        gmk3 = GMK[:].rearrange("p (a b) -> p a b", b=NE)
        gind3 = GIND[:].rearrange("p (a b) -> p a b", b=NE)
        wet3 = WETM[:].rearrange("p (a b) -> p a b", b=NE)
        b.on("ve", lambda v: v.reduce_max(out=GM1[:, :, None], in_=zb3,
                                          axis=AX.X, negate=True))
        vedge(c)
        b.on("ve", lambda v: v.tensor_add(
            out=ge3, in0=zb3, in1=GM1[:, :, None].broadcast_to([P, 16, NE])))
        b.inc("ve", c["ve"], 1, lambda v: v.tensor_copy(out=VNOP[:, 0:1], in_=VNOP[:, 1:2]))
        b.wait("se", c["ve"])
        b.inc("se", c["se"], 1, lambda s_: s_.activation(out=GE[:], in_=GE[:],
                                                         func=AF.Exp))
        b.wait("ve", c["se"])
        b.on("ve", lambda v: v.reduce_sum(out=GS[:, :, None], in_=ge3,
                                          axis=AX.X))
        vedge(c)
        b.on("ve", lambda v: v.reciprocal(out=GS[:], in_=GS[:]))
        vedge(c)
        b.on("ve", lambda v: v.tensor_mul(
            out=gp3, in0=ge3, in1=GS[:, :, None].broadcast_to([P, 16, NE])))
        b.on("ve", lambda v: v.reduce_max(out=GM1[:, :, None], in_=gp3,
                                          axis=AX.X))
        vedge(c)
        b.on("ve", lambda v: v.tensor_tensor(
            out=gmk3, in0=gp3, in1=GM1[:, :, None].broadcast_to([P, 16, NE]),
            op=ALU.is_equal))
        b.on("ve", lambda v: v.scalar_tensor_tensor(
            out=GMK[:], in0=GMK[:], scalar=-1e30, in1=GP_[:], op0=ALU.mult,
            op1=ALU.add))
        b.on("ve", lambda v: v.reduce_max(out=GM2[:, :, None], in_=gmk3,
                                          axis=AX.X))
        vedge(c)
        b.on("ve", lambda v: v.tensor_tensor(
            out=gind3, in0=gp3, in1=GM2[:, :, None].broadcast_to([P, 16, NE]),
            op=ALU.is_ge))
        b.on("ve", lambda v: v.tensor_add(out=GM1[:], in0=GM1[:], in1=GM2[:]))
        vedge(c)
        b.on("ve", lambda v: v.reciprocal(out=GM1[:], in_=GM1[:]))
        vedge(c)
        b.on("ve", lambda v: v.tensor_mul(out=gp3, in0=gp3, in1=gind3))
        b.inc("ve", c["ve"], 1, lambda v: v.tensor_mul(
            out=wet3, in0=gp3, in1=GM1[:, :, None].broadcast_to([P, 16, NE])))
        we_done = c["ve"].n
        b.wait("te", c["ve"], we_done)
        tr_evac = {}
        for pt in range(16):
            if pt - 1 in tr_evac:
                b.wait("te", c["ve"], tr_evac[pt - 1])
            b.inc("te", c["pe"], 1, lambda t, pt=pt: t.transpose(
                pb7[0:NE, (pt % 4) * P:(pt % 4 + 1) * P],
                WETM[:, pt * NE:(pt + 1) * NE], IDF[:]))
            b.wait("ve", c["pe"])
            b.inc("ve", c["ve"], 1, lambda v, pt=pt: v.tensor_copy(
                out=WEFM[:, pt * P:(pt + 1) * P],
                in_=pb7[0:NE, (pt % 4) * P:(pt % 4 + 1) * P]))
            tr_evac[pt] = c["ve"].n
        if stage >= 6 and MOE_PART == 1:
            dump_and_finish(s, c, BF)
            return

        if stage >= 6 and MOE_PART == 0:
            dump_and_finish(s, c, BF)
            return
        b.wait("sy", c["ve"])
        b.inc("sy", c["ld"], 16, lambda y: y.dma_start(out=B2F[:],
                                                       in_=dt["b2l"]))
        b2f_ld = c["ld"].n
        b.wait("te", c["ld"], b2f_ld)
        first_core_pass = (s == 0)
        def acc_init(ttg):
            t0 = ttg * TT
            b.wait("te", c["ve"])
            acc_evac = {}
            for oc in range(NCH):
                if (oc % 4) in acc_evac:
                    b.wait("te", c["ve"], acc_evac[oc % 4])
                b.inc("te", c["pe"], 1, lambda t, oc=oc, t0=t0: t.matmul(
                    psA_s(oc), B2F[:, oc * P:(oc + 1) * P],
                    WEFM[:, t0:t0 + TT], start=True, stop=True))
                b.wait("ve", c["pe"])
                b.inc("ve", c["ve"], 1, lambda v, oc=oc, t0=t0: v.tensor_add(
                    out=BF[:, oc * T + t0:oc * T + t0 + TT], in0=psA_s(oc),
                    in1=BF[:, oc * T + t0:oc * T + t0 + TT]))
                acc_evac[oc % 4] = c["ve"].n

        acc_init(0)
        for ttg in range(NTT):
            t0 = ttg * TT
            if stage >= 6 and MOE_PART == 2 and ttg == 1:
                break
            first_pass = first_core_pass and ttg == 0

            def mk_loader(ring, ldq, wb_vals, dram_first, dram_scr, scr_back):
                def do_load(e, he):
                    li = len(ring.load_vals)
                    if first_pass:
                        if li >= ring.nslots:
                            b.on(ldq, lambda g, j=li - ring.nslots:
                                 g.wait_ge(g_wb.sem, wb_vals[j]))
                        ring.load(ldq, lambda g, e=e, he=he, sl=li % 2:
                                  g.dma_start(out=dram_first[0][:, sl],
                                              in_=dram_first[1](e, he)))
                        b.wait("sy", c["wg"], ring.load_vals[li])
                        b.inc("sy", g_wb, 16, lambda y, e=e, he=he, sl=li % 2:
                              y.dma_start(out=scr_back(e, he),
                                          in_=dram_first[0][:, sl]))
                        wb_vals[li] = g_wb.n
                    else:
                        ring.load(ldq, lambda s_, e=e, he=he, sl=li % 2:
                                  s_.dma_start(out=dram_first[0][:, sl],
                                               in_=dram_scr(e, he)))
                    return li
                return do_load

            ldq = "gp" if first_pass else "se"
            ldc = c["wg"] if first_pass else c["sq"]
            if not first_pass and s == 0:
                b.wait("se", g_wb)
            r1 = Ring(b, 2, ldc, c["pe"])
            r2 = Ring(b, 2, ldc, c["pe"])
            wb1, wb2 = {}, {}
            load_w1 = mk_loader(
                r1, ldq, wb1,
                (W1E, lambda e, he: w1_c[e][:, :, he * TT:(he + 1) * TT]),
                lambda e, he: w1b_c[e][:, :, he * TT:(he + 1) * TT],
                lambda e, he: w1b_c[e][:, :, he * TT:(he + 1) * TT])
            load_w2 = mk_loader(
                r2, ldq, wb2,
                (W2E, lambda e, he: w2_c[e][:, he * 4:(he + 1) * 4, :]),
                lambda e, he: w2b_c[e][:, he * 4:(he + 1) * 4, :],
                lambda e, he: w2b_c[e][:, he * 4:(he + 1) * 4, :])

            ne_run = NE if MOE_PART >= 9 else (MOE_PART - 2 if MOE_PART >= 3
                                               else 0)
            w1_sched = [(e, he) for e in range(ne_run) for he in range(NE)]
            w2_sched = [(e, he) for e in range(ne_run) for he in range(NE)]
            w1_pos = w2_pos = 0
            LA1 = 2
            for _ in range(min(LA1, len(w1_sched))):
                load_w1(*w1_sched[w1_pos]); w1_pos += 1
            for _ in range(min(2, len(w2_sched))):
                load_w2(*w2_sched[w2_pos]); w2_pos += 1
            for e in range(ne_run):
                b.wait("gp", c["pe"])
                b.inc("gp", c["wg"], 16, lambda g, e=e, t0=t0: g.dma_start(
                    out=WER[0:1, :], in_=WEFM[e:e + 1, t0:t0 + TT]))
                werow_ld = c["wg"].n
                b.wait("te", c["ve"])
                gelu_mark = {}
                for he in range(NE):
                    li = e * NE + he
                    r1.use(li)
                    slot = li % 2
                    for hh in range(4):
                        hcg = he * 4 + hh
                        if (hcg % 4) in gelu_mark:
                            b.wait("te", c["se"], gelu_mark[hcg % 4])
                        for ch in range(NCH):
                            lastmm = ch == NCH - 1
                            def amm(t, sl=slot, ch=ch, hh=hh, hcg=hcg, t0=t0):
                                return t.matmul(
                                    psA_s(hcg),
                                    W1E[:, sl, ch, hh * P:(hh + 1) * P],
                                    CB1[:, ch * T + t0:ch * T + t0 + TT],
                                    start=(ch == 0), stop=(ch == NCH - 1))
                            if lastmm:
                                b.inc("te", c["pe"], 1, amm)
                            else:
                                b.on("te", amm)
                        b.wait("se", c["pe"])
                        b.inc("se", c["se"], 1, lambda s_, hcg=hcg, e=e:
                              s_.activation(
                                  out=GBUF[:, hcg, :], in_=psA_s(hcg),
                                  func=AF.Gelu_apprx_tanh,
                                  bias=B1S[:, e * NHM + hcg:e * NHM + hcg + 1]))
                        gelu_mark[hcg % 4] = c["se"].n
                    r1.consumed(li)
                    if w1_pos < len(w1_sched):
                        load_w1(*w1_sched[w1_pos]); w1_pos += 1
                b.wait("te", c["wg"], werow_ld)
                b.wait("te", c["ve"])
                b.inc("te", c["pe"], 1, lambda t: t.matmul(
                    pb6[:], ONESR[:], WER[0:1, :], start=True, stop=True))
                b.wait("ve", c["pe"])
                b.inc("ve", c["ve"], 1, lambda v: v.tensor_copy(out=WEB[:],
                                                                in_=pb6[:]))
                web_done = c["ve"].n
                b.wait("te", c["se"])
                b.wait("te", c["ve"])
                for he in range(NE):
                    li = e * NE + he
                    r2.use(li)
                    slot = li % 2
                    for oc in range(NCH):
                        for hh in range(4):
                            hcg = he * 4 + hh
                            lastmm = (oc == NCH - 1 and hh == 3)
                            def bmm(t, sl=slot, oc=oc, hh=hh, hcg=hcg, he=he):
                                return t.matmul(
                                    moe_ps(oc),
                                    W2E[:, sl, hh, oc * P:(oc + 1) * P],
                                    GBUF[:, hcg, :],
                                    start=(he == 0 and hh == 0),
                                    stop=(he == NE - 1 and hh == 3))
                            if lastmm:
                                b.inc("te", c["pe"], 1, bmm)
                            else:
                                b.on("te", bmm)
                    r2.consumed(li)
                    if w2_pos < len(w2_sched):
                        load_w2(*w2_sched[w2_pos]); w2_pos += 1
                b.wait("ve", c["pe"])
                b.wait("ve", c["ve"], web_done)
                for oc in range(NCH):
                    b.on("ve", lambda v, oc=oc: v.tensor_mul(
                        out=MTMP[:], in0=moe_ps(oc), in1=WEB[:]))
                    b.on("ve", lambda v, oc=oc, t0=t0: v.tensor_add(
                        out=BF[:, oc * T + t0:oc * T + t0 + TT], in0=MTMP[:],
                        in1=BF[:, oc * T + t0:oc * T + t0 + TT]))
                b.inc("ve", c["ve"], 1, lambda v: v.tensor_copy(out=VNOP[:, 0:1], in_=VNOP[:, 1:2]))
            if ttg + 1 < NTT and MOE_PART >= 9:
                acc_init(ttg + 1)
            if stage == 6 or MOE_PART < 9:
                continue
            stats(c, BF, ttg, True, 1e-5, MCAST, MSQ, FTMP)
            b.wait("sy", c["pe"])
            xnt_ld = []
            for ch in range(NCH):
                b.inc("sy", c["ld"], 16, lambda y, i=ch, t0=t0: y.dma_start(
                    out=XNT8[:, i, :], in_=xnew_c[s, i][:, t0:t0 + TT]))
                xnt_ld.append(c["ld"].n)
            for ch in range(NCH):
                o = ch * T + t0
                b.on("ve", lambda v, o=o: v.tensor_sub(
                    out=FTMP[:], in0=BF[:, o:o + TT], in1=MEANT[:]))
                b.on("ve", lambda v, o=o: v.tensor_mul(
                    out=FTMP[:], in0=FTMP[:], in1=RSTDT[:]))
                b.wait("ve", c["ld"], xnt_ld[ch])
                b.inc("ve", c["ve"], 1, lambda v, o=o, i=ch: v.tensor_add(
                    out=BF[:, o:o + TT], in0=XNT8[:, i, :], in1=FTMP[:]))
            b.wait("sy", c["ve"])
            for ch in range(NCH):
                b.inc("sy", c["out"], 16, lambda y, i=ch, t0=t0: y.dma_start(
                    out=ys_c[s, i][:, t0:t0 + TT],
                    in_=BF[:, i * T + t0:i * T + t0 + TT]))
        if stage == 6 or (stage >= 6 and MOE_PART < 9):
            dump_and_finish(s, c, BF)
            return

    for s in range(NSEQ):
        walk_seq(s, C[s])

    for s in range(NSEQ):
        b.wait("sy", C[s]["out"])

    blk = b.stack.enter_context(nc.Block())
    b.flush(blk)
    b.stack.close()
    return nc


IDFB = None


def _prep_inputs(inputs):
    x = inputs["L0_attn_x"]
    v128 = lambda v: np.ascontiguousarray(
        np.asarray(v, dtype=np.float32).reshape(NCH, P).T)
    common = {
        "wfc": np.ascontiguousarray(np.asarray(inputs["L0_attn_ff_fc_w"]).T),
        "wpr": np.ascontiguousarray(np.asarray(inputs["L0_attn_ff_proj_w"]).T),
        "wqkv": np.ascontiguousarray(inputs["L0_ffn_w_qkv"]),
        "wo": np.ascontiguousarray(inputs["L0_ffn_w_o"]),
        "wg": np.ascontiguousarray(inputs["L0_ffn_w_gate"]),
        "w1": np.ascontiguousarray(inputs["L0_ffn_w1"]),
        "w2": np.ascontiguousarray(inputs["L0_ffn_w2"]),
        "rms1w": v128(inputs["L0_attn_input_norm_w"]),
        "alog": v128(inputs["L0_attn_a_log"]),
        "cout": v128(inputs["L0_attn_c_out"]),
        "dskip": v128(inputs["L0_attn_d_skip"]),
        "rms2w": v128(inputs["L0_attn_ff_norm_w"]),
        "ln1w": v128(inputs["L0_ffn_ln1_w"]),
        "ln1b": v128(inputs["L0_ffn_ln1_b"]),
        "ln2w": v128(inputs["L0_ffn_ln2_w"]),
        "ln2b": v128(inputs["L0_ffn_ln2_b"]),
        "bo": v128(inputs["L0_ffn_b_o"]),
        "bqkv": np.ascontiguousarray(
            np.asarray(inputs["L0_ffn_b_qkv"], dtype=np.float32)
            .reshape(24, P).T),
        "b1s": np.ascontiguousarray(
            np.asarray(inputs["L0_ffn_b1"], dtype=np.float32)
            .reshape(NE, NHM, P).transpose(2, 0, 1).reshape(P, NE * NHM)),
        "b2l": np.ascontiguousarray(inputs["L0_ffn_b2"]),
    }
    common = {k: np.asarray(v, dtype=np.float32) for k, v in common.items()}
    in_maps = []
    for core in range(8):
        xs = np.ascontiguousarray(
            np.asarray(x[4 * core:4 * core + 4]).transpose(0, 2, 1),
            dtype=np.float32)
        m = {"xs": xs}
        m.update(common)
        in_maps.append(m)
    return in_maps


def kernel(**inputs):
    in_maps = _prep_inputs(inputs)
    nc = build_program(STAGE_FULL)
    res = run_bass_kernel_spmd(nc, in_maps, core_ids=list(range(8)))
    out = np.empty((32, T, D), dtype=np.float32)
    for core in range(8):
        ys = res.results[core]["ys"]
        for i in range(NSEQ):
            out[4 * core + i] = ys[i].T
    return out
